# version 14
# baseline (speedup 1.0000x reference)
import numpy as np
import ml_dtypes
import concourse.bass as bass
import concourse.mybir as mybir
from concourse.bass_utils import run_bass_kernel_spmd
from contextlib import ExitStack

F32 = mybir.dt.float32
BF16 = mybir.dt.bfloat16
AF = mybir.ActivationFunctionType
ALU = mybir.AluOpType
AX = mybir.AxisListType
BF = ml_dtypes.bfloat16

T = 2048
D = 1024
NT = T // 128
NCT = D // 128

ENGS = ("te", "ve", "se", "gp", "sp")
SECT = {"te": "tensor", "ve": "vector", "se": "scalar", "gp": "gpsimd", "sp": "sync"}
DMA_QUEUES = ("gp", "sp", "se")
NSLOT = 8


class Buf:
    __slots__ = ("name", "writer", "readers")

    def __init__(self, name):
        self.name = name
        self.writer = None
        self.readers = []


class Op:
    __slots__ = ("eng", "emit", "idx", "gidx", "deps", "dma", "q", "dma_idx",
                 "need_inc", "waits", "sem_key", "sem_val", "fragile")

    def __init__(self):
        self.deps = set()
        self.need_inc = False
        self.waits = []
        self.dma = False
        self.fragile = False
        self.sem_key = None
        self.sem_val = 0


class Sched:

    def __init__(self, nc):
        self.nc = nc
        self.ops = {e: [] for e in ENGS}
        self.glist = []
        self.dma_counts = {q: 0 for q in DMA_QUEUES}
        self.dma_slot_last = {}

    def buf(self, name):
        return Buf(name)

    def op(self, eng, emit, reads=(), writes=(), dma=False, extra_deps=(), fragile=False):
        o = Op()
        o.eng = eng
        o.emit = emit
        o.dma = dma
        o.fragile = fragile or getattr(emit, "_fragile", False)
        o.idx = len(self.ops[eng])
        o.gidx = len(self.glist)
        for d in extra_deps:
            o.deps.add(d)
        if dma:
            assert eng in DMA_QUEUES
            i = self.dma_counts[eng]
            o.q = eng
            o.dma_idx = i
            self.dma_counts[eng] = i + 1
            prev = self.dma_slot_last.get((eng, i % NSLOT))
            if prev is not None:
                o.deps.add(prev)
            self.dma_slot_last[(eng, i % NSLOT)] = o
        raw = set()
        for b in reads:
            if b.writer is not None:
                o.deps.add(b.writer)
                raw.add(b.writer)
            if dma:
                b.readers.append(o)
            else:
                b.readers = [r for r in b.readers if r.dma or r.eng != eng] + [o]
        for b in writes:
            if b.writer is not None:
                o.deps.add(b.writer)
            for r in b.readers:
                o.deps.add(r)
            b.writer = o
            b.readers = []
        o.deps = {d for d in o.deps
                  if (d.dma or d.eng != eng or d.fragile or o.fragile
                      or (d in raw and eng in ("ve", "se"))) and d is not o}
        self.ops[eng].append(o)
        self.glist.append(o)
        return o

    def finalize(self):
        for o in self.glist:
            for d in o.deps:
                if not d.dma:
                    d.need_inc = True
        cnt = {e: 0 for e in ENGS}
        for e in ENGS:
            for o in self.ops[e]:
                if o.dma:
                    o.sem_key = (o.q, o.dma_idx % NSLOT)
                    o.sem_val = 16 * (o.dma_idx // NSLOT + 1)
                elif o.need_inc:
                    cnt[e] += 1
                    o.sem_key = e
                    o.sem_val = cnt[e]
        self.inc_totals = cnt
        clocks = {e: {} for e in ENGS}
        snap = [None] * len(self.glist)
        for o in self.glist:
            c = clocks[o.eng]
            for d in sorted(o.deps, key=lambda d: d.gidx):
                key, val = d.sem_key, d.sem_val
                if c.get(key, 0) < val:
                    o.waits.append((key, val))
                    for k2, v2 in snap[d.gidx].items():
                        if c.get(k2, 0) < v2:
                            c[k2] = v2
                    if c.get(key, 0) < val:
                        c[key] = val
            if o.dma:
                s = dict(c)
                s[o.sem_key] = max(s.get(o.sem_key, 0), o.sem_val)
                snap[o.gidx] = s
            else:
                s = dict(c)
                if o.need_inc:
                    s[o.eng] = max(s.get(o.eng, 0), o.sem_val)
                snap[o.gidx] = s
        n_waits = sum(len(o.waits) for o in self.glist)
        self.stats = dict(ops={e: len(self.ops[e]) for e in ENGS},
                          incs=cnt, waits=n_waits, dmas=dict(self.dma_counts))

    def emit_all(self, stack):
        nc = self.nc
        esems = {e: stack.enter_context(nc.semaphore(f"sem_{e}")) for e in ENGS}
        dsems = {}
        for q in DMA_QUEUES:
            for s in range(min(NSLOT, self.dma_counts[q])):
                dsems[(q, s)] = stack.enter_context(nc.semaphore(f"dsem_{q}{s}"))

        def sem_for(key):
            return esems[key] if isinstance(key, str) else dsems[key]

        with nc.Block() as blk:
            for e in ENGS:
                if not self.ops[e]:
                    continue

                def replay(eng, e=e):
                    for o in self.ops[e]:
                        for key, val in o.waits:
                            eng.wait_ge(sem_for(key), val)
                        if o.emit is None:
                            continue
                        inst = o.emit(eng)
                        if o.dma:
                            inst.then_inc(sem_for(o.sem_key), 16)
                        elif o.need_inc:
                            inst.then_inc(sem_for(o.sem_key), 1)

                getattr(blk, SECT[e])(replay)


class Arena:

    def __init__(self, sched, tensor, cols, name):
        self.k = sched
        self.t = tensor
        self.cols = cols
        self.name = name
        self.free_list = [(0, cols)]
        self.allocs = {}
        self.ghosts = []
        self.peak = 0

    def alloc(self, name, size):
        assert name not in self.allocs, name
        size = (size + 7) & ~7
        for i, (off, sz) in enumerate(self.free_list):
            if sz >= size:
                if sz == size:
                    self.free_list.pop(i)
                else:
                    self.free_list[i] = (off + size, sz - size)
                b = self.k.buf(f"{self.name}.{name}")
                self.peak = max(self.peak, self.cols - sum(z for _, z in self.free_list))
                keep = []
                for goff, gsz, gbuf in self.ghosts:
                    if goff < off + size and off < goff + gsz:
                        if gbuf.writer is not None:
                            b.readers.append(gbuf.writer)
                        b.readers.extend(gbuf.readers)
                        if not (off <= goff and goff + gsz <= off + size):
                            keep.append((goff, gsz, gbuf))
                    else:
                        keep.append((goff, gsz, gbuf))
                self.ghosts = keep
                self.allocs[name] = (off, size, b)
                return b
        raise MemoryError(f"arena {self.name}: cannot alloc {name}({size}); free={self.free_list}")

    def off(self, name):
        return self.allocs[name][0]

    def ap(self, name, c0=0, c1=None):
        off, size, _ = self.allocs[name]
        if c1 is None:
            c1 = size
        assert 0 <= c0 <= c1 <= size, (name, c0, c1, size)
        return self.t[:, off + c0: off + c1]

    def vap(self, name, c0, c1, pattern, **kw):
        off, size, _ = self.allocs[name]
        assert c1 <= size
        return self.t[:, off + c0: off + c1].rearrange(pattern, **kw)

    def rap(self, name, p0, p1, c0, c1):
        off, size, _ = self.allocs[name]
        assert c1 <= size
        return self.t[p0:p1, off + c0: off + c1]

    def buf(self, name):
        return self.allocs[name][2]

    def free(self, name):
        off, size, b = self.allocs.pop(name)
        self.ghosts.append((off, size, b))
        self.free_list.append((off, size))
        self.free_list.sort()
        merged = []
        for off2, sz2 in self.free_list:
            if merged and merged[-1][0] + merged[-1][1] == off2:
                merged[-1] = (merged[-1][0], merged[-1][1] + sz2)
            else:
                merged.append((off2, sz2))
        self.free_list = merged


class Common:
    def __init__(self, nc, k, stack, bf_cols=60544, f32_cols=22912):
        self.nc = nc
        self.k = k
        self.stack = stack
        tb = stack.enter_context(nc.sbuf_tensor("arena_bf", [128, bf_cols], BF16))
        tf = stack.enter_context(nc.sbuf_tensor("arena_f32", [128, f32_cols], F32))
        self.abf = Arena(k, tb, bf_cols, "bf")
        self.af = Arena(k, tf, f32_cols, "f32")
        self.ps = {}
        self.psbuf = {}
        for nm, shape, dt in [("A", [128, 512], F32), ("B", [128, 512], F32),
                              ("C", [128, 1024], F32), ("D", [128, 1024], F32),
                              ("E", [128, 512], F32), ("T", [128, 1024], BF16)]:
            self.ps[nm] = stack.enter_context(nc.psum_tensor(f"ps{nm}", shape, dt))
            self.psbuf[nm] = k.buf(f"ps{nm}")

    def load(self, q, dst_ap, src_ap, wbuf, rbufs=()):
        return self.k.op(q, lambda e: e.dma_start(dst_ap, src_ap),
                         reads=list(rbufs), writes=[wbuf], dma=True)

    def store(self, q, dst_ap, src_ap, rbufs, wbufs=()):
        return self.k.op(q, lambda e: e.dma_start(dst_ap, src_ap),
                         reads=list(rbufs), writes=list(wbufs), dma=True)



def MM(out_ap, lhsT_ap, rhs_ap, start, stop, skip=False):
    return lambda e, o=out_ap, l=lhsT_ap, r=rhs_ap, st=start, sp=stop, sk=skip: \
        e.matmul(o, l, r, start=st, stop=sp, skip_group_check=sk)


def TRANS(out_ap, in_ap, ident_ap):
    return lambda e, o=out_ap, i=in_ap, d=ident_ap: e.transpose(o, i, d)


def CP(out_ap, in_ap):
    return lambda e, o=out_ap, i=in_ap: e.tensor_copy(o, i)


def TT(out_ap, a_ap, b_ap, op):
    return lambda e, o=out_ap, a=a_ap, b=b_ap, p=op: e.tensor_tensor(o, a, b, p)


def _frag(fn):
    fn._fragile = True
    return fn


def TS(out_ap, in_ap, s1, s2, op0, op1=ALU.bypass, accum=None):
    if s2 is None and accum is None:
        fn = lambda e, o=out_ap, i=in_ap, s=s1, p=op0: e.tensor_single_scalar(o, i, s, p)
    else:
        fn = lambda e, o=out_ap, i=in_ap, a=s1, b=s2, p0=op0, p1=op1, ac=accum: \
            e.tensor_scalar(o, i, a, b, p0, p1, accum_out=ac)
    if not isinstance(s1, (int, float)) or (s2 is not None and not isinstance(s2, (int, float))) or accum is not None:
        _frag(fn)
    return fn


def STT(out_ap, in0_ap, scalar, in1_ap, op0, op1, accum=None):
    return _frag(lambda e, o=out_ap, i=in0_ap, s=scalar, j=in1_ap, p0=op0, p1=op1, ac=accum:
                 e.scalar_tensor_tensor(o, i, s, j, p0, p1, accum_out=ac))


def ACT(out_ap, in_ap, func, bias=0.0, scale=1.0, accum=None):
    fn = lambda e, o=out_ap, i=in_ap, f=func, b=bias, s=scale, ac=accum: \
        e.activation(o, i, f, bias=b, scale=s, accum_out=ac)
    if accum is not None:
        _frag(fn)
    return fn


def RED(out_ap, in_ap, op, negate=False):
    return _frag(lambda e, o=out_ap, i=in_ap, p=op, n=negate: e.tensor_reduce(o, i, AX.X, p, negate=n))


def RECIP(out_ap, in_ap):
    return _frag(lambda e, o=out_ap, i=in_ap: e.reciprocal(o, i))


def MEMSET(ap, val):
    return lambda e, a=ap, v=val: e.memset(a, v)


def SCAN(out_ap, d0_ap, d1_ap, init_ap):
    return _frag(lambda e, o=out_ap, a=d0_ap, b=d1_ap, i=init_ap:
                 e.tensor_tensor_scan(o, a, b, i, ALU.mult, ALU.add))


def RED2(out_ap, in_ap, op, negate=False):
    return _frag(lambda e, o=out_ap, i=in_ap, p=op, n=negate: e.tensor_reduce(o, i, AX.X, p, negate=n))



def emit_ln_stats(C, x_tiled, rolls, st, per_tile_cb, src_buf=None):
    k, af = C.k, C.af
    sb = af.buf(st)
    jb = af.buf("JUNK")
    for tt in range(NT):
        slot = rolls[tt % len(rolls)]
        xb = af.buf(slot)
        C.load("sp", af.ap(slot), x_tiled[tt], xb, rbufs=[src_buf] if src_buf is not None else [])
        b = tt * 8
        k.op("ve", fragile=True, emit=RED(af.ap(st, b, b + 1), af.ap(slot), ALU.add, negate=True), reads=[xb], writes=[sb])
        k.op("ve", TS(af.ap(st, b + 1, b + 2), af.ap(st, b, b + 1), 1.0 / D, None, ALU.mult),
             reads=[sb], writes=[sb])
        k.op("se", ACT(af.ap("JUNK"), af.ap(slot), AF.Square, bias=af.ap(st, b + 1, b + 2),
                       scale=1.0, accum=af.ap(st, b + 2, b + 3)),
             reads=[xb, sb], writes=[jb, sb])
        k.op("ve", TS(af.ap(st, b + 2, b + 3), af.ap(st, b + 2, b + 3), 1.0 / D, 1e-5, ALU.mult, ALU.add),
             reads=[sb], writes=[sb])
        k.op("se", ACT(af.ap(st, b + 3, b + 4), af.ap(st, b + 2, b + 3), AF.Ln),
             reads=[sb], writes=[sb])
        k.op("se", ACT(af.ap(st, b + 4, b + 5), af.ap(st, b + 3, b + 4), AF.Exp, scale=-0.5),
             reads=[sb], writes=[sb])
        k.op("ve", TT(af.ap(st, b + 5, b + 6), af.ap(st, b + 1, b + 2), af.ap(st, b + 4, b + 5), ALU.mult),
             reads=[sb], writes=[sb])
        per_tile_cb(tt, slot)


def emit_norm_transpose(C, tt, slot, roll_bf, dstT, scale_ap, bias_ap, sbufs):
    k, af, abf = C.k, C.af, C.abf
    nm = roll_bf[tt % len(roll_bf)]
    k.op("se", ACT(abf.ap(nm), af.ap(slot), AF.Identity, bias=bias_ap, scale=scale_ap),
         reads=[af.buf(slot)] + sbufs, writes=[abf.buf(nm)])
    for ct in range(NCT):
        k.op("te", TRANS(C.ps["T"][:, ct * 128:(ct + 1) * 128],
                         abf.ap(nm, ct * 128, (ct + 1) * 128), abf.ap("IDENT")),
             reads=[abf.buf(nm), abf.buf("IDENT")], writes=[C.psbuf["T"]])
    src = C.ps["T"][:, 0:1024].rearrange("p (a c) -> p a c", a=NCT)
    dst = abf.vap(dstT, 0, NCT * T, "p (a c) -> p a c", c=T)[:, :, tt * 128:(tt + 1) * 128]
    k.op("ve", CP(dst, src), reads=[C.psbuf["T"]], writes=[abf.buf(dstT)])


def emit_bias_rep(C, brow_dram, dst_name, width=D):
    k, af = C.k, C.af
    af.alloc("BROW_t", width)
    C.load("sp", af.t[0:1, af.off("BROW_t"):af.off("BROW_t") + width], brow_dram, af.buf("BROW_t"))
    nh = (width + 511) // 512
    for half in range(nh):
        w = min(512, width - half * 512)
        k.op("te", MM(C.ps["C"][:, half * 512:half * 512 + w],
                      af.t[0:1, af.off("ONES1"):af.off("ONES1") + 128],
                      af.t[0:1, af.off("BROW_t") + half * 512:af.off("BROW_t") + half * 512 + w],
                      True, True),
             reads=[af.buf("BROW_t"), af.buf("ONES1")], writes=[C.psbuf["C"]])
    k.op("ve", CP(af.ap(dst_name), C.ps["C"][:, 0:width]),
         reads=[C.psbuf["C"]], writes=[af.buf(dst_name)])
    af.free("BROW_t")



def emit_layer1(C, names):
    nc, k = C.nc, C.k
    xin_b = names["x_in_buf"]
    xout_b = names["x_out_buf"]
    ssp_b = names["s_spill_buf"]
    abf, af = C.abf, C.af
    HG = 4
    NGRP = 4

    x_in, x_out, s_spill = names["x_in"], names["x_out"], names["s_spill"]
    x_tiled = x_in.rearrange("(a p) c -> a p c", p=128)
    s_tiled = s_spill.rearrange("(a p) c -> a p c", p=128)
    xo_tiled = x_out.rearrange("(a p) c -> a p c", p=128)

    abf.alloc("XT", NCT * T)
    abf.alloc("IDENT", 128)
    C.load("gp", abf.ap("IDENT"), names["ident"], abf.buf("IDENT"))
    af.alloc("STATS", 128)
    af.alloc("ONES1", 128)
    k.op("ve", MEMSET(af.t[0:1, af.off("ONES1"):af.off("ONES1") + 128], 1.0), writes=[af.buf("ONES1")])
    af.alloc("JUNK", 1024)
    af.alloc("LGT", NT * 8)
    af.alloc("MSK", NT * 8)
    af.alloc("SCR", NT * 4)
    rolls = ["XR0", "XR1"]
    for nm in rolls:
        af.alloc(nm, 1024)
    roll_bf = ["XNR0", "XNR1"]
    for nm in roll_bf:
        abf.alloc(nm, 1024)

    emit_ln_stats(C, x_tiled, rolls, "STATS", src_buf=names["x_in_buf"],
                  per_tile_cb=lambda tt, slot: emit_norm_transpose(
                      C, tt, slot, roll_bf, "XT",
                      C.af.ap("STATS", tt * 8 + 4, tt * 8 + 5),
                      C.af.ap("STATS", tt * 8 + 5, tt * 8 + 6), [C.af.buf("STATS")]))

    if "dbg_stats" in names:
        C.store("sp", names["dbg_stats"], af.ap("STATS", 0, 128), [af.buf("STATS")])
    abf.alloc("OUTCT", NCT * T)
    abf.alloc("PST", 16 * 512)
    pstb = abf.buf("PST")
    pst_tiles = [k.buf(f"l1pst{j}") for j in range(16)]
    pst_first = [True] * 16
    abf.alloc("EMSK", 896)
    C.load("gp", abf.ap("EMSK"), names["emask"], abf.buf("EMSK"))
    for nm in ("SP100", "SP101", "SP110", "SP111"):
        af.alloc(nm, 512)
    af.alloc("OSTG", 1024)
    af.alloc("DENS", 1024)
    af.alloc("ONES64", 64)
    k.op("ve", MEMSET(af.rap("ONES64", 64, 65, 0, 64), 1.0), writes=[af.buf("ONES64")])

    for hg in range(NGRP):
        abf.alloc("QT", 2 * T)
        abf.alloc("KT", 2 * T)
        VSTR = HG * 66
        abf.alloc("VB", NT * VSTR)
        abf.alloc("WQ", NCT * 128)
        abf.alloc("WK", NCT * 128)
        abf.alloc("WV", NCT * 256)
        wt = names["wqkvT"]
        for ct in range(NCT):
            C.load("gp", abf.ap("WV", ct * 256, (ct + 1) * 256),
                   wt[ct * 128:(ct + 1) * 128, 2048 + hg * 256:2048 + (hg + 1) * 256], abf.buf("WV"))
        vb4 = abf.vap("VB", 0, NT * VSTR, "p (t h c) -> p t h c", h=HG, c=66)
        k.op("ve", MEMSET(vb4[:, :, :, 64:66], 1.0), writes=[abf.buf("VB")])
        k.op("ve", MEMSET(vb4[:, :, :, 65:66], 0.0), writes=[abf.buf("VB")])

        for pr in range(2):
            for ct in range(NCT):
                C.load("gp", abf.ap("WQ", ct * 128, (ct + 1) * 128),
                       wt[ct * 128:(ct + 1) * 128, hg * 256 + pr * 128:hg * 256 + (pr + 1) * 128],
                       abf.buf("WQ"))
                C.load("gp", abf.ap("WK", ct * 128, (ct + 1) * 128),
                       wt[ct * 128:(ct + 1) * 128, 1024 + hg * 256 + pr * 128:1024 + hg * 256 + (pr + 1) * 128],
                       abf.buf("WK"))
            for dst, wnm in (("QT", "WQ"), ("KT", "WK")):
                for qc in range(4):
                    pj = ("C", "D")[qc % 2]
                    for kt in range(NCT):
                        k.op("te", MM(C.ps[pj][:, 0:512],
                                      abf.ap(wnm, kt * 128, (kt + 1) * 128),
                                      abf.ap("XT", kt * T + qc * 512, kt * T + (qc + 1) * 512),
                                      kt == 0, kt == NCT - 1),
                             reads=[abf.buf(wnm), abf.buf("XT")], writes=[C.psbuf[pj]])
                    k.op("ve", CP(abf.ap(dst, pr * T + qc * 512, pr * T + (qc + 1) * 512),
                                  C.ps[pj][:, 0:512]),
                         reads=[C.psbuf[pj]], writes=[abf.buf(dst)])
        for mt in range(NT):
            pj = ("D", "C")[mt % 2]
            for kt in range(NCT):
                k.op("te", MM(C.ps[pj][:, 0:256],
                              abf.ap("XT", kt * T + mt * 128, kt * T + (mt + 1) * 128),
                              abf.ap("WV", kt * 256, (kt + 1) * 256),
                              kt == 0, kt == NCT - 1),
                     reads=[abf.buf("WV"), abf.buf("XT")], writes=[C.psbuf[pj]])
            k.op("ve", CP(abf.vap("VB", mt * VSTR, (mt + 1) * VSTR, "p (h c) -> p h c", c=66)[:, :, 0:64],
                          C.ps[pj][:, 0:256].rearrange("p (h d) -> p h d", h=HG)),
                 reads=[C.psbuf[pj]], writes=[abf.buf("VB")])
        abf.free("WQ"); abf.free("WK"); abf.free("WV")

        if "dbg_qt" in names and hg == 0:
            C.store("gp", names["dbg_qt"], abf.ap("QT", 0, T), [abf.buf("QT")])
            C.store("gp", names["dbg_kt"], abf.ap("KT", 0, T), [abf.buf("KT")])
            C.store("gp", names["dbg_xt"], abf.ap("XT", 0, T), [abf.buf("XT")])
        for qc in range(4):
            q0 = qc * 512
            ktmax = min(NT, 4 * (qc + 1))
            for gp2 in range(2):
                pr = gp2
                pw_ops = [{}, {}]
                ex_ops = [{}, {}]

                def av1(kt, s_i, ktmax=ktmax, gp2=gp2):
                    off = s_i * 512
                    sl = s_i * 8 + kt % 8
                    ed = [pw_ops[1][kt]] if (s_i == 0 and kt in pw_ops[1]) else []
                    k.op("te", MM(C.ps["C"][0:66, off:off + 512],
                                  abf.ap("VB", kt * VSTR + (gp2 * 2 + s_i) * 66,
                                         kt * VSTR + (gp2 * 2 + s_i + 1) * 66),
                                  abf.ap("PST", sl * 512, (sl + 1) * 512),
                                  kt == 0, kt == ktmax - 1, skip=True),
                         reads=[abf.buf("VB"), pst_tiles[sl]], writes=[C.psbuf["C"]],
                         extra_deps=ed)

                for kt in range(ktmax):
                    dc = q0 - kt * 128
                    for s_i in range(2):
                        half = s_i * 64
                        pband = ("A", "B")[kt % 2] if s_i == 0 else ("E", "D")[kt % 2]
                        psap = C.ps[pband][:, 0:512]
                        sp1n = f"SP1{s_i}{kt % 2}"
                        sp1b = af.buf(sp1n)
                        sl = s_i * 8 + kt % 8
                        ed = [ex_ops[1][kt - 2]] if (s_i == 0 and (kt - 2) in ex_ops[1]) else []
                        k.op("te", MM(psap,
                                      abf.rap("KT", half, half + 64, pr * T + kt * 128, pr * T + (kt + 1) * 128),
                                      abf.rap("QT", half, half + 64, pr * T + q0, pr * T + q0 + 512),
                                      True, True),
                             reads=[abf.buf("KT"), abf.buf("QT")], writes=[C.psbuf[pband]],
                             extra_deps=ed)
                        ex_ops[s_i][kt] = k.op("se", ACT(af.ap(sp1n), psap, AF.Exp, bias=0.0, scale=0.125),
                                               reads=[C.psbuf[pband]], writes=[sp1b])
                        k.op("se", ACT(af.ap(sp1n), af.ap(sp1n), AF.Ln, bias=1.0, scale=1.0),
                             reads=[sp1b], writes=[sp1b])
                        if dc <= 127:
                            k.op("ve", TT(af.ap(sp1n), af.ap(sp1n), abf.ap("EMSK", dc + 384, dc + 384 + 512), ALU.mult),
                                 reads=[sp1b, abf.buf("EMSK")], writes=[sp1b])
                        wl = [pst_tiles[sl]]
                        if pst_first[sl]:
                            wl.append(pstb)
                            pst_first[sl] = False
                        pw_ops[s_i][kt] = k.op("ve", TT(abf.ap("PST", sl * 512, (sl + 1) * 512),
                                                        af.ap(sp1n), af.ap(sp1n), ALU.mult),
                                               reads=[sp1b], writes=wl)
                    if kt >= 2:
                        av1(kt - 2, 0)
                        av1(kt - 2, 1)
                for ktl in (ktmax - 2, ktmax - 1):
                    av1(ktl, 0)
                    av1(ktl, 1)
                for s_i in range(2):
                    off = s_i * 512
                    h = hg * HG + gp2 * 2 + s_i
                    k.op("ve", CP(af.rap("OSTG", 0, 64, off, off + 512), C.ps["C"][0:64, off:off + 512]),
                         reads=[C.psbuf["C"]], writes=[af.buf("OSTG")])
                    k.op("ve", TS(af.rap("DENS", 64, 65, off, off + 512), C.ps["C"][64:65, off:off + 512],
                                  1e-9, None, ALU.add),
                         reads=[C.psbuf["C"]], writes=[af.buf("DENS")])
                    k.op("se", ACT(af.rap("DENS", 64, 65, off, off + 512),
                                   af.rap("DENS", 64, 65, off, off + 512), AF.Ln),
                         reads=[af.buf("DENS")], writes=[af.buf("DENS")])
                    k.op("se", ACT(af.rap("DENS", 64, 65, off, off + 512),
                                   af.rap("DENS", 64, 65, off, off + 512), AF.Exp, scale=-1.0),
                         reads=[af.buf("DENS")], writes=[af.buf("DENS")])
                    k.op("te", MM(C.ps["D"][0:64, off:off + 512], af.rap("ONES64", 64, 65, 0, 64),
                                  af.rap("DENS", 64, 65, off, off + 512), True, True, skip=True),
                         reads=[af.buf("ONES64"), af.buf("DENS")], writes=[C.psbuf["D"]])
                    ct, chalf = h // 2, (h % 2) * 64
                    k.op("ve", TT(abf.rap("OUTCT", chalf, chalf + 64, ct * T + q0, ct * T + q0 + 512),
                                  af.rap("OSTG", 0, 64, off, off + 512),
                                  C.ps["D"][0:64, off:off + 512], ALU.mult),
                         reads=[af.buf("OSTG"), C.psbuf["D"]], writes=[abf.buf("OUTCT")])
        abf.free("QT"); abf.free("KT"); abf.free("VB")
    abf.free("PST"); abf.free("EMSK")
    for nm in ("SP100", "SP101", "SP110", "SP111"):
        af.free(nm)
    af.free("OSTG"); af.free("DENS"); af.free("ONES64")

    abf.alloc("WP", NCT * 1024)
    for ct in range(NCT):
        C.load("gp", abf.ap("WP", ct * 1024, (ct + 1) * 1024),
               names["wprojT"][ct * 128:(ct + 1) * 128, :], abf.buf("WP"))
    af.alloc("BIAS", 1024)
    emit_bias_rep(C, names["bproj"], "BIAS")
    af.alloc("SOUT", 2 * 1024)
    for mt in range(NT):
        pb = "C" if mt % 2 == 0 else "D"
        for ct in range(NCT):
            for half in range(2):
                k.op("te", MM(C.ps[pb][:, half * 512:(half + 1) * 512],
                              abf.ap("OUTCT", ct * T + mt * 128, ct * T + (mt + 1) * 128),
                              abf.ap("WP", ct * 1024 + half * 512, ct * 1024 + (half + 1) * 512),
                              ct == 0, ct == NCT - 1),
                     reads=[abf.buf("OUTCT"), abf.buf("WP")], writes=[C.psbuf[pb]])
        sl = mt % 2
        k.op("ve", TT(af.ap("SOUT", sl * 1024, (sl + 1) * 1024), C.ps[pb][:, 0:1024], af.ap("BIAS"), ALU.add),
             reads=[C.psbuf[pb], af.buf("BIAS")], writes=[af.buf("SOUT")])
        C.store("sp", s_tiled[mt], af.ap("SOUT", sl * 1024, (sl + 1) * 1024), [af.buf("SOUT")], wbufs=[ssp_b])
        if "s_dbg" in names:
            C.store("sp", names["s_dbg"].rearrange("(a p) c -> a p c", p=128)[mt],
                    af.ap("SOUT", sl * 1024, (sl + 1) * 1024), [af.buf("SOUT")])
    abf.free("WP"); abf.free("OUTCT"); af.free("BIAS"); af.free("SOUT")

    abf.alloc("GW", NCT * 8)
    C.load("gp", abf.vap("GW", 0, NCT * 8, "p (a c) -> p a c", c=8),
           names["gwT"].rearrange("(a p) c -> p a c", p=128), abf.buf("GW"))
    for mt in range(NT):
        for kt in range(NCT):
            k.op("te", MM(C.ps["E"][:, 0:8],
                          abf.ap("XT", kt * T + mt * 128, kt * T + (mt + 1) * 128),
                          abf.ap("GW", kt * 8, (kt + 1) * 8),
                          kt == 0, kt == NCT - 1),
                 reads=[abf.buf("XT"), abf.buf("GW")], writes=[C.psbuf["E"]])
        k.op("ve", CP(af.ap("LGT", mt * 8, (mt + 1) * 8), C.ps["E"][:, 0:8]),
             reads=[C.psbuf["E"]], writes=[af.buf("LGT")])
    abf.free("GW")
    for mt in range(NT):
        b4 = mt * 4
        k.op("ve", fragile=True, emit=RED(af.ap("SCR", b4, b4 + 1), af.ap("LGT", mt * 8, (mt + 1) * 8), ALU.max),
             reads=[af.buf("LGT")], writes=[af.buf("SCR")])
        k.op("se", ACT(af.ap("SCR", b4 + 1, b4 + 2), af.ap("SCR", b4, b4 + 1), AF.Sigmoid),
             reads=[af.buf("SCR")], writes=[af.buf("SCR")])
        k.op("ve", TS(af.ap("MSK", mt * 8, (mt + 1) * 8), af.ap("LGT", mt * 8, (mt + 1) * 8),
                      af.ap("SCR", b4, b4 + 1), None, ALU.is_ge),
             reads=[af.buf("LGT"), af.buf("SCR")], writes=[af.buf("MSK")])
        k.op("ve", TT(af.ap("SCR", b4 + 2, b4 + 3), af.ap("STATS", mt * 8 + 4, mt * 8 + 5),
                      af.ap("SCR", b4 + 1, b4 + 2), ALU.mult),
             reads=[af.buf("SCR"), af.buf("STATS")], writes=[af.buf("SCR")])
        k.op("ve", TT(af.ap("SCR", b4 + 3, b4 + 4), af.ap("STATS", mt * 8 + 5, mt * 8 + 6),
                      af.ap("SCR", b4 + 1, b4 + 2), ALU.mult),
             reads=[af.buf("SCR"), af.buf("STATS")], writes=[af.buf("SCR")])

    af.alloc("ACC", NT * 1024)
    accb = af.buf("ACC")
    af.alloc("SILU", 512)
    silub = af.buf("SILU")
    for i in range(16):
        abf.alloc(f"H{i}", 1024)
    for i in range(4):
        abf.alloc(f"WG{i}", NCT * 128)
        abf.alloc(f"WU{i}", NCT * 128)
    abf.alloc("WD", 16 * 1024)
    wdb = abf.buf("WD")

    def expert(ei, wgt, wut, wdt, src_name, masked):
        for tc in range(2):
            t0 = tc * 1024
            for ft in range(16):
                wg_s, wu_s = f"WG{ft % 4}", f"WU{ft % 4}"
                C.load("gp", abf.vap(wg_s, 0, NCT * 128, "p (a c) -> p a c", c=128),
                       wgt.rearrange("(a p) f -> p a f", p=128)[:, :, ft * 128:(ft + 1) * 128],
                       abf.buf(wg_s))
                C.load("gp", abf.vap(wu_s, 0, NCT * 128, "p (a c) -> p a c", c=128),
                       wut.rearrange("(a p) f -> p a f", p=128)[:, :, ft * 128:(ft + 1) * 128],
                       abf.buf(wu_s))
                if tc == 0:
                    C.load("gp", abf.ap("WD", ft * 1024, (ft + 1) * 1024),
                           wdt[ft * 128:(ft + 1) * 128, :], wdb)
                for nh in range(2):
                    q0 = t0 + nh * 512
                    for kt in range(NCT):
                        k.op("te", MM(C.ps["A"][:, 0:512],
                                      abf.ap(wg_s, kt * 128, (kt + 1) * 128),
                                      abf.ap(src_name, kt * T + q0, kt * T + q0 + 512),
                                      kt == 0, kt == NCT - 1),
                             reads=[abf.buf(wg_s), abf.buf(src_name)], writes=[C.psbuf["A"]])
                    k.op("se", ACT(af.ap("SILU"), C.ps["A"][:, 0:512], AF.Silu),
                         reads=[C.psbuf["A"]], writes=[silub])
                    for kt in range(NCT):
                        k.op("te", MM(C.ps["B"][:, 0:512],
                                      abf.ap(wu_s, kt * 128, (kt + 1) * 128),
                                      abf.ap(src_name, kt * T + q0, kt * T + q0 + 512),
                                      kt == 0, kt == NCT - 1),
                             reads=[abf.buf(wu_s), abf.buf(src_name)], writes=[C.psbuf["B"]])
                    k.op("ve", TT(abf.ap(f"H{ft}", nh * 512, (nh + 1) * 512), af.ap("SILU"),
                                  C.ps["B"][:, 0:512], ALU.mult),
                         reads=[silub, C.psbuf["B"]], writes=[abf.buf(f"H{ft}")])
            for mi in range(8):
                mt = tc * 8 + mi
                pb = "C" if mi % 2 == 0 else "D"
                for ft in range(16):
                    for half in range(2):
                        k.op("te", MM(C.ps[pb][:, half * 512:(half + 1) * 512],
                                      abf.ap(f"H{ft}", mi * 128, (mi + 1) * 128),
                                      abf.ap("WD", ft * 1024 + half * 512, ft * 1024 + (half + 1) * 512),
                                      ft == 0, ft == 15),
                             reads=[abf.buf(f"H{ft}"), wdb], writes=[C.psbuf[pb]])
                if masked:
                    k.op("ve", STT(af.ap("ACC", mt * 1024, (mt + 1) * 1024), C.ps[pb][:, 0:1024],
                                   af.ap("MSK", mt * 8 + ei, mt * 8 + ei + 1),
                                   af.ap("ACC", mt * 1024, (mt + 1) * 1024), ALU.mult, ALU.add),
                         reads=[C.psbuf[pb], af.buf("MSK"), accb], writes=[accb])
                else:
                    k.op("ve", CP(af.ap("ACC", mt * 1024, (mt + 1) * 1024), C.ps[pb][:, 0:1024]),
                         reads=[C.psbuf[pb]], writes=[accb])

    expert(-1, names["sgT"], names["suT"], names["sdT"], "XT", masked=False)
    abf.free("XT")

    abf.alloc("XST", NCT * T)
    for tt in range(NT):
        slot = rolls[tt % 2]
        xb = af.buf(slot)
        C.load("sp", af.ap(slot), x_tiled[tt], xb, rbufs=[xin_b])
        b4 = tt * 4
        emit_norm_transpose(C, tt, slot, roll_bf, "XST",
                            af.ap("SCR", b4 + 2, b4 + 3), af.ap("SCR", b4 + 3, b4 + 4),
                            [af.buf("SCR")])
    for e_i in range(8):
        expert(e_i, names["rgT"][e_i], names["ruT"][e_i], names["rdT"][e_i], "XST", masked=True)

    for i in range(16):
        abf.free(f"H{i}")
    for i in range(4):
        abf.free(f"WG{i}"); abf.free(f"WU{i}")
    abf.free("WD"); abf.free("XST")
    af.free("SILU");  af.free("LGT"); af.free("MSK"); af.free("JUNK")

    af.alloc("SRT", 1024)
    for mt in range(NT):
        slot = rolls[mt % 2]
        C.load("sp", af.ap("SRT"), s_tiled[mt], af.buf("SRT"), rbufs=[ssp_b])
        C.load("sp", af.ap(slot), x_tiled[mt], af.buf(slot), rbufs=[xin_b])
        acc_ap = af.ap("ACC", mt * 1024, (mt + 1) * 1024)
        k.op("se", ACT(acc_ap, acc_ap, AF.Silu), reads=[accb], writes=[accb])
        k.op("ve", TT(acc_ap, acc_ap, af.ap("SRT"), ALU.mult),
             reads=[accb, af.buf("SRT")], writes=[accb])
        k.op("ve", TT(acc_ap, acc_ap, af.ap(slot), ALU.add),
             reads=[accb, af.buf(slot)], writes=[accb])
        C.store("sp", xo_tiled[mt], acc_ap, [accb], wbufs=[xout_b])
    af.free("SRT"); af.free("ACC"); af.free("SCR")
    for nm in rolls:
        af.free(nm)
    for nm in roll_bf:
        abf.free(nm)
    af.free("STATS")
    abf.free("IDENT")
    af.free("ONES1")



def host_prep_l1(inp):
    ct = np.ascontiguousarray
    d = {}
    d["wqkvT"] = ct(inp["L1_attn_w_qkv"].T)
    d["wprojT"] = ct(inp["L1_attn_w_proj"].T)
    d["bproj"] = ct(inp["L1_attn_b_proj"].reshape(1, D))
    d["gwT"] = ct(inp["L1_ffn_gate_w"].T)
    d["sgT"] = ct(inp["L1_ffn_shared_gate"].T)
    d["suT"] = ct(inp["L1_ffn_shared_up"].T)
    d["sdT"] = ct(inp["L1_ffn_shared_down"].T)
    d["rgT"] = ct(inp["L1_ffn_r_gate"].transpose(0, 2, 1))
    d["ruT"] = ct(inp["L1_ffn_r_up"].transpose(0, 2, 1))
    d["rdT"] = ct(inp["L1_ffn_r_down"].transpose(0, 2, 1))
    d["ident"] = np.eye(128, dtype=np.float32)
    E = np.zeros((128, 896), dtype=np.float32)
    for i in range(128):
        E[i, 384 + i:] = 1.0
    d["emask"] = E
    return d


L1_DRAM_SPECS = [
    ("wqkvT", [D, 3 * D]), ("wprojT", [D, D]), ("bproj", [1, D]),
    ("gwT", [D, 8]), ("sgT", [D, 2048]), ("suT", [D, 2048]), ("sdT", [2048, D]),
    ("rgT", [8, D, 2048]), ("ruT", [8, D, 2048]), ("rdT", [8, 2048, D]),
    ("ident", [128, 128]), ("emask", [128, 896]),
]




def emit_layer3(C, names):
    nc, k = C.nc, C.k
    xin_b = names["x_in_buf"]
    xout_b = names["x_out_buf"]
    ssp_b = names["s_spill_buf"]
    abf, af = C.abf, C.af
    HG = 4
    NGRP = 4

    x_in, x_out, s_spill = names["x_in"], names["x_out"], names["s_spill"]
    x_tiled = x_in.rearrange("(a p) c -> a p c", p=128)
    s_tiled = s_spill.rearrange("(a p) c -> a p c", p=128)
    xo_tiled = x_out.rearrange("(a p) c -> a p c", p=128)

    abf.alloc("XT", NCT * T)
    abf.alloc("IDENT", 128)
    C.load("gp", abf.ap("IDENT"), names["ident"], abf.buf("IDENT"))
    af.alloc("STATS", 128)
    af.alloc("ONES1", 128)
    k.op("ve", MEMSET(af.t[0:1, af.off("ONES1"):af.off("ONES1") + 128], 1.0), writes=[af.buf("ONES1")])
    af.alloc("JUNK", 1024)
    af.alloc("LGT", NT * 8)
    af.alloc("CW", NT * 8)
    af.alloc("SSC", NT * 8)
    rolls = ["XR0", "XR1"]
    for nm in rolls:
        af.alloc(nm, 1024)
    roll_bf = ["XNR0", "XNR1"]
    for nm in roll_bf:
        abf.alloc(nm, 1024)

    emit_ln_stats(C, x_tiled, rolls, "STATS", src_buf=names["x_in_buf"],
                  per_tile_cb=lambda tt, slot: emit_norm_transpose(
                      C, tt, slot, roll_bf, "XT",
                      C.af.ap("STATS", tt * 8 + 4, tt * 8 + 5),
                      C.af.ap("STATS", tt * 8 + 5, tt * 8 + 6), [C.af.buf("STATS")]))

    abf.alloc("OUTCT", NCT * T)
    abf.alloc("PST", 16 * 512)
    pstb = abf.buf("PST")
    pst_tiles = [k.buf(f"l3pst{j}") for j in range(16)]
    pst_first = [True] * 16
    for nm in ("SP100", "SP101", "SP110", "SP111"):
        af.alloc(nm, 512)
    af.alloc("STRB0", 3968)
    af.alloc("STRB1", 3968)
    af.alloc("OSTG", 1024)
    af.alloc("DENS", 1024)
    af.alloc("ONES64", 64)
    k.op("ve", MEMSET(af.rap("ONES64", 64, 65, 0, 64), 1.0), writes=[af.buf("ONES64")])
    af.alloc("BCOL", 24)
    C.load("sp", af.ap("BCOL", 0, 16), names["b_in_cols"], af.buf("BCOL"))
    af.alloc("VBIAS", 256)
    af.alloc("BINV", 1024)
    C.load("sp", af.t[0:1, af.off("BINV"):af.off("BINV") + 1024], names["b_inR"][0:1, 2048:3072], af.buf("BINV"))

    win = names["W_in"]
    for hg in range(NGRP):
        abf.alloc("QT", 2 * T)
        abf.alloc("KT", 2 * T)
        VSTR = HG * 66
        abf.alloc("VB", NT * VSTR)
        abf.alloc("WQ", NCT * 128)
        abf.alloc("WK", NCT * 128)
        abf.alloc("WV", NCT * 256)
        for ct in range(NCT):
            C.load("gp", abf.ap("WV", ct * 256, (ct + 1) * 256),
                   win[ct * 128:(ct + 1) * 128, 2048 + hg * 256:2048 + (hg + 1) * 256], abf.buf("WV"))
        vb4 = abf.vap("VB", 0, NT * VSTR, "p (t h c) -> p t h c", h=HG, c=66)
        k.op("ve", MEMSET(vb4[:, :, :, 64:66], 1.0), writes=[abf.buf("VB")])
        k.op("ve", MEMSET(vb4[:, :, :, 65:66], 0.0), writes=[abf.buf("VB")])
        k.op("te", MM(C.ps["C"][:, 0:256],
                      af.t[0:1, af.off("ONES1"):af.off("ONES1") + 128],
                      af.t[0:1, af.off("BINV") + hg * 256:af.off("BINV") + (hg + 1) * 256],
                      True, True),
             reads=[af.buf("ONES1"), af.buf("BINV")], writes=[C.psbuf["C"]])
        k.op("ve", CP(af.ap("VBIAS"), C.ps["C"][:, 0:256]),
             reads=[C.psbuf["C"]], writes=[af.buf("VBIAS")])

        for pr in range(2):
            for ct in range(NCT):
                C.load("gp", abf.ap("WQ", ct * 128, (ct + 1) * 128),
                       win[ct * 128:(ct + 1) * 128, hg * 256 + pr * 128:hg * 256 + (pr + 1) * 128],
                       abf.buf("WQ"))
                C.load("gp", abf.ap("WK", ct * 128, (ct + 1) * 128),
                       win[ct * 128:(ct + 1) * 128, 1024 + hg * 256 + pr * 128:1024 + hg * 256 + (pr + 1) * 128],
                       abf.buf("WK"))
            acol = hg * 2 + pr
            for dst, wnm, boff in (("QT", "WQ", acol), ("KT", "WK", 8 + acol)):
                for qc in range(4):
                    pj = ("C", "D")[qc % 2]
                    for kt in range(NCT):
                        k.op("te", MM(C.ps[pj][:, 0:512],
                                      abf.ap(wnm, kt * 128, (kt + 1) * 128),
                                      abf.ap("XT", kt * T + qc * 512, kt * T + (qc + 1) * 512),
                                      kt == 0, kt == NCT - 1),
                             reads=[abf.buf(wnm), abf.buf("XT")], writes=[C.psbuf[pj]])
                    k.op("ve", TS(abf.ap(dst, pr * T + qc * 512, pr * T + (qc + 1) * 512),
                                  C.ps[pj][:, 0:512], af.ap("BCOL", boff, boff + 1), None, ALU.add),
                         reads=[C.psbuf[pj], af.buf("BCOL")], writes=[abf.buf(dst)])
        for mt in range(NT):
            pj = ("D", "C")[mt % 2]
            for kt in range(NCT):
                k.op("te", MM(C.ps[pj][:, 0:256],
                              abf.ap("XT", kt * T + mt * 128, kt * T + (mt + 1) * 128),
                              abf.ap("WV", kt * 256, (kt + 1) * 256),
                              kt == 0, kt == NCT - 1),
                     reads=[abf.buf("WV"), abf.buf("XT")], writes=[C.psbuf[pj]])
            k.op("ve", TT(abf.vap("VB", mt * VSTR, (mt + 1) * VSTR, "p (h c) -> p h c", c=66)[:, :, 0:64],
                          C.ps[pj][:, 0:256].rearrange("p (h d) -> p h d", h=HG),
                          af.ap("VBIAS").rearrange("p (h d) -> p h d", h=HG), ALU.add),
                 reads=[C.psbuf[pj], af.buf("VBIAS")], writes=[abf.buf("VB")])
        abf.free("WQ"); abf.free("WK"); abf.free("WV")

        for gp2 in range(2):
            pr = gp2
            for s_i in range(2):
                h = hg * HG + gp2 * 2 + s_i
                strb = f"STRB{s_i}"
                C.load("sp", af.ap(strb), names["strips"][h], af.buf(strb))
                k.op("ve", TS(af.ap(strb), af.ap(strb), 8.0, None, ALU.mult),
                     reads=[af.buf(strb)], writes=[af.buf(strb)])
            for qc in range(4):
                q0 = qc * 512
                tt_ops = [{}, {}]
                pw_ops = [{}, {}]

                def av3(kt, s_i):
                    off = s_i * 512
                    sl = s_i * 8 + kt % 8
                    ed = [pw_ops[1][kt]] if (s_i == 0 and kt in pw_ops[1]) else []
                    k.op("te", MM(C.ps["C"][0:66, off:off + 512],
                                  abf.ap("VB", kt * VSTR + (gp2 * 2 + s_i) * 66,
                                         kt * VSTR + (gp2 * 2 + s_i + 1) * 66),
                                  abf.ap("PST", sl * 512, (sl + 1) * 512),
                                  kt == 0, kt == NT - 1, skip=True),
                         reads=[abf.buf("VB"), pst_tiles[sl]], writes=[C.psbuf["C"]],
                         extra_deps=ed)

                for kt in range(NT):
                    for s_i in range(2):
                        half = s_i * 64
                        pband = ("A", "B")[kt % 2] if s_i == 0 else ("E", "D")[kt % 2]
                        psap = C.ps[pband][:, 0:512]
                        sp1n = f"SP1{s_i}{kt % 2}"
                        sp1b = af.buf(sp1n)
                        s0 = 1920 + q0 - kt * 128
                        sl = s_i * 8 + kt % 8
                        ed = [tt_ops[1][kt - 2]] if (s_i == 0 and (kt - 2) in tt_ops[1]) else []
                        k.op("te", MM(psap,
                                      abf.rap("KT", half, half + 64, pr * T + kt * 128, pr * T + (kt + 1) * 128),
                                      abf.rap("QT", half, half + 64, pr * T + q0, pr * T + q0 + 512),
                                      True, True),
                             reads=[abf.buf("KT"), abf.buf("QT")], writes=[C.psbuf[pband]],
                             extra_deps=ed)
                        tt_ops[s_i][kt] = k.op("ve", TT(af.ap(sp1n), psap,
                                                        af.ap(f"STRB{s_i}", s0, s0 + 512), ALU.add),
                                               reads=[C.psbuf[pband], af.buf(f"STRB{s_i}")], writes=[sp1b])
                        wl = [pst_tiles[sl]]
                        if pst_first[sl]:
                            wl.append(pstb)
                            pst_first[sl] = False
                        pw_ops[s_i][kt] = k.op("se", ACT(abf.ap("PST", sl * 512, (sl + 1) * 512), af.ap(sp1n),
                                                         AF.Exp, bias=0.0, scale=0.125),
                                               reads=[sp1b], writes=wl)
                    if kt >= 2:
                        av3(kt - 2, 0)
                        av3(kt - 2, 1)
                for ktl in (NT - 2, NT - 1):
                    av3(ktl, 0)
                    av3(ktl, 1)
                for s_i in range(2):
                    off = s_i * 512
                    h = hg * HG + gp2 * 2 + s_i
                    k.op("ve", CP(af.rap("OSTG", 0, 64, off, off + 512), C.ps["C"][0:64, off:off + 512]),
                         reads=[C.psbuf["C"]], writes=[af.buf("OSTG")])
                    k.op("se", ACT(af.rap("DENS", 64, 65, off, off + 512), C.ps["C"][64:65, off:off + 512], AF.Ln),
                         reads=[C.psbuf["C"]], writes=[af.buf("DENS")])
                    k.op("se", ACT(af.rap("DENS", 64, 65, off, off + 512),
                                   af.rap("DENS", 64, 65, off, off + 512), AF.Exp, scale=-1.0),
                         reads=[af.buf("DENS")], writes=[af.buf("DENS")])
                    k.op("te", MM(C.ps["D"][0:64, off:off + 512], af.rap("ONES64", 64, 65, 0, 64),
                                  af.rap("DENS", 64, 65, off, off + 512), True, True, skip=True),
                         reads=[af.buf("ONES64"), af.buf("DENS")], writes=[C.psbuf["D"]])
                    ct2, chalf = h // 2, (h % 2) * 64
                    k.op("ve", TT(abf.rap("OUTCT", chalf, chalf + 64, ct2 * T + q0, ct2 * T + q0 + 512),
                                  af.rap("OSTG", 0, 64, off, off + 512),
                                  C.ps["D"][0:64, off:off + 512], ALU.mult),
                         reads=[af.buf("OSTG"), C.psbuf["D"]], writes=[abf.buf("OUTCT")])
        abf.free("QT"); abf.free("KT"); abf.free("VB")
    abf.free("PST")
    for nm in ("SP100", "SP101", "SP110", "SP111", "STRB0", "STRB1"):
        af.free(nm)
    af.free("OSTG"); af.free("DENS"); af.free("ONES64"); af.free("BCOL"); af.free("VBIAS"); af.free("BINV")

    abf.alloc("WP", NCT * 1024)
    for ct in range(NCT):
        C.load("gp", abf.ap("WP", ct * 1024, (ct + 1) * 1024),
               names["W_out"][ct * 128:(ct + 1) * 128, :], abf.buf("WP"))
    af.alloc("BIAS", 1024)
    emit_bias_rep(C, names["b_out"], "BIAS")
    af.alloc("SOUT", 2 * 1024)
    for mt in range(NT):
        pb = "C" if mt % 2 == 0 else "D"
        for ct in range(NCT):
            for half in range(2):
                k.op("te", MM(C.ps[pb][:, half * 512:(half + 1) * 512],
                              abf.ap("OUTCT", ct * T + mt * 128, ct * T + (mt + 1) * 128),
                              abf.ap("WP", ct * 1024 + half * 512, ct * 1024 + (half + 1) * 512),
                              ct == 0, ct == NCT - 1),
                     reads=[abf.buf("OUTCT"), abf.buf("WP")], writes=[C.psbuf[pb]])
        sl = mt % 2
        k.op("ve", TT(af.ap("SOUT", sl * 1024, (sl + 1) * 1024), C.ps[pb][:, 0:1024], af.ap("BIAS"), ALU.add),
             reads=[C.psbuf[pb], af.buf("BIAS")], writes=[af.buf("SOUT")])
        C.store("sp", s_tiled[mt], af.ap("SOUT", sl * 1024, (sl + 1) * 1024), [af.buf("SOUT")], wbufs=[ssp_b])
        if "s_dbg" in names:
            C.store("sp", names["s_dbg"].rearrange("(a p) c -> a p c", p=128)[mt],
                    af.ap("SOUT", sl * 1024, (sl + 1) * 1024), [af.buf("SOUT")])
    abf.free("WP"); abf.free("OUTCT"); af.free("BIAS"); af.free("SOUT")

    abf.alloc("GW", NCT * 8)
    C.load("gp", abf.vap("GW", 0, NCT * 8, "p (a c) -> p a c", c=8),
           names["gwT"].rearrange("(a p) c -> p a c", p=128), abf.buf("GW"))
    for mt in range(NT):
        for kt in range(NCT):
            k.op("te", MM(C.ps["E"][:, 0:8],
                          abf.ap("XT", kt * T + mt * 128, kt * T + (mt + 1) * 128),
                          abf.ap("GW", kt * 8, (kt + 1) * 8),
                          kt == 0, kt == NCT - 1),
                 reads=[abf.buf("XT"), abf.buf("GW")], writes=[C.psbuf["E"]])
        k.op("ve", CP(af.ap("LGT", mt * 8, (mt + 1) * 8), C.ps["E"][:, 0:8]),
             reads=[C.psbuf["E"]], writes=[af.buf("LGT")])
    abf.free("GW")
    BIG = 1.0e30
    for mt in range(NT):
        b8 = mt * 8
        lg = af.ap("LGT", mt * 8, (mt + 1) * 8)
        lgb = af.buf("LGT")
        scb = af.buf("SSC")
        cwb = af.buf("CW")
        cw = af.ap("CW", mt * 8, (mt + 1) * 8)
        k.op("ve", fragile=True, emit=RED(af.ap("SSC", b8, b8 + 1), lg, ALU.max, negate=True),
             reads=[lgb], writes=[scb])
        k.op("se", ACT(lg, lg, AF.Exp, bias=af.ap("SSC", b8, b8 + 1), scale=1.0,
                       accum=af.ap("SSC", b8 + 1, b8 + 2)),
             reads=[lgb, scb], writes=[lgb, scb])
        k.op("ve", fragile=True, emit=RECIP(af.ap("SSC", b8 + 2, b8 + 3), af.ap("SSC", b8 + 1, b8 + 2)),
             reads=[scb], writes=[scb])
        k.op("se", ACT(lg, lg, AF.Identity, bias=0.0, scale=af.ap("SSC", b8 + 2, b8 + 3)),
             reads=[lgb, scb], writes=[lgb])
        k.op("ve", fragile=True, emit=RED(af.ap("SSC", b8 + 3, b8 + 4), lg, ALU.max),
             reads=[lgb], writes=[scb])
        k.op("ve", TS(af.ap("CW", mt * 8, (mt + 1) * 8), lg, af.ap("SSC", b8 + 3, b8 + 4), None, ALU.is_ge),
             reads=[lgb, scb], writes=[cwb])
        k.op("ve", STT(af.ap("JUNK", 0, 8), cw, -BIG, lg, ALU.mult, ALU.add),
             reads=[cwb, lgb], writes=[af.buf("JUNK")])
        k.op("ve", fragile=True, emit=RED(af.ap("SSC", b8 + 5, b8 + 6), af.ap("JUNK", 0, 8), ALU.max),
             reads=[af.buf("JUNK")], writes=[scb])
        k.op("ve", TS(af.ap("JUNK", 8, 16), af.ap("JUNK", 0, 8), af.ap("SSC", b8 + 5, b8 + 6), None, ALU.is_ge),
             reads=[af.buf("JUNK"), scb], writes=[af.buf("JUNK")])
        if "dbg_rt" in names and mt == 0:
            C.store("sp", names["dbg_rt"][:, 0:16], af.ap("JUNK", 0, 16), [af.buf("JUNK")])
            C.store("sp", names["dbg_rt"][:, 16:24], af.ap("SSC", b8, b8 + 8), [scb])
            C.store("sp", names["dbg_rt"][:, 24:32], lg, [lgb])
        k.op("ve", TT(cw, cw, af.ap("JUNK", 8, 16), ALU.add),
             reads=[cwb, af.buf("JUNK")], writes=[cwb])
        k.op("ve", TT(cw, cw, lg, ALU.mult),
             reads=[cwb, lgb], writes=[cwb])

    if "dbg_cw" in names:
        C.store("sp", names["dbg_cw"].rearrange("a p c -> p a c"), af.ap("CW", 0, NT * 8), [af.buf("CW")])
    af.alloc("ACC", NT * 1024)
    accb = af.buf("ACC")
    af.alloc("SILU", 512)
    silub = af.buf("SILU")
    abf.alloc("H3", 4 * 2048)
    for sl in range(2):
        abf.alloc(f"EWG{sl}", 8 * 512)
        abf.alloc(f"EWU{sl}", 8 * 512)
        abf.alloc(f"EWD{sl}", 4 * 1024)

    def expert3(ei, wgt, wut, wdt, masked, sl):
        wg_s, wu_s, wd_s = f"EWG{sl}", f"EWU{sl}", f"EWD{sl}"
        C.load("gp", abf.vap(wg_s, 0, 8 * 512, "p (a c) -> p a c", c=512),
               wgt.rearrange("(a p) f -> p a f", p=128), abf.buf(wg_s))
        C.load("gp", abf.vap(wu_s, 0, 8 * 512, "p (a c) -> p a c", c=512),
               wut.rearrange("(a p) f -> p a f", p=128), abf.buf(wu_s))
        C.load("gp", abf.vap(wd_s, 0, 4 * 1024, "p (a c) -> p a c", c=1024),
               wdt.rearrange("(a p) f -> p a f", p=128), abf.buf(wd_s))
        for ft in range(4):
            for qc in range(4):
                q0 = qc * 512
                for kt in range(NCT):
                    k.op("te", MM(C.ps["A"][:, 0:512],
                                  abf.ap(wg_s, kt * 512 + ft * 128, kt * 512 + (ft + 1) * 128),
                                  abf.ap("XT", kt * T + q0, kt * T + q0 + 512),
                                  kt == 0, kt == NCT - 1),
                         reads=[abf.buf(wg_s), abf.buf("XT")], writes=[C.psbuf["A"]])
                k.op("se", ACT(af.ap("SILU"), C.ps["A"][:, 0:512], AF.Silu),
                     reads=[C.psbuf["A"]], writes=[silub])
                for kt in range(NCT):
                    k.op("te", MM(C.ps["B"][:, 0:512],
                                  abf.ap(wu_s, kt * 512 + ft * 128, kt * 512 + (ft + 1) * 128),
                                  abf.ap("XT", kt * T + q0, kt * T + q0 + 512),
                                  kt == 0, kt == NCT - 1),
                         reads=[abf.buf(wu_s), abf.buf("XT")], writes=[C.psbuf["B"]])
                k.op("ve", TT(abf.ap("H3", ft * 2048 + q0, ft * 2048 + q0 + 512), af.ap("SILU"),
                              C.ps["B"][:, 0:512], ALU.mult),
                     reads=[silub, C.psbuf["B"]], writes=[abf.buf("H3")])
        for mt in range(NT):
            pb = "C" if mt % 2 == 0 else "D"
            for ft in range(4):
                for half in range(2):
                    k.op("te", MM(C.ps[pb][:, half * 512:(half + 1) * 512],
                                  abf.ap("H3", ft * 2048 + mt * 128, ft * 2048 + (mt + 1) * 128),
                                  abf.ap(wd_s, ft * 1024 + half * 512, ft * 1024 + (half + 1) * 512),
                                  ft == 0, ft == 3),
                         reads=[abf.buf("H3"), abf.buf(wd_s)], writes=[C.psbuf[pb]])
            if masked:
                k.op("ve", STT(af.ap("ACC", mt * 1024, (mt + 1) * 1024), C.ps[pb][:, 0:1024],
                               af.ap("CW", mt * 8 + ei, mt * 8 + ei + 1),
                               af.ap("ACC", mt * 1024, (mt + 1) * 1024), ALU.mult, ALU.add),
                     reads=[C.psbuf[pb], af.buf("CW"), accb], writes=[accb])
            else:
                k.op("ve", CP(af.ap("ACC", mt * 1024, (mt + 1) * 1024), C.ps[pb][:, 0:1024]),
                     reads=[C.psbuf[pb]], writes=[accb])

    expert3(-1, names["sgT"], names["suT"], names["sdT"], False, 0)
    if "dbg_sh" in names:
        C.store("sp", names["dbg_sh"].rearrange("a p c -> p a c"), af.ap("ACC", 0, NT * 1024), [accb])
    for e_i in range(8):
        expert3(e_i, names["wgT"][e_i], names["wuT"][e_i], names["wdT"][e_i], True, (e_i + 1) % 2)
    if "dbg_acc" in names:
        C.store("sp", names["dbg_acc"].rearrange("a p c -> p a c"), af.ap("ACC", 0, NT * 1024), [accb])

    abf.free("H3"); abf.free("XT")
    for sl in range(2):
        abf.free(f"EWG{sl}"); abf.free(f"EWU{sl}"); abf.free(f"EWD{sl}")
    af.free("SILU");  af.free("LGT"); af.free("CW"); af.free("SSC"); af.free("JUNK")

    af.alloc("SRT", 1024)
    for mt in range(NT):
        slot = rolls[mt % 2]
        C.load("sp", af.ap("SRT"), s_tiled[mt], af.buf("SRT"), rbufs=[ssp_b])
        C.load("sp", af.ap(slot), x_tiled[mt], af.buf(slot), rbufs=[xin_b])
        acc_ap = af.ap("ACC", mt * 1024, (mt + 1) * 1024)
        k.op("se", ACT(acc_ap, acc_ap, AF.Silu), reads=[accb], writes=[accb])
        k.op("ve", TT(acc_ap, acc_ap, af.ap("SRT"), ALU.mult),
             reads=[accb, af.buf("SRT")], writes=[accb])
        k.op("ve", TT(acc_ap, acc_ap, af.ap(slot), ALU.add),
             reads=[accb, af.buf(slot)], writes=[accb])
        C.store("sp", xo_tiled[mt], acc_ap, [accb], wbufs=[xout_b])
    af.free("SRT"); af.free("ACC")
    for nm in rolls:
        af.free(nm)
    for nm in roll_bf:
        abf.free(nm)
    af.free("STATS")
    abf.free("IDENT")
    af.free("ONES1")


def host_prep_l3(inp):
    ct = np.ascontiguousarray
    d = {}
    d["W_in"] = ct(inp["L3_attn_W_in"])
    b_in = inp["L3_attn_b_in"]
    d["b_inR"] = ct(b_in.reshape(1, 3072))
    d["b_in_cols"] = ct(b_in[0:2048].reshape(16, 128).T)
    d["W_out"] = ct(inp["L3_attn_W_out"])
    d["b_out"] = ct(inp["L3_attn_b_out"].reshape(1, D))
    rel = inp["L3_attn_rel_emb"]
    i_idx = np.arange(128)[:, None]
    u_idx = np.arange(3968)[None, :]
    pidx = np.minimum(6015 + i_idx - u_idx, 4095)
    pidx = np.maximum(pidx, 0)
    d["strips"] = ct(rel[pidx, :].transpose(2, 0, 1))
    d["gwT"] = ct(inp["L3_ffn_gate_w"].T)
    d["sgT"] = ct(inp["L3_ffn_sg"].T)
    d["suT"] = ct(inp["L3_ffn_su"].T)
    d["sdT"] = ct(inp["L3_ffn_sd"].T)
    d["wgT"] = ct(inp["L3_ffn_wg"].transpose(0, 2, 1))
    d["wuT"] = ct(inp["L3_ffn_wu"].transpose(0, 2, 1))
    d["wdT"] = ct(inp["L3_ffn_wd"].transpose(0, 2, 1))
    d["ident"] = np.eye(128, dtype=np.float32)
    return d


L3_DRAM_SPECS = [
    ("W_in", [D, 3 * D]), ("b_inR", [1, 3 * D]), ("b_in_cols", [128, 16]),
    ("W_out", [D, D]), ("b_out", [1, D]), ("strips", [16, 128, 3968]),
    ("gwT", [D, 8]), ("sgT", [D, 512]), ("suT", [D, 512]), ("sdT", [512, D]),
    ("wgT", [8, D, 512]), ("wuT", [8, D, 512]), ("wdT", [8, 512, D]),
    ("ident", [128, 128]),
]




def emit_layer2(C, names):
    nc, k = C.nc, C.k
    xin_b = names["x_in_buf"]
    xout_b = names["x_out_buf"]
    ssp_b = names["s_spill_buf"]
    abf, af = C.abf, C.af

    x_in, x_out, s_spill = names["x_in"], names["x_out"], names["s_spill"]
    x_tiled = x_in.rearrange("(a p) c -> a p c", p=128)
    s_tiled = s_spill.rearrange("(a p) c -> a p c", p=128)
    xo_tiled = x_out.rearrange("(a p) c -> a p c", p=128)

    abf.alloc("XT", NCT * T)
    abf.alloc("IDENT", 128)
    C.load("gp", abf.ap("IDENT"), names["ident"], abf.buf("IDENT"))
    af.alloc("STATS", 128)
    af.alloc("ONES1", 128)
    k.op("ve", MEMSET(af.t[0:1, af.off("ONES1"):af.off("ONES1") + 128], 1.0), writes=[af.buf("ONES1")])
    af.alloc("JUNK", 1024)
    af.alloc("LGT", NT * 16)
    af.alloc("CW", NT * 16)
    af.alloc("ONES4", 8)
    k.op("ve", MEMSET(af.ap("ONES4"), 1.0), writes=[af.buf("ONES4")])
    rolls = ["XR0", "XR1"]
    for nm in rolls:
        af.alloc(nm, 1024)
    roll_bf = ["XNR0", "XNR1"]
    for nm in roll_bf:
        abf.alloc(nm, 1024)

    emit_ln_stats(C, x_tiled, rolls, "STATS", src_buf=names["x_in_buf"],
                  per_tile_cb=lambda tt, slot: emit_norm_transpose(
                      C, tt, slot, roll_bf, "XT",
                      C.af.ap("STATS", tt * 8 + 4, tt * 8 + 5),
                      C.af.ap("STATS", tt * 8 + 5, tt * 8 + 6), [C.af.buf("STATS")]))

    abf.alloc("OUTCT", NCT * T)
    abf.alloc("PST", 16 * 512)
    pstb = abf.buf("PST")
    pst_tiles = [k.buf(f"l2pst{j}") for j in range(16)]
    pst_first = [True] * 16
    abf.alloc("HT", T)
    abf.alloc("QT1", T)
    VSTR = 2 * 66
    abf.alloc("VB", NT * VSTR)
    abf.alloc("PW", NCT * 128)
    abf.alloc("AWB", 128)
    abf.alloc("VWB", 128)
    af.alloc("OSTG", 1024)
    af.alloc("DENS", 1024)
    af.alloc("ONES64", 64)
    k.op("ve", MEMSET(af.rap("ONES64", 64, 65, 0, 64), 1.0), writes=[af.buf("ONES64")])

    for pr in range(8):
        for ct in range(NCT):
            C.load("gp", abf.ap("PW", ct * 128, (ct + 1) * 128),
                   names["projw"][pr][ct * 128:(ct + 1) * 128, :], abf.buf("PW"))
        C.load("gp", abf.ap("AWB"), names["attnw"][pr], abf.buf("AWB"))
        C.load("gp", abf.ap("VWB"), names["valuew"][pr], abf.buf("VWB"))
        vb4 = abf.vap("VB", 0, NT * VSTR, "p (t h c) -> p t h c", h=2, c=66)
        k.op("ve", MEMSET(vb4[:, :, :, 64:66], 1.0), writes=[abf.buf("VB")])
        k.op("ve", MEMSET(vb4[:, :, :, 65:66], 0.0), writes=[abf.buf("VB")])
        for qc in range(4):
            pj = ("C", "D")[qc % 2]
            for kt in range(NCT):
                k.op("te", MM(C.ps[pj][:, 0:512],
                              abf.ap("PW", kt * 128, (kt + 1) * 128),
                              abf.ap("XT", kt * T + qc * 512, kt * T + (qc + 1) * 512),
                              kt == 0, kt == NCT - 1),
                     reads=[abf.buf("PW"), abf.buf("XT")], writes=[C.psbuf[pj]])
            k.op("ve", CP(abf.ap("HT", qc * 512, (qc + 1) * 512), C.ps[pj][:, 0:512]),
                 reads=[C.psbuf[pj]], writes=[abf.buf("HT")])
        for qc in range(4):
            pj = ("C", "D")[qc % 2]
            k.op("te", MM(C.ps[pj][:, 0:512], abf.ap("AWB"),
                          abf.ap("HT", qc * 512, (qc + 1) * 512), True, True),
                 reads=[abf.buf("AWB"), abf.buf("HT")], writes=[C.psbuf[pj]])
            k.op("ve", CP(abf.ap("QT1", qc * 512, (qc + 1) * 512), C.ps[pj][:, 0:512]),
                 reads=[C.psbuf[pj]], writes=[abf.buf("QT1")])
        for mt in range(NT):
            pj = ("D", "C")[mt % 2]
            k.op("te", MM(C.ps[pj][:, 0:128],
                          abf.ap("HT", mt * 128, (mt + 1) * 128),
                          abf.ap("VWB"), True, True),
                 reads=[abf.buf("HT"), abf.buf("VWB")], writes=[C.psbuf[pj]])
            k.op("ve", CP(abf.vap("VB", mt * VSTR, (mt + 1) * VSTR, "p (h c) -> p h c", c=66)[:, :, 0:64],
                          C.ps[pj][:, 0:128].rearrange("p (h d) -> p h d", h=2)),
                 reads=[C.psbuf[pj]], writes=[abf.buf("VB")])
        for qc in range(4):
            q0 = qc * 512
            pw_ops = [{}, {}]

            def av2b(kt, s_i):
                off = s_i * 512
                sl = s_i * 8 + kt % 8
                ed = [pw_ops[1][kt]] if (s_i == 0 and kt in pw_ops[1]) else []
                k.op("te", MM(C.ps["C"][0:66, off:off + 512],
                              abf.ap("VB", kt * VSTR + s_i * 66, kt * VSTR + (s_i + 1) * 66),
                              abf.ap("PST", sl * 512, (sl + 1) * 512),
                              kt == 0, kt == NT - 1, skip=True),
                     reads=[abf.buf("VB"), pst_tiles[sl]], writes=[C.psbuf["C"]],
                     extra_deps=ed)

            for kt in range(NT):
                for s_i in range(2):
                    half = s_i * 64
                    pband = ("A", "B")[kt % 2] if s_i == 0 else ("E", "D")[kt % 2]
                    psap = C.ps[pband][:, 0:512]
                    sl = s_i * 8 + kt % 8
                    ed = [pw_ops[1][kt - 2]] if (s_i == 0 and (kt - 2) in pw_ops[1]) else []
                    k.op("te", MM(psap,
                                  abf.rap("HT", half, half + 64, kt * 128, (kt + 1) * 128),
                                  abf.rap("QT1", half, half + 64, q0, q0 + 512),
                                  True, True),
                         reads=[abf.buf("HT"), abf.buf("QT1")], writes=[C.psbuf[pband]],
                         extra_deps=ed)
                    wl = [pst_tiles[sl]]
                    if pst_first[sl]:
                        wl.append(pstb)
                        pst_first[sl] = False
                    pw_ops[s_i][kt] = k.op("se", ACT(abf.ap("PST", sl * 512, (sl + 1) * 512), psap,
                                                     AF.Exp, bias=0.0, scale=1.0 / 64.0),
                                           reads=[C.psbuf[pband]], writes=wl)
                if kt >= 2:
                    av2b(kt - 2, 0)
                    av2b(kt - 2, 1)
            for ktl in (NT - 2, NT - 1):
                av2b(ktl, 0)
                av2b(ktl, 1)
            for s_i in range(2):
                off = s_i * 512
                h = 2 * pr + s_i
                k.op("ve", CP(af.rap("OSTG", 0, 64, off, off + 512), C.ps["C"][0:64, off:off + 512]),
                     reads=[C.psbuf["C"]], writes=[af.buf("OSTG")])
                k.op("se", ACT(af.rap("DENS", 64, 65, off, off + 512), C.ps["C"][64:65, off:off + 512], AF.Ln),
                     reads=[C.psbuf["C"]], writes=[af.buf("DENS")])
                k.op("se", ACT(af.rap("DENS", 64, 65, off, off + 512),
                               af.rap("DENS", 64, 65, off, off + 512), AF.Exp, scale=-1.0),
                     reads=[af.buf("DENS")], writes=[af.buf("DENS")])
                k.op("te", MM(C.ps["D"][0:64, off:off + 512], af.rap("ONES64", 64, 65, 0, 64),
                              af.rap("DENS", 64, 65, off, off + 512), True, True, skip=True),
                     reads=[af.buf("ONES64"), af.buf("DENS")], writes=[C.psbuf["D"]])
                ct2, chalf = h // 2, (h % 2) * 64
                k.op("ve", TT(abf.rap("OUTCT", chalf, chalf + 64, ct2 * T + q0, ct2 * T + q0 + 512),
                              af.rap("OSTG", 0, 64, off, off + 512),
                              C.ps["D"][0:64, off:off + 512], ALU.mult),
                     reads=[af.buf("OSTG"), C.psbuf["D"]], writes=[abf.buf("OUTCT")])
    abf.free("PST"); abf.free("HT"); abf.free("QT1"); abf.free("VB")
    abf.free("PW"); abf.free("AWB"); abf.free("VWB")
    af.free("OSTG"); af.free("DENS"); af.free("ONES64")

    abf.alloc("WP", NCT * 1024)
    for ct in range(NCT):
        C.load("gp", abf.ap("WP", ct * 1024, (ct + 1) * 1024),
               names["outwT"][ct * 128:(ct + 1) * 128, :], abf.buf("WP"))
    af.alloc("BIAS", 1024)
    emit_bias_rep(C, names["outb"], "BIAS")
    af.alloc("SOUT", 2 * 1024)
    for mt in range(NT):
        pb = "C" if mt % 2 == 0 else "D"
        for ct in range(NCT):
            for half in range(2):
                k.op("te", MM(C.ps[pb][:, half * 512:(half + 1) * 512],
                              abf.ap("OUTCT", ct * T + mt * 128, ct * T + (mt + 1) * 128),
                              abf.ap("WP", ct * 1024 + half * 512, ct * 1024 + (half + 1) * 512),
                              ct == 0, ct == NCT - 1),
                     reads=[abf.buf("OUTCT"), abf.buf("WP")], writes=[C.psbuf[pb]])
        sl = mt % 2
        k.op("ve", TT(af.ap("SOUT", sl * 1024, (sl + 1) * 1024), C.ps[pb][:, 0:1024], af.ap("BIAS"), ALU.add),
             reads=[C.psbuf[pb], af.buf("BIAS")], writes=[af.buf("SOUT")])
        C.store("sp", s_tiled[mt], af.ap("SOUT", sl * 1024, (sl + 1) * 1024), [af.buf("SOUT")], wbufs=[ssp_b])
        if "s_dbg" in names:
            C.store("sp", names["s_dbg"].rearrange("(a p) c -> a p c", p=128)[mt],
                    af.ap("SOUT", sl * 1024, (sl + 1) * 1024), [af.buf("SOUT")])
    abf.free("WP"); abf.free("OUTCT"); af.free("BIAS"); af.free("SOUT")

    abf.alloc("GW", NCT * 16)
    C.load("gp", abf.vap("GW", 0, NCT * 16, "p (a c) -> p a c", c=16),
           names["rwT"].rearrange("(a p) c -> p a c", p=128), abf.buf("GW"))
    for mt in range(NT):
        for kt in range(NCT):
            k.op("te", MM(C.ps["E"][:, 0:16],
                          abf.ap("XT", kt * T + mt * 128, kt * T + (mt + 1) * 128),
                          abf.ap("GW", kt * 16, (kt + 1) * 16),
                          kt == 0, kt == NCT - 1),
                 reads=[abf.buf("XT"), abf.buf("GW")], writes=[C.psbuf["E"]])
        k.op("ve", CP(af.ap("LGT", mt * 16, (mt + 1) * 16), C.ps["E"][:, 0:16]),
             reads=[C.psbuf["E"]], writes=[af.buf("LGT")])
    abf.free("GW")
    af.alloc("CBR", 16)
    af.alloc("CBROW", 16)
    C.load("sp", af.t[0:1, af.off("CBROW"):af.off("CBROW") + 16], names["cbias"], af.buf("CBROW"))
    k.op("te", MM(C.ps["E"][:, 0:16], af.t[0:1, af.off("ONES1"):af.off("ONES1") + 128],
                  af.t[0:1, af.off("CBROW"):af.off("CBROW") + 16], True, True),
         reads=[af.buf("ONES1"), af.buf("CBROW")], writes=[C.psbuf["E"]])
    k.op("ve", CP(af.ap("CBR"), C.ps["E"][:, 0:16]), reads=[C.psbuf["E"]], writes=[af.buf("CBR")])

    BIG = 1.0e30
    for mt in range(NT):
        b16 = mt * 16
        lgb, cwb, scb, jkb = af.buf("LGT"), af.buf("CW"), af.buf("JUNK"), af.buf("JUNK")
        lg = af.ap("LGT", b16, b16 + 16)
        k.op("se", ACT(lg, lg, AF.Sigmoid), reads=[lgb], writes=[lgb])
        k.op("ve", TT(af.ap("JUNK", 0, 16), lg, af.ap("CBR"), ALU.add),
             reads=[lgb, af.buf("CBR")], writes=[jkb])
        for g in range(4):
            sbg = af.ap("JUNK", g * 4, (g + 1) * 4)
            k.op("ve", RED(af.ap("JUNK", 128 + g, 129 + g), sbg, ALU.max), reads=[jkb], writes=[scb])
            k.op("ve", TS(af.ap("JUNK", 16 + g * 4, 16 + (g + 1) * 4), sbg,
                          af.ap("JUNK", 128 + g, 129 + g), None, ALU.is_ge),
                 reads=[jkb, scb], writes=[jkb])
            k.op("ve", STT(af.ap("JUNK", 32 + g * 4, 32 + (g + 1) * 4),
                           af.ap("JUNK", 16 + g * 4, 16 + (g + 1) * 4), -BIG, sbg, ALU.mult, ALU.add),
                 reads=[jkb], writes=[jkb])
            k.op("ve", RED(af.ap("JUNK", 132 + g, 133 + g), af.ap("JUNK", 32 + g * 4, 32 + (g + 1) * 4),
                           ALU.max), reads=[jkb], writes=[scb])
        k.op("ve", TT(af.ap("JUNK", 48, 52), af.ap("JUNK", 128, 132), af.ap("JUNK", 132, 136), ALU.add),
             reads=[scb], writes=[jkb])
        k.op("ve", RED(af.ap("JUNK", 136, 137), af.ap("JUNK", 48, 52), ALU.max), reads=[jkb], writes=[scb])
        k.op("ve", TS(af.ap("JUNK", 52, 56), af.ap("JUNK", 48, 52), af.ap("JUNK", 136, 137), None, ALU.is_ge),
             reads=[jkb, scb], writes=[jkb])
        k.op("ve", STT(af.ap("JUNK", 56, 60), af.ap("JUNK", 52, 56), -BIG, af.ap("JUNK", 48, 52), ALU.mult, ALU.add),
             reads=[jkb], writes=[jkb])
        k.op("ve", RED(af.ap("JUNK", 137, 138), af.ap("JUNK", 56, 60), ALU.max), reads=[jkb], writes=[scb])
        k.op("ve", TS(af.ap("JUNK", 60, 64), af.ap("JUNK", 56, 60), af.ap("JUNK", 137, 138), None, ALU.is_ge),
             reads=[jkb, scb], writes=[jkb])
        k.op("ve", TT(af.ap("JUNK", 52, 56), af.ap("JUNK", 52, 56), af.ap("JUNK", 60, 64), ALU.add),
             reads=[jkb], writes=[jkb])
        for g in range(4):
            k.op("ve", TS(af.ap("JUNK", 64 + g * 4, 64 + (g + 1) * 4), af.ap("ONES4", 0, 4),
                          af.ap("JUNK", 52 + g, 53 + g), None, ALU.mult),
                 reads=[af.buf("ONES4"), jkb], writes=[jkb])
        k.op("ve", TS(af.ap("JUNK", 64, 80), af.ap("JUNK", 64, 80), -1.0, None, ALU.add),
             reads=[jkb], writes=[jkb])
        k.op("ve", STT(af.ap("JUNK", 80, 96), af.ap("JUNK", 64, 80), BIG, af.ap("JUNK", 0, 16), ALU.mult, ALU.add),
             reads=[jkb], writes=[jkb])
        k.op("ve", MEMSET(af.ap("JUNK", 96, 112), 0.0), writes=[jkb])
        for kk in range(4):
            k.op("ve", RED(af.ap("JUNK", 138 + kk, 139 + kk), af.ap("JUNK", 80, 96), ALU.max),
                 reads=[jkb], writes=[scb])
            k.op("ve", TS(af.ap("JUNK", 112, 128), af.ap("JUNK", 80, 96),
                          af.ap("JUNK", 138 + kk, 139 + kk), None, ALU.is_ge),
                 reads=[jkb, scb], writes=[jkb])
            k.op("ve", TT(af.ap("JUNK", 96, 112), af.ap("JUNK", 96, 112), af.ap("JUNK", 112, 128), ALU.add),
                 reads=[jkb], writes=[jkb])
            if kk < 3:
                k.op("ve", STT(af.ap("JUNK", 80, 96), af.ap("JUNK", 112, 128), -BIG,
                               af.ap("JUNK", 80, 96), ALU.mult, ALU.add),
                     reads=[jkb], writes=[jkb])
        if "dbg_rt2" in names and mt == 0:
            C.store("sp", names["dbg_rt2"][:, 0:144], af.ap("JUNK", 0, 144), [jkb])
            C.store("sp", names["dbg_rt2"][:, 144:160], lg, [lgb])
        k.op("ve", TT(af.ap("CW", b16, b16 + 16), lg, af.ap("JUNK", 96, 112), ALU.mult),
             reads=[lgb, jkb], writes=[cwb])
        k.op("ve", RED(af.ap("JUNK", 142, 143), af.ap("CW", b16, b16 + 16), ALU.add),
             reads=[cwb], writes=[scb])
        k.op("ve", TS(af.ap("JUNK", 142, 143), af.ap("JUNK", 142, 143), 1.0e-20, None, ALU.add),
             reads=[scb], writes=[scb])
        k.op("ve", RECIP(af.ap("JUNK", 143, 144), af.ap("JUNK", 142, 143)),
             reads=[scb], writes=[scb])
        k.op("ve", TS(af.ap("CW", b16, b16 + 16), af.ap("CW", b16, b16 + 16),
                      af.ap("JUNK", 143, 144), None, ALU.mult),
             reads=[cwb, scb], writes=[cwb])
    if "dbg_cw" in names:
        C.store("sp", names["dbg_cw"].rearrange("a p c -> p a c"), af.ap("CW", 0, NT * 16), [af.buf("CW")])

    af.alloc("ACC", NT * 1024)
    accb = af.buf("ACC")
    af.alloc("SILU", 512)
    silub = af.buf("SILU")
    for i in range(16):
        abf.alloc(f"H{i}", 1024)
    for i in range(4):
        abf.alloc(f"WG{i}", NCT * 128)
        abf.alloc(f"WU{i}", NCT * 128)
    abf.alloc("WD", 16 * 1024)
    wdb = abf.buf("WD")

    def big_expert(wgt, wut, wdt):
        for tc in range(2):
            t0 = tc * 1024
            for ft in range(16):
                wg_s, wu_s = f"WG{ft % 4}", f"WU{ft % 4}"
                C.load("gp", abf.vap(wg_s, 0, NCT * 128, "p (a c) -> p a c", c=128),
                       wgt.rearrange("(a p) f -> p a f", p=128)[:, :, ft * 128:(ft + 1) * 128],
                       abf.buf(wg_s))
                C.load("gp", abf.vap(wu_s, 0, NCT * 128, "p (a c) -> p a c", c=128),
                       wut.rearrange("(a p) f -> p a f", p=128)[:, :, ft * 128:(ft + 1) * 128],
                       abf.buf(wu_s))
                if tc == 0:
                    C.load("gp", abf.ap("WD", ft * 1024, (ft + 1) * 1024),
                           wdt[ft * 128:(ft + 1) * 128, :], wdb)
                for nh in range(2):
                    q0 = t0 + nh * 512
                    for kt in range(NCT):
                        k.op("te", MM(C.ps["A"][:, 0:512],
                                      abf.ap(wg_s, kt * 128, (kt + 1) * 128),
                                      abf.ap("XT", kt * T + q0, kt * T + q0 + 512),
                                      kt == 0, kt == NCT - 1),
                             reads=[abf.buf(wg_s), abf.buf("XT")], writes=[C.psbuf["A"]])
                    k.op("se", ACT(af.ap("SILU"), C.ps["A"][:, 0:512], AF.Silu),
                         reads=[C.psbuf["A"]], writes=[silub])
                    for kt in range(NCT):
                        k.op("te", MM(C.ps["B"][:, 0:512],
                                      abf.ap(wu_s, kt * 128, (kt + 1) * 128),
                                      abf.ap("XT", kt * T + q0, kt * T + q0 + 512),
                                      kt == 0, kt == NCT - 1),
                             reads=[abf.buf(wu_s), abf.buf("XT")], writes=[C.psbuf["B"]])
                    k.op("ve", TT(abf.ap(f"H{ft}", nh * 512, (nh + 1) * 512), af.ap("SILU"),
                                  C.ps["B"][:, 0:512], ALU.mult),
                         reads=[silub, C.psbuf["B"]], writes=[abf.buf(f"H{ft}")])
            for mi in range(8):
                mt = tc * 8 + mi
                pb = "C" if mi % 2 == 0 else "D"
                for ft in range(16):
                    for half in range(2):
                        k.op("te", MM(C.ps[pb][:, half * 512:(half + 1) * 512],
                                      abf.ap(f"H{ft}", mi * 128, (mi + 1) * 128),
                                      abf.ap("WD", ft * 1024 + half * 512, ft * 1024 + (half + 1) * 512),
                                      ft == 0, ft == 15),
                             reads=[abf.buf(f"H{ft}"), wdb], writes=[C.psbuf[pb]])
                k.op("ve", CP(af.ap("ACC", mt * 1024, (mt + 1) * 1024), C.ps[pb][:, 0:1024]),
                     reads=[C.psbuf[pb]], writes=[accb])

    big_expert(names["sgT"], names["suT"], names["sdT"])
    for i in range(16):
        abf.free(f"H{i}")
    for i in range(4):
        abf.free(f"WG{i}"); abf.free(f"WU{i}")
    abf.free("WD")

    abf.alloc("H2", 2 * 2048)
    for sl in range(2):
        abf.alloc(f"EWG{sl}", NCT * 256)
        abf.alloc(f"EWU{sl}", NCT * 256)
        abf.alloc(f"EWD{sl}", 2 * 1024)

    def down4(ei, sl, qc):
        wd_s = f"EWD{sl}"
        for mt in range(qc * 4, qc * 4 + 4):
            pb = "C" if mt % 2 == 0 else "D"
            for ft in range(2):
                for half in range(2):
                    k.op("te", MM(C.ps[pb][:, half * 512:(half + 1) * 512],
                                  abf.ap("H2", ft * 2048 + mt * 128, ft * 2048 + (mt + 1) * 128),
                                  abf.ap(wd_s, ft * 1024 + half * 512, ft * 1024 + (half + 1) * 512),
                                  ft == 0, ft == 1),
                         reads=[abf.buf("H2"), abf.buf(wd_s)], writes=[C.psbuf[pb]])
            k.op("ve", STT(af.ap("ACC", mt * 1024, (mt + 1) * 1024), C.ps[pb][:, 0:1024],
                           af.ap("CW", mt * 16 + ei, mt * 16 + ei + 1),
                           af.ap("ACC", mt * 1024, (mt + 1) * 1024), ALU.mult, ALU.add),
                 reads=[C.psbuf[pb], af.buf("CW"), accb], writes=[accb])

    def small_expert(ei, sl, prev):
        wg_s, wu_s, wd_s = f"EWG{sl}", f"EWU{sl}", f"EWD{sl}"
        C.load("gp", abf.vap(wg_s, 0, NCT * 256, "p (a c) -> p a c", c=256),
               names["gatew"][ei].rearrange("(a p) f -> p a f", p=128), abf.buf(wg_s))
        C.load("gp", abf.vap(wu_s, 0, NCT * 256, "p (a c) -> p a c", c=256),
               names["upw"][ei].rearrange("(a p) f -> p a f", p=128), abf.buf(wu_s))
        C.load("gp", abf.vap(wd_s, 0, 2 * 1024, "p (a c) -> p a c", c=1024),
               names["downw"][ei].rearrange("(a p) f -> p a f", p=128), abf.buf(wd_s))
        for qc in range(4):
            if prev is not None:
                down4(prev, 1 - sl, qc)
            q0 = qc * 512
            for ft in range(2):
                for kt in range(NCT):
                    k.op("te", MM(C.ps["A"][:, 0:512],
                                  abf.ap(wg_s, kt * 256 + ft * 128, kt * 256 + (ft + 1) * 128),
                                  abf.ap("XT", kt * T + q0, kt * T + q0 + 512),
                                  kt == 0, kt == NCT - 1),
                         reads=[abf.buf(wg_s), abf.buf("XT")], writes=[C.psbuf["A"]])
                k.op("se", ACT(af.ap("SILU"), C.ps["A"][:, 0:512], AF.Silu),
                     reads=[C.psbuf["A"]], writes=[silub])
                for kt in range(NCT):
                    k.op("te", MM(C.ps["B"][:, 0:512],
                                  abf.ap(wu_s, kt * 256 + ft * 128, kt * 256 + (ft + 1) * 128),
                                  abf.ap("XT", kt * T + q0, kt * T + q0 + 512),
                                  kt == 0, kt == NCT - 1),
                         reads=[abf.buf(wu_s), abf.buf("XT")], writes=[C.psbuf["B"]])
                k.op("ve", TT(abf.ap("H2", ft * 2048 + q0, ft * 2048 + q0 + 512), af.ap("SILU"),
                              C.ps["B"][:, 0:512], ALU.mult),
                     reads=[silub, C.psbuf["B"]], writes=[abf.buf("H2")])

    for e_i in range(16):
        small_expert(e_i, e_i % 2, e_i - 1 if e_i > 0 else None)
    for qc in range(4):
        down4(15, 1, qc)
    abf.free("H2"); abf.free("XT")
    for sl in range(2):
        abf.free(f"EWG{sl}"); abf.free(f"EWU{sl}"); abf.free(f"EWD{sl}")
    af.free("SILU");  af.free("LGT"); af.free("CW"); af.free("JUNK")
    af.free("CBR"); af.free("CBROW"); af.free("ONES4")

    af.alloc("SRT", 1024)
    for mt in range(NT):
        slot = rolls[mt % 2]
        C.load("sp", af.ap("SRT"), s_tiled[mt], af.buf("SRT"), rbufs=[ssp_b])
        C.load("sp", af.ap(slot), x_tiled[mt], af.buf(slot), rbufs=[xin_b])
        acc_ap = af.ap("ACC", mt * 1024, (mt + 1) * 1024)
        k.op("se", ACT(acc_ap, acc_ap, AF.Silu), reads=[accb], writes=[accb])
        k.op("ve", TT(acc_ap, acc_ap, af.ap("SRT"), ALU.mult),
             reads=[accb, af.buf("SRT")], writes=[accb])
        k.op("ve", TT(acc_ap, acc_ap, af.ap(slot), ALU.add),
             reads=[accb, af.buf(slot)], writes=[accb])
        C.store("sp", xo_tiled[mt], acc_ap, [accb], wbufs=[xout_b])
    af.free("SRT"); af.free("ACC")
    for nm in rolls:
        af.free(nm)
    for nm in roll_bf:
        abf.free(nm)
    af.free("STATS")
    abf.free("IDENT")
    af.free("ONES1")


def host_prep_l2(inp):
    ct = np.ascontiguousarray
    d = {}
    pw = inp["L2_attn_proj_weight"]
    d["projw"] = ct(pw.reshape(1024, 8, 128).transpose(1, 0, 2))
    aw = inp["L2_attn_attn_weight"]
    vw = inp["L2_attn_value_weight"]
    bd_a = np.zeros((8, 128, 128), dtype=np.float32)
    bd_v = np.zeros((8, 128, 128), dtype=np.float32)
    for p in range(8):
        bd_a[p, 0:64, 0:64] = aw[2 * p]
        bd_a[p, 64:128, 64:128] = aw[2 * p + 1]
        bd_v[p, 0:64, 0:64] = vw[2 * p]
        bd_v[p, 64:128, 64:128] = vw[2 * p + 1]
    d["attnw"] = bd_a
    d["valuew"] = bd_v
    d["outwT"] = ct(inp["L2_attn_out_w"].T)
    d["outb"] = ct(inp["L2_attn_out_b"].reshape(1, D))
    d["rwT"] = ct(inp["L2_ffn_router_w"].T)
    d["cbias"] = ct(inp["L2_ffn_correction_bias"].reshape(1, 16))
    d["gatew"] = ct(inp["L2_ffn_gate_w"])
    d["upw"] = ct(inp["L2_ffn_up_w"])
    d["downw"] = ct(inp["L2_ffn_down_w"])
    d["sgT"] = ct(inp["L2_ffn_shared_gate_w"].T)
    d["suT"] = ct(inp["L2_ffn_shared_up_w"].T)
    d["sdT"] = ct(inp["L2_ffn_shared_down_w"].T)
    d["ident"] = np.eye(128, dtype=np.float32)
    return d


L2_DRAM_SPECS = [
    ("projw", [8, D, 128]), ("attnw", [8, 128, 128]), ("valuew", [8, 128, 128]),
    ("outwT", [D, D]), ("outb", [1, D]),
    ("rwT", [D, 16]), ("cbias", [1, 16]),
    ("gatew", [16, D, 256]), ("upw", [16, D, 256]), ("downw", [16, 256, D]),
    ("sgT", [D, 2048]), ("suT", [D, 2048]), ("sdT", [2048, D]),
    ("ident", [128, 128]),
]




def emit_layer0(C, names):
    nc, k = C.nc, C.k
    xin_b = names["x_in_buf"]
    xout_b = names["x_out_buf"]
    ssp_b = names["s_spill_buf"]
    abf, af = C.abf, C.af
    DIN = 2048
    NDT = DIN // 128

    x_in, x_out, s_spill = names["x_in"], names["x_out"], names["s_spill"]
    x_tiled = x_in.rearrange("(a p) c -> a p c", p=128)
    s_tiled = s_spill.rearrange("(a p) c -> a p c", p=128)
    xo_tiled = x_out.rearrange("(a p) c -> a p c", p=128)
    xn_sp = names["xn_spill"].rearrange("(a p) c -> a p c", p=128)
    sz_sp = names["sz_scr"]
    xt_sp = names["xt_scr"]
    wq1_sp = names["wq1_scr"]
    wq2_sp = names["wq2_scr"]
    dtm_sp = names["dtm_scr"]

    abf.alloc("IDENT", 128)
    C.load("gp", abf.ap("IDENT"), names["ident"], abf.buf("IDENT"))
    af.alloc("STATS", 128)
    af.alloc("ONES1", 128)
    k.op("ve", MEMSET(af.t[0:1, af.off("ONES1"):af.off("ONES1") + 128], 1.0), writes=[af.buf("ONES1")])
    af.alloc("JUNK", 1024)
    rolls = ["XR0", "XR1"]
    for nm in rolls:
        af.alloc(nm, 1024)
    roll_bf = ["XNR0", "XNR1"]
    for nm in roll_bf:
        abf.alloc(nm, 1024)

    def ln_of(src_ap_fn, src_buf, tt, dst_nm, dst_buf, grep=None, brep=None):
        b = tt * 8
        st = "STATS"
        sb = af.buf(st)
        src = src_ap_fn()
        k.op("ve", RED(af.ap(st, b, b + 1), src, ALU.add, negate=True), reads=[src_buf], writes=[sb])
        k.op("ve", TS(af.ap(st, b + 1, b + 2), af.ap(st, b, b + 1), 1.0 / D, None, ALU.mult),
             reads=[sb], writes=[sb])
        k.op("se", ACT(af.ap("JUNK"), src, AF.Square, bias=af.ap(st, b + 1, b + 2),
                       scale=1.0, accum=af.ap(st, b + 2, b + 3)),
             reads=[src_buf, sb], writes=[af.buf("JUNK"), sb])
        k.op("ve", TS(af.ap(st, b + 2, b + 3), af.ap(st, b + 2, b + 3), 1.0 / D, 1e-5, ALU.mult, ALU.add),
             reads=[sb], writes=[sb])
        k.op("se", ACT(af.ap(st, b + 3, b + 4), af.ap(st, b + 2, b + 3), AF.Ln), reads=[sb], writes=[sb])
        k.op("se", ACT(af.ap(st, b + 4, b + 5), af.ap(st, b + 3, b + 4), AF.Exp, scale=-0.5),
             reads=[sb], writes=[sb])
        k.op("ve", TT(af.ap(st, b + 5, b + 6), af.ap(st, b + 1, b + 2), af.ap(st, b + 4, b + 5), ALU.mult),
             reads=[sb], writes=[sb])
        if grep is None:
            k.op("se", ACT(abf.ap(dst_nm), src, AF.Identity,
                           bias=af.ap(st, b + 5, b + 6), scale=af.ap(st, b + 4, b + 5)),
                 reads=[src_buf, sb], writes=[dst_buf])
        else:
            k.op("se", ACT(af.ap("LNN"), src, AF.Identity,
                           bias=af.ap(st, b + 5, b + 6), scale=af.ap(st, b + 4, b + 5)),
                 reads=[src_buf, sb], writes=[af.buf("LNN")])
            k.op("ve", TT(af.ap("LNN"), af.ap("LNN"), af.ap(grep), ALU.mult),
                 reads=[af.buf("LNN"), af.buf(grep)], writes=[af.buf("LNN")])
            k.op("ve", TT(abf.ap(dst_nm), af.ap("LNN"), af.ap(brep), ALU.add),
                 reads=[af.buf("LNN"), af.buf(brep)], writes=[dst_buf])

    def transpose_tile(nm, dstT, tt):
        for ct in range(NCT):
            k.op("te", TRANS(C.ps["T"][:, ct * 128:(ct + 1) * 128],
                             abf.ap(nm, ct * 128, (ct + 1) * 128), abf.ap("IDENT")),
                 reads=[abf.buf(nm), abf.buf("IDENT")], writes=[C.psbuf["T"]])
        src = C.ps["T"][:, 0:1024].rearrange("p (a c) -> p a c", a=NCT)
        dst = abf.vap(dstT, 0, NCT * T, "p (a c) -> p a c", c=T)[:, :, tt * 128:(tt + 1) * 128]
        k.op("ve", CP(dst, src), reads=[C.psbuf["T"]], writes=[abf.buf(dstT)])

    abf.alloc("XT", NCT * T)
    af.alloc("XNF", 1024)

    def p1_cb(tt, slot):
        emit_norm_transpose(C, tt, slot, roll_bf, "XT",
                            C.af.ap("STATS", tt * 8 + 4, tt * 8 + 5),
                            C.af.ap("STATS", tt * 8 + 5, tt * 8 + 6), [C.af.buf("STATS")])
        k.op("se", ACT(af.ap("XNF"), af.ap(slot), AF.Identity,
                       bias=af.ap("STATS", tt * 8 + 5, tt * 8 + 6),
                       scale=af.ap("STATS", tt * 8 + 4, tt * 8 + 5)),
             reads=[af.buf(slot), af.buf("STATS")], writes=[af.buf("XNF")])
        C.store("sp", xn_sp[tt], af.ap("XNF"), [af.buf("XNF")], wbufs=[names["xn_spill_buf"]])

    emit_ln_stats(C, x_tiled, rolls, "STATS", p1_cb, src_buf=names["x_in_buf"])
    af.free("XNF")
    C.store("sp", xt_sp, abf.ap("XT"), [abf.buf("XT")], wbufs=[names["xt_scr_buf"]])
    abf.free("XT")

    abf.alloc("WCV0", 1024)
    abf.alloc("WCV1", 1024)
    wff2_sp = names["wff2_scr"]
    for wt2 in range(32):
        ws = f"WCV{wt2 % 2}"
        C.load("gp", abf.ap(ws), names["w_ff2"][wt2 * 128:(wt2 + 1) * 128, :], abf.buf(ws))
        C.store("sp", wff2_sp[wt2 * 128:(wt2 + 1) * 128, :], abf.ap(ws), [abf.buf(ws)],
                wbufs=[names["wff2_scr_buf"]])
    abf.free("WCV0"); abf.free("WCV1")

    af.alloc("G1R", 1024)
    af.alloc("B1R", 1024)
    af.alloc("LNN", 1024)
    emit_bias_rep(C, names["ln1_g"], "G1R")
    emit_bias_rep(C, names["ln1_b"], "B1R")
    abf.alloc("H1T", NCT * T)
    for tt in range(NT):
        slot = rolls[tt % 2]
        C.load("sp", af.ap(slot), xn_sp[tt], af.buf(slot), rbufs=[names["xn_spill_buf"]])
        nm = roll_bf[tt % 2]
        ln_of(lambda slot=slot: af.ap(slot), af.buf(slot), tt, nm, abf.buf(nm), "G1R", "B1R")
        transpose_tile(nm, "H1T", tt)
    af.free("G1R"); af.free("B1R"); af.free("LNN")

    abf.alloc("XCT", NDT * T)
    abf.alloc("WIN0", NCT * 128)
    abf.alloc("WIN1", NCT * 128)
    abf.alloc("SZR", T)
    af.alloc("CVT0", 2052)
    af.alloc("CVT1", 2052)
    af.alloc("SCN0", 2048)
    k.op("ve", MEMSET(af.ap("CVT0", 0, 4), 0.0), writes=[af.buf("CVT0")])
    k.op("ve", MEMSET(af.ap("CVT1", 0, 4), 0.0), writes=[af.buf("CVT1")])
    af.alloc("CWC", 64)
    C.load("sp", af.ap("CWC"), names["convw_cols"], af.buf("CWC"))
    af.alloc("CBC", 16)
    C.load("sp", af.ap("CBC"), names["convb_cols"], af.buf("CBC"))
    win = names["w_in"]
    for ft in range(32):
        wslot = f"WIN{ft % 2}"
        C.load("gp", abf.vap(wslot, 0, NCT * 128, "p (a c) -> p a c", c=128),
               win.rearrange("(a p) f -> p a f", p=128)[:, :, ft * 128:(ft + 1) * 128], abf.buf(wslot))
        cv = f"CVT{ft % 2}"
        for qc in range(4):
            pb = "A" if qc % 2 == 0 else "B"
            for kt in range(NCT):
                k.op("te", MM(C.ps[pb][:, 0:512],
                              abf.ap(wslot, kt * 128, (kt + 1) * 128),
                              abf.ap("H1T", kt * T + qc * 512, kt * T + (qc + 1) * 512),
                              kt == 0, kt == NCT - 1),
                     reads=[abf.buf(wslot), abf.buf("H1T")], writes=[C.psbuf[pb]])
            if ft < 16:
                k.op("ve", CP(af.ap(cv, 4 + qc * 512, 4 + (qc + 1) * 512), C.ps[pb][:, 0:512]),
                     reads=[C.psbuf[pb]], writes=[af.buf(cv)])
            else:
                k.op("se", ACT(abf.ap("SZR", qc * 512, (qc + 1) * 512), C.ps[pb][:, 0:512], AF.Silu),
                     reads=[C.psbuf[pb]], writes=[abf.buf("SZR")])
        if ft < 16:
            cwcol = lambda j, ft=ft: af.ap("CWC", ft * 4 + j, ft * 4 + j + 1)
            k.op("ve", TS(af.ap("SCN0"), af.ap(cv, 1, 2049), cwcol(0), None, ALU.mult),
                 reads=[af.buf(cv), af.buf("CWC")], writes=[af.buf("SCN0")])
            for j in range(1, 4):
                k.op("ve", STT(af.ap("SCN0"), af.ap(cv, 1 + j, 1 + j + 2048), cwcol(j),
                               af.ap("SCN0"), ALU.mult, ALU.add),
                     reads=[af.buf(cv), af.buf("CWC"), af.buf("SCN0")], writes=[af.buf("SCN0")])
            k.op("se", ACT(abf.ap("XCT", ft * T, (ft + 1) * T), af.ap("SCN0"), AF.Silu,
                           bias=af.ap("CBC", ft, ft + 1), scale=1.0),
                 reads=[af.buf("SCN0"), af.buf("CBC")], writes=[abf.buf("XCT")])
        else:
            C.store("sp", sz_sp[(ft - 16) * 128:(ft - 15) * 128, :], abf.ap("SZR"), [abf.buf("SZR")], wbufs=[names["sz_scr_buf"]])
    abf.free("WIN0"); abf.free("WIN1"); abf.free("SZR"); abf.free("H1T")
    af.free("CVT0"); af.free("CVT1"); af.free("SCN0"); af.free("CWC"); af.free("CBC")

    abf.alloc("DTLO", T)
    af.alloc("BROWS", T)
    af.alloc("CROWS", T)
    abf.alloc("WXA", NDT * 96)
    C.load("gp", abf.vap("WXA", 0, NDT * 96, "p (a c) -> p a c", c=96),
           names["w_x"].rearrange("(a p) c -> p a c", p=128), abf.buf("WXA"))
    for (c0, c1, M, dst, isbf) in ((0, 64, 64, "DTLO", True), (64, 80, 16, "BROWS", False),
                                   (80, 96, 16, "CROWS", False)):
        for qc in range(4):
            pb = "A" if qc % 2 == 0 else "B"
            for kt in range(NDT):
                k.op("te", MM(C.ps[pb][0:M, 0:512],
                              abf.ap("WXA", kt * 96 + c0, kt * 96 + c1),
                              abf.ap("XCT", kt * T + qc * 512, kt * T + (qc + 1) * 512),
                              kt == 0, kt == NDT - 1),
                     reads=[abf.buf("WXA"), abf.buf("XCT")], writes=[C.psbuf[pb]])
            if isbf:
                k.op("ve", CP(abf.rap(dst, 0, M, qc * 512, (qc + 1) * 512), C.ps[pb][0:M, 0:512]),
                     reads=[C.psbuf[pb]], writes=[abf.buf(dst)])
            else:
                k.op("ve", CP(af.rap(dst, 0, M, qc * 512, (qc + 1) * 512), C.ps[pb][0:M, 0:512]),
                     reads=[C.psbuf[pb]], writes=[af.buf(dst)])
    abf.free("WXA")
    abf.alloc("WDT", 128)
    af.alloc("NBDT", 16)
    af.alloc("SPR", 512)
    C.load("sp", af.ap("NBDT"), names["bdt_cols"], af.buf("NBDT"))
    abf.alloc("DTM0", T)
    abf.alloc("DTM1", T)
    for ct in range(NDT):
        dts = f"DTM{ct % 2}"
        C.load("gp", abf.rap("WDT", 0, 64, 0, 128), names["w_dt"][0:64, ct * 128:(ct + 1) * 128],
               abf.buf("WDT"))
        for qc in range(4):
            pb = "A" if qc % 2 == 0 else "B"
            k.op("te", MM(C.ps[pb][:, 0:512], abf.rap("WDT", 0, 64, 0, 128),
                          abf.rap("DTLO", 0, 64, qc * 512, (qc + 1) * 512), True, True),
                 reads=[abf.buf("WDT"), abf.buf("DTLO")], writes=[C.psbuf[pb]])
            k.op("se", ACT(af.ap("SPR"), C.ps[pb][:, 0:512], AF.Exp,
                           bias=af.ap("NBDT", ct, ct + 1), scale=1.0),
                 reads=[C.psbuf[pb], af.buf("NBDT")], writes=[af.buf("SPR")])
            k.op("se", ACT(abf.ap(dts, qc * 512, (qc + 1) * 512),
                           af.ap("SPR"), AF.Ln, bias=1.0),
                 reads=[af.buf("SPR")], writes=[abf.buf(dts)])
        C.store("sp", dtm_sp[ct * 128:(ct + 1) * 128, :], abf.ap(dts), [abf.buf(dts)], wbufs=[names["dtm_scr_buf"]])
    abf.free("WDT"); abf.free("DTLO"); af.free("SPR"); af.free("NBDT")
    abf.free("DTM0"); abf.free("DTM1")

    af.alloc("APERM", 256)
    C.load("sp", af.ap("APERM"), names["alog_perm"], af.buf("APERM"))
    k.op("se", ACT(af.ap("APERM"), af.ap("APERM"), AF.Exp), reads=[af.buf("APERM")], writes=[af.buf("APERM")])
    k.op("ve", TS(af.ap("APERM"), af.ap("APERM"), -1.0, None, ALU.mult),
         reads=[af.buf("APERM")], writes=[af.buf("APERM")])
    af.alloc("RB", 128)
    C.load("sp", af.rap("RB", 0, 16, 0, 128), names["rbsel"], af.buf("RB"))
    af.alloc("BREP", T)
    af.alloc("CREP", T)
    for (src, dst) in (("BROWS", "BREP"), ("CROWS", "CREP")):
        for qc in range(4):
            k.op("te", MM(C.ps["A"][:, 0:512], af.rap("RB", 0, 16, 0, 128),
                          af.rap(src, 0, 16, qc * 512, (qc + 1) * 512), True, True),
                 reads=[af.buf("RB"), af.buf(src)], writes=[C.psbuf["A"]])
            k.op("ve", CP(af.ap(dst, qc * 512, (qc + 1) * 512), C.ps["A"][:, 0:512]),
                 reads=[C.psbuf["A"]], writes=[af.buf(dst)])
    af.free("BROWS"); af.free("CROWS"); af.free("RB")
    abf.alloc("RSEL", 16 * 128)
    C.load("gp", abf.vap("RSEL", 0, 16 * 128, "p (a c) -> p a c", c=128), names["rmats"], abf.buf("RSEL"))
    abf.alloc("SSEL", 16 * 128)
    C.load("gp", abf.vap("SSEL", 0, 16 * 128, "p (a c) -> p a c", c=128), names["smats"], abf.buf("SSEL"))
    af.alloc("DSKC", 16)
    C.load("sp", af.ap("DSKC"), names["dskip_cols"], af.buf("DSKC"))

    af.alloc("GAM", 8)
    for i4 in range(2):
        af.alloc(f"WQC{i4}", 1024)
    af.alloc("WQC_A0", 1024)
    af.alloc("WQC_A1", 1024)
    af.alloc("ONECOL", 8)
    qthunks = []

    def _q_init():
        k.op("ve", MEMSET(af.ap("GAM", 0, 2), 0.0), writes=[af.buf("GAM")])
        k.op("ve", MEMSET(af.rap("ONECOL", 0, 128, 0, 1), 1.0), writes=[af.buf("ONECOL")])
    qthunks.append(_q_init)

    def quantize_p1(wt_dram, nrow_tiles, gcol):
        ncol = wt_dram.shape[1]
        nchunk = max(1, ncol // 1024)
        w = ncol // nchunk
        for rt in range(nrow_tiles):
            for cc in range(nchunk):
                def th(rt=rt, cc=cc, w=w, wt_dram=wt_dram, gcol=gcol, nchunk=nchunk):
                    i = rt * nchunk + cc
                    slot = f"WQC{i % 2}"
                    C.load("sp" if i % 2 == 0 else "se", af.ap(slot, 0, w),
                           wt_dram[rt * 128:(rt + 1) * 128, cc * w:(cc + 1) * w],
                           af.buf(slot))
                    k.op("ve", _frag(lambda e, o=af.ap("GAM", 4, 5), i2=af.ap(slot, 0, w): e.tensor_reduce(
                        o, i2, AX.X, ALU.add, apply_absolute_value=True)),
                        reads=[af.buf(slot)], writes=[af.buf("GAM")])
                    k.op("ve", TT(af.ap("GAM", gcol, gcol + 1), af.ap("GAM", gcol, gcol + 1),
                                  af.ap("GAM", 4, 5), ALU.add),
                         reads=[af.buf("GAM")], writes=[af.buf("GAM")])
                qthunks.append(th)

    def quantize_p2(wt_dram, wq_dram, nrow_tiles, rgcol, wq_buf):
        ncol = wt_dram.shape[1]
        nchunk = max(1, ncol // 1024)
        w = ncol // nchunk
        for rt in range(nrow_tiles):
            for cc in range(nchunk):
                def th(rt=rt, cc=cc, w=w, wt_dram=wt_dram, wq_dram=wq_dram, rgcol=rgcol,
                       wq_buf=wq_buf, nchunk=nchunk):
                    i = rt * nchunk + cc
                    slot = f"WQC{i % 2}"
                    aslot = f"WQC_A{i % 2}"
                    C.load("sp" if i % 2 == 0 else "se", af.ap(slot, 0, w),
                           wt_dram[rt * 128:(rt + 1) * 128, cc * w:(cc + 1) * w],
                           af.buf(slot))
                    k.op("se", ACT(af.ap(slot, 0, w), af.ap(slot, 0, w), AF.Identity,
                                   scale=af.ap("GAM", rgcol, rgcol + 1)),
                         reads=[af.buf(slot), af.buf("GAM")], writes=[af.buf(slot)])
                    k.op("ve", TS(af.ap(aslot, 0, w), af.ap(slot, 0, w), -0.5, None, ALU.is_lt),
                         reads=[af.buf(slot)], writes=[af.buf(aslot)])
                    k.op("ve", STT(af.ap(aslot, 0, w), af.ap(slot, 0, w), 0.5,
                                   af.ap(aslot, 0, w), ALU.is_gt, ALU.subtract),
                         reads=[af.buf(slot), af.buf(aslot)], writes=[af.buf(aslot)])
                    C.store("gp", wq_dram[rt * 128:(rt + 1) * 128, cc * w:(cc + 1) * w],
                            af.ap(aslot, 0, w), [af.buf(aslot)], wbufs=[wq_buf])
                qthunks.append(th)

    quantize_p1(names["w1T"], 8, 0)
    quantize_p1(names["w2T"], 32, 1)

    def _q_stats():
        for gcol, n_elems, rgcol in ((0, 4096 * 1024, 2), (1, 4096 * 1024, 3)):
            k.op("te", MM(C.ps["E"][0:1, 0:1], af.rap("GAM", 0, 128, gcol, gcol + 1),
                          af.rap("ONECOL", 0, 128, 0, 1), True, True),
                 reads=[af.buf("GAM"), af.buf("ONECOL")], writes=[C.psbuf["E"]])
            k.op("ve", TS(af.rap("GAM", 0, 1, 4, 5), C.ps["E"][0:1, 0:1], 1.0 / n_elems, 1e-5, ALU.mult, ALU.add),
                 reads=[C.psbuf["E"]], writes=[af.buf("GAM")])
            k.op("ve", RECIP(af.rap("GAM", 0, 1, 5, 6), af.rap("GAM", 0, 1, 4, 5)),
                 reads=[af.buf("GAM")], writes=[af.buf("GAM")])
            k.op("te", MM(C.ps["E"][:, 0:1], af.t[0:1, af.off("ONES1"):af.off("ONES1") + 128],
                          af.rap("GAM", 0, 1, 5, 6), True, True),
                 reads=[af.buf("ONES1"), af.buf("GAM")], writes=[C.psbuf["E"]])
            k.op("ve", CP(af.ap("GAM", rgcol, rgcol + 1), C.ps["E"][:, 0:1]),
                 reads=[C.psbuf["E"]], writes=[af.buf("GAM")])
    qthunks.append(_q_stats)
    quantize_p2(names["w1T"], wq1_sp, 8, 2, names["wq1_scr_buf"])
    quantize_p2(names["w2T"], wq2_sp, 32, 3, names["wq2_scr_buf"])

    abf.alloc("DXT", T)
    abf.alloc("SZT0", T)
    abf.alloc("SZT1", T)
    abf.alloc("DTT0", T)
    abf.alloc("DTT1", T)
    for i4 in range(4):
        abf.alloc(f"HCB{i4}", 512)
        af.alloc(f"DA{i4}", 512)
        af.alloc(f"BB{i4}", 512)
        af.alloc(f"HH{i4}", 512)
        af.alloc(f"BST{i4}", 512)
    af.alloc("V1", 512)
    ydst = [("C", 0), ("C", 512), ("D", 0), ("D", 512)]
    for ct in range(NDT):
        szs = f"SZT{ct % 2}"
        dtt = f"DTT{ct % 2}"
        C.load("sp", abf.ap(szs), sz_sp[ct * 128:(ct + 1) * 128, :], abf.buf(szs), rbufs=[names["sz_scr_buf"]])
        C.load("sp", abf.ap(dtt), dtm_sp[ct * 128:(ct + 1) * 128, :], abf.buf(dtt), rbufs=[names["dtm_scr_buf"]])
        k.op("ve", TT(abf.ap("DXT"), abf.ap(dtt),
                      abf.ap("XCT", ct * T, (ct + 1) * T), ALU.mult),
             reads=[abf.buf(dtt), abf.buf("XCT")], writes=[abf.buf("DXT")])
        def front(it):
            j2, qc = it // 4, it % 4
            j = ct * 16 + j2
            p2 = it % 4
            pa = "A" if it % 2 == 0 else "E"
            da, bb2 = f"DA{p2}", f"BB{p2}"
            k.op("te", MM(C.ps[pa][:, 0:512],
                          abf.ap("RSEL", j2 * 128, (j2 + 1) * 128),
                          abf.ap(dtt, qc * 512, (qc + 1) * 512), True, True),
                 reads=[abf.buf("RSEL"), abf.buf(dtt)], writes=[C.psbuf[pa]])
            k.op("te", MM(C.ps["B"][:, 0:512],
                          abf.ap("RSEL", j2 * 128, (j2 + 1) * 128),
                          abf.ap("DXT", qc * 512, (qc + 1) * 512), True, True),
                 reads=[abf.buf("RSEL"), abf.buf("DXT")], writes=[C.psbuf["B"]])
            k.op("se", ACT(af.ap(da), C.ps[pa][:, 0:512], AF.Exp,
                           bias=0.0, scale=af.ap("APERM", j, j + 1)),
                 reads=[C.psbuf[pa], af.buf("APERM")], writes=[af.buf(da)])
            bst = f"BST{p2}"
            k.op("se", ACT(af.ap(bst), C.ps["B"][:, 0:512], AF.Identity),
                 reads=[C.psbuf["B"]], writes=[af.buf(bst)])
            k.op("gp", TT(af.ap(bb2), af.ap(bst),
                          af.ap("BREP", qc * 512, (qc + 1) * 512), ALU.mult),
                 reads=[af.buf(bst), af.buf("BREP")], writes=[af.buf(bb2)])

        front(0)
        front(1)
        for it in range(64):
            j2, qc = it // 4, it % 4
            p2 = it % 4
            da, bb2, hh, hcb = f"DA{p2}", f"BB{p2}", f"HH{qc}", f"HCB{p2}"
            hhprev = f"HH{qc - 1}"
            init = 0.0 if qc == 0 else af.ap(hhprev, 511, 512)
            k.op("ve", SCAN(af.ap(hh), af.ap(da), af.ap(bb2), init),
                 reads=[af.buf(da), af.buf(bb2)] + ([af.buf(hhprev)] if qc else []),
                 writes=[af.buf(hh)])
            if it + 2 < 64:
                front(it + 2)
            k.op("ve", TT(abf.ap(hcb), af.ap(hh),
                          af.ap("CREP", qc * 512, (qc + 1) * 512), ALU.mult),
                 reads=[af.buf(hh), af.buf("CREP")], writes=[abf.buf(hcb)])
            yb, yoff = ydst[qc]
            k.op("te", MM(C.ps[yb][:, yoff:yoff + 512],
                          abf.ap("SSEL", j2 * 128, (j2 + 1) * 128),
                          abf.ap(hcb, 0, 512),
                          j2 == 0, j2 == 15, skip=True),
                 reads=[abf.buf("SSEL"), abf.buf(hcb)], writes=[C.psbuf[yb]])
        for qc in range(4):
            yb, yoff = ydst[qc]
            k.op("ve", TS(af.ap("V1"), abf.ap("XCT", ct * T + qc * 512, ct * T + (qc + 1) * 512),
                          af.ap("DSKC", ct, ct + 1), None, ALU.mult),
                 reads=[abf.buf("XCT"), af.buf("DSKC")], writes=[af.buf("V1")])
            k.op("ve", TT(af.ap("V1"), af.ap("V1"), C.ps[yb][:, yoff:yoff + 512], ALU.add),
                 reads=[af.buf("V1"), C.psbuf[yb]], writes=[af.buf("V1")])
            k.op("ve", TT(abf.ap("XCT", ct * T + qc * 512, ct * T + (qc + 1) * 512),
                          af.ap("V1"), abf.ap(szs, qc * 512, (qc + 1) * 512), ALU.mult),
                 reads=[af.buf("V1"), abf.buf(szs)], writes=[abf.buf("XCT")])
        nth = (len(qthunks) * (ct + 1)) // NDT - (len(qthunks) * ct) // NDT
        for _ in range(nth):
            qthunks.pop(0)()
    for th in qthunks:
        th()
    for i4 in range(2):
        af.free(f"WQC{i4}")
    af.free("WQC_A0"); af.free("WQC_A1"); af.free("ONECOL"); af.free("GAM")
    abf.free("DXT"); abf.free("SZT0"); abf.free("SZT1")
    abf.free("DTT0"); abf.free("DTT1")
    abf.free("RSEL"); abf.free("SSEL")
    for i4 in range(4):
        abf.free(f"HCB{i4}")
        af.free(f"DA{i4}"); af.free(f"BB{i4}"); af.free(f"HH{i4}"); af.free(f"BST{i4}")
    af.free("V1")
    af.free("APERM"); af.free("BREP"); af.free("CREP"); af.free("DSKC")

    af.alloc("T1", NT * 1024)
    t1b = af.buf("T1")
    abf.alloc("WOF", NDT * 1024)
    for ct in range(NDT):
        C.load("gp", abf.ap("WOF", ct * 1024, (ct + 1) * 1024),
               names["w_out"][ct * 128:(ct + 1) * 128, :], abf.buf("WOF"))
    for mt in range(NT):
        pb = "C" if mt % 2 == 0 else "D"
        for ct in range(NDT):
            for half in range(2):
                k.op("te", MM(C.ps[pb][:, half * 512:(half + 1) * 512],
                              abf.ap("XCT", ct * T + mt * 128, ct * T + (mt + 1) * 128),
                              abf.ap("WOF", ct * 1024 + half * 512, ct * 1024 + (half + 1) * 512),
                              ct == 0, ct == NDT - 1),
                     reads=[abf.buf("XCT"), abf.buf("WOF")], writes=[C.psbuf[pb]])
        slot = rolls[mt % 2]
        C.load("sp", af.ap(slot), xn_sp[mt], af.buf(slot), rbufs=[names["xn_spill_buf"]])
        k.op("ve", TT(af.ap("T1", mt * 1024, (mt + 1) * 1024), C.ps[pb][:, 0:1024], af.ap(slot), ALU.add),
             reads=[C.psbuf[pb], af.buf(slot)], writes=[t1b])
    abf.free("XCT"); abf.free("WOF")
    for nm in rolls:
        af.free(nm)

    af.alloc("G2R", 1024)
    af.alloc("B2R", 1024)
    emit_bias_rep(C, names["ln2_g"], "G2R")
    emit_bias_rep(C, names["ln2_b"], "B2R")
    af.alloc("LNN", 1024)
    abf.alloc("H2T", NCT * T)
    for tt in range(NT):
        nm = roll_bf[tt % 2]
        ln_of(lambda tt=tt: af.ap("T1", tt * 1024, (tt + 1) * 1024), t1b, tt, nm, abf.buf(nm),
              "G2R", "B2R")
        transpose_tile(nm, "H2T", tt)
    af.free("G2R"); af.free("B2R"); af.free("LNN")

    abf.alloc("AGC", 32 * 1024)
    abf.alloc("WF0", NCT * 128)
    abf.alloc("WF1", NCT * 128)
    abf.alloc("WF2", NCT * 128)
    abf.alloc("WF3", NCT * 128)
    abf.alloc("WFD0", 1024)
    abf.alloc("WFD1", 1024)
    af.alloc("BFF1", 64)
    C.load("sp", af.ap("BFF1"), names["bff1_cols"], af.buf("BFF1"))
    af.alloc("BIAS2", 1024)
    emit_bias_rep(C, names["b_ff2"], "BIAS2")
    af.alloc("GEL", 512)
    wff1 = names["w_ff1"]
    wff2 = names["w_ff2"]
    for tcp in range(2):
        for ft in range(32):
            wa, wg = f"WF{ft % 2}", f"WF{2 + ft % 2}"
            C.load("gp", abf.vap(wa, 0, NCT * 128, "p (a c) -> p a c", c=128),
                   wff1.rearrange("(a p) f -> p a f", p=128)[:, :, ft * 128:(ft + 1) * 128], abf.buf(wa))
            C.load("gp", abf.vap(wg, 0, NCT * 128, "p (a c) -> p a c", c=128),
                   wff1.rearrange("(a p) f -> p a f", p=128)[:, :, 4096 + ft * 128:4096 + (ft + 1) * 128],
                   abf.buf(wg))
            for sub in range(2):
                q0 = tcp * 1024 + sub * 512
                pa = "A" if sub == 0 else "E"
                for kt in range(NCT):
                    k.op("te", MM(C.ps[pa][:, 0:512],
                                  abf.ap(wa, kt * 128, (kt + 1) * 128),
                                  abf.ap("H2T", kt * T + q0, kt * T + q0 + 512),
                                  kt == 0, kt == NCT - 1),
                         reads=[abf.buf(wa), abf.buf("H2T")], writes=[C.psbuf[pa]])
                for kt in range(NCT):
                    k.op("te", MM(C.ps["B"][:, 0:512],
                                  abf.ap(wg, kt * 128, (kt + 1) * 128),
                                  abf.ap("H2T", kt * T + q0, kt * T + q0 + 512),
                                  kt == 0, kt == NCT - 1),
                         reads=[abf.buf(wg), abf.buf("H2T")], writes=[C.psbuf["B"]])
                k.op("se", ACT(af.ap("GEL"), C.ps["B"][:, 0:512], AF.Gelu_apprx_tanh,
                               bias=af.ap("BFF1", 32 + ft, 33 + ft), scale=1.0),
                     reads=[C.psbuf["B"], af.buf("BFF1")], writes=[af.buf("GEL")])
                k.op("ve", STT(abf.ap("AGC", ft * 1024 + sub * 512, ft * 1024 + (sub + 1) * 512),
                               C.ps[pa][:, 0:512],
                               af.ap("BFF1", ft, ft + 1), af.ap("GEL"), ALU.add, ALU.mult),
                     reads=[C.psbuf[pa], af.buf("BFF1"), af.buf("GEL")], writes=[abf.buf("AGC")])
        for mip in range(4):
            for ft in range(32):
                ws = f"WFD{ft % 2}"
                C.load("sp" if ft % 2 == 0 else "se", abf.ap(ws), wff2_sp[ft * 128:(ft + 1) * 128, :],
                       abf.buf(ws), rbufs=[names["wff2_scr_buf"]])
                for mi in (2 * mip, 2 * mip + 1):
                    pb = "C" if mi % 2 == 0 else "D"
                    for half in range(2):
                        k.op("te", MM(C.ps[pb][:, half * 512:(half + 1) * 512],
                                      abf.ap("AGC", ft * 1024 + mi * 128, ft * 1024 + (mi + 1) * 128),
                                      abf.ap(ws, half * 512, (half + 1) * 512),
                                      ft == 0, ft == 31),
                             reads=[abf.buf("AGC"), abf.buf(ws)], writes=[C.psbuf[pb]])
            for mi in (2 * mip, 2 * mip + 1):
                mt = tcp * 8 + mi
                pb = "C" if mi % 2 == 0 else "D"
                k.op("ve", TT(af.ap("JUNK"), C.ps[pb][:, 0:1024], af.ap("BIAS2"), ALU.add),
                     reads=[C.psbuf[pb], af.buf("BIAS2")], writes=[af.buf("JUNK")])
                k.op("gp", TT(af.ap("T1", mt * 1024, (mt + 1) * 1024),
                              af.ap("T1", mt * 1024, (mt + 1) * 1024), af.ap("JUNK"), ALU.add),
                     reads=[t1b, af.buf("JUNK")], writes=[t1b])
                C.store("sp", s_tiled[mt], af.ap("T1", mt * 1024, (mt + 1) * 1024), [t1b], wbufs=[ssp_b])
                if "s_dbg" in names:
                    C.store("sp", names["s_dbg"].rearrange("(a p) c -> a p c", p=128)[mt],
                            af.ap("T1", mt * 1024, (mt + 1) * 1024), [t1b])
    abf.free("H2T"); abf.free("AGC")
    for nm in ("WF0", "WF1", "WF2", "WF3", "WFD0", "WFD1"):
        abf.free(nm)
    af.free("BFF1"); af.free("BIAS2"); af.free("GEL"); af.free("T1")

    abf.alloc("XT2", NCT * T)
    C.load("sp", abf.ap("XT2"), xt_sp, abf.buf("XT2"), rbufs=[names["xt_scr_buf"]])
    af.alloc("ACC", NT * 1024)
    accb = af.buf("ACC")
    af.alloc("B1C", 32)
    C.load("sp", af.ap("B1C"), names["b1_cols"], af.buf("B1C"))
    af.alloc("BIAS2b", 1024)
    emit_bias_rep(C, names["b2row"], "BIAS2b")
    abf.alloc("HBC", 32 * 1024)
    abf.alloc("Q10", NCT * 128)
    abf.alloc("Q11", NCT * 128)
    abf.alloc("Q20", 1024)
    abf.alloc("Q21", 1024)
    for tcp in range(2):
        for ft in range(32):
            ws = f"Q1{ft % 2}"
            C.load("sp", abf.vap(ws, 0, NCT * 128, "p (a c) -> p a c", c=128),
                   wq1_sp.rearrange("(a p) f -> p a f", p=128)[:, :, ft * 128:(ft + 1) * 128], abf.buf(ws),
                   rbufs=[names["wq1_scr_buf"]])
            for sub in range(2):
                q0 = tcp * 1024 + sub * 512
                pa = "A" if sub == 0 else "E"
                for kt in range(NCT):
                    k.op("te", MM(C.ps[pa][:, 0:512],
                                  abf.ap(ws, kt * 128, (kt + 1) * 128),
                                  abf.ap("XT2", kt * T + q0, kt * T + q0 + 512),
                                  kt == 0, kt == NCT - 1),
                         reads=[abf.buf(ws), abf.buf("XT2")], writes=[C.psbuf[pa]])
                k.op("se", ACT(abf.ap("HBC", ft * 1024 + sub * 512, ft * 1024 + (sub + 1) * 512),
                               C.ps[pa][:, 0:512], AF.Relu,
                               bias=af.ap("B1C", ft, ft + 1), scale=1.0),
                     reads=[C.psbuf[pa], af.buf("B1C")], writes=[abf.buf("HBC")])
        for mip in range(4):
            for ft in range(32):
                ws = f"Q2{ft % 2}"
                C.load("sp" if ft % 2 == 0 else "se", abf.ap(ws), wq2_sp[ft * 128:(ft + 1) * 128, :],
                       abf.buf(ws), rbufs=[names["wq2_scr_buf"]])
                for mi in (2 * mip, 2 * mip + 1):
                    pb = "C" if mi % 2 == 0 else "D"
                    for half in range(2):
                        k.op("te", MM(C.ps[pb][:, half * 512:(half + 1) * 512],
                                      abf.ap("HBC", ft * 1024 + mi * 128, ft * 1024 + (mi + 1) * 128),
                                      abf.ap(ws, half * 512, (half + 1) * 512),
                                      ft == 0, ft == 31),
                             reads=[abf.buf("HBC"), abf.buf(ws)], writes=[C.psbuf[pb]])
            for mi in (2 * mip, 2 * mip + 1):
                mt = tcp * 8 + mi
                pb = "C" if mi % 2 == 0 else "D"
                k.op("ve", TT(af.ap("ACC", mt * 1024, (mt + 1) * 1024), C.ps[pb][:, 0:1024],
                              af.ap("BIAS2b"), ALU.add),
                     reads=[C.psbuf[pb], af.buf("BIAS2b")], writes=[accb])
    abf.free("XT2"); abf.free("HBC")
    for nm in ("Q10", "Q11", "Q20", "Q21"):
        abf.free(nm)
    af.free("B1C"); af.free("BIAS2b")

    for nm in rolls:
        af.alloc(nm, 1024)
    af.alloc("SRT", 1024)
    for mt in range(NT):
        slot = rolls[mt % 2]
        C.load("sp", af.ap("SRT"), s_tiled[mt], af.buf("SRT"), rbufs=[ssp_b])
        C.load("sp", af.ap(slot), x_tiled[mt], af.buf(slot), rbufs=[xin_b])
        acc_ap = af.ap("ACC", mt * 1024, (mt + 1) * 1024)
        k.op("se", ACT(acc_ap, acc_ap, AF.Silu), reads=[accb], writes=[accb])
        k.op("ve", TT(acc_ap, acc_ap, af.ap("SRT"), ALU.mult),
             reads=[accb, af.buf("SRT")], writes=[accb])
        k.op("ve", TT(acc_ap, acc_ap, af.ap(slot), ALU.add),
             reads=[accb, af.buf(slot)], writes=[accb])
        C.store("sp", xo_tiled[mt], acc_ap, [accb], wbufs=[xout_b])
    af.free("SRT"); af.free("ACC")
    for nm in rolls:
        af.free(nm)
    for nm in roll_bf:
        abf.free(nm)
    af.free("STATS"); af.free("JUNK")
    abf.free("IDENT")
    af.free("ONES1")


def host_prep_l0(inp):
    ct = np.ascontiguousarray
    d = {}
    d["ln1_g"] = ct(inp["L0_attn_ln1_g"].reshape(1, D))
    d["ln1_b"] = ct(inp["L0_attn_ln1_b"].reshape(1, D))
    d["w_in"] = ct(inp["L0_attn_w_in"])
    d["convw_cols"] = ct(inp["L0_attn_conv_w"].reshape(16, 128, 4).transpose(1, 0, 2).reshape(128, 64))
    d["convb_cols"] = ct(inp["L0_attn_conv_b"].reshape(16, 128).T)
    d["w_x"] = ct(inp["L0_attn_w_x"])
    d["w_dt"] = ct(inp["L0_attn_w_dt"])
    d["bdt_cols"] = ct(inp["L0_attn_b_dt"].reshape(16, 128).T)
    al = inp["L0_attn_a_log"]
    ap_ = np.zeros((128, 256), np.float32)
    for di in range(8):
        for n in range(16):
            ap_[di * 16 + n, :] = al[8 * np.arange(256) + di, n]
    d["alog_perm"] = ap_
    d["dskip_cols"] = ct(inp["L0_attn_d_skip"].reshape(16, 128).T)
    d["w_out"] = ct(inp["L0_attn_w_out"])
    d["ln2_g"] = ct(inp["L0_attn_ln2_g"].reshape(1, D))
    d["ln2_b"] = ct(inp["L0_attn_ln2_b"].reshape(1, D))
    d["w_ff1"] = ct(inp["L0_attn_w_ff1"])
    bff1 = inp["L0_attn_b_ff1"]
    d["bff1_cols"] = ct(np.concatenate([bff1[:4096].reshape(32, 128).T,
                                        bff1[4096:].reshape(32, 128).T], axis=1))
    d["w_ff2"] = ct(inp["L0_attn_w_ff2"])
    d["b_ff2"] = ct(inp["L0_attn_b_ff2"].reshape(1, D))
    d["w1T"] = ct(inp["L0_ffn_W1"].T)
    d["b1_cols"] = ct(inp["L0_ffn_b1"].reshape(32, 128).T)
    d["w2T"] = ct(inp["L0_ffn_W2"].T)
    d["b2row"] = ct(inp["L0_ffn_b2"].reshape(1, D))
    rm = np.zeros((16, 128, 128), np.float32)
    sm = np.zeros((16, 128, 128), np.float32)
    for jj in range(16):
        for di in range(8):
            for n in range(16):
                rm[jj, jj * 8 + di, di * 16 + n] = 1.0
                sm[jj, di * 16 + n, jj * 8 + di] = 1.0
    d["rmats"] = rm
    d["smats"] = sm
    rb = np.zeros((16, 128), np.float32)
    for di in range(8):
        for n in range(16):
            rb[n, di * 16 + n] = 1.0
    d["rbsel"] = rb
    d["ident"] = np.eye(128, dtype=np.float32)
    return d


L0_DRAM_SPECS = [
    ("ln1_g", [1, D]), ("ln1_b", [1, D]), ("w_in", [D, 4096]),
    ("convw_cols", [128, 64]), ("convb_cols", [128, 16]),
    ("w_x", [2048, 96]), ("w_dt", [64, 2048]), ("bdt_cols", [128, 16]),
    ("alog_perm", [128, 256]), ("dskip_cols", [128, 16]), ("w_out", [2048, D]),
    ("ln2_g", [1, D]), ("ln2_b", [1, D]), ("w_ff1", [D, 8192]),
    ("bff1_cols", [128, 64]), ("w_ff2", [4096, D]), ("b_ff2", [1, D]),
    ("w1T", [D, 4096]), ("b1_cols", [128, 32]), ("w2T", [4096, D]), ("b2row", [1, D]),
    ("rmats", [16, 128, 128]), ("smats", [16, 128, 128]), ("rbsel", [16, 128]),
    ("ident", [128, 128]),
]

LAYER_SCRATCH = {0: [("xn_spill", [T, D], F32), ("sz_scr", [2048, T], BF16),
                    ("xt_scr", [D, T], BF16), ("wq1_scr", [D, 4096], BF16),
                    ("wq2_scr", [4096, D], BF16), ("dtm_scr", [2048, T], BF16),
                    ("wff2_scr", [4096, D], BF16)]}


def build_program(layers, dram_specs_per_layer, emitters):
    nc = bass.Bass("TRN2", target_bir_lowering=False)
    names_all = []
    x_cur = nc.dram_tensor("x_in", [T, D], F32, kind="ExternalInput").ap()
    x_final = nc.dram_tensor("x_out", [T, D], F32, kind="ExternalOutput").ap()
    with ExitStack() as stack:
        k = Sched(nc)
        x_cur_buf = k.buf("dram.x_in")
        for li, (lid, specs) in enumerate(zip(layers, dram_specs_per_layer)):
            names = {}
            names["x_in"] = x_cur
            names["x_in_buf"] = x_cur_buf
            if li == len(layers) - 1:
                names["x_out"] = x_final
            else:
                names["x_out"] = nc.dram_tensor(f"x_mid{li}", [T, D], F32, kind="Internal").ap()
            names["x_out_buf"] = k.buf(f"dram.x_out{li}")
            names["s_spill"] = nc.dram_tensor(f"s_spill{li}", [T, D], F32, kind="Internal").ap()
            names["s_spill_buf"] = k.buf(f"dram.s_spill{li}")
            for nm, shape, dt in LAYER_SCRATCH.get(lid, []):
                names[nm] = nc.dram_tensor(f"L{lid}_{nm}", shape, dt, kind="Internal").ap()
                names[nm + "_buf"] = k.buf(f"dram.L{lid}_{nm}")
            for nm, shape in specs:
                names[nm] = nc.dram_tensor(f"L{lid}_{nm}", shape, F32, kind="ExternalInput").ap()
            names_all.append(names)
            x_cur = names["x_out"]
            x_cur_buf = names["x_out_buf"]
        C = Common(nc, k, stack)
        for names, emitter in zip(names_all, emitters):
            emitter(C, names)
        fo = k.op("sp", None)
        k.finalize()
        last = {}
        for q in DMA_QUEUES:
            for o in k.ops[q]:
                if o.dma:
                    last[o.sem_key] = o.sem_val
        fo.waits.extend([(key, val) for key, val in sorted(last.items()) if (key, val) not in fo.waits])
        k.emit_all(stack)
    return nc, k


LAYER_TABLE = [
    (0, "L0_DRAM_SPECS", "host_prep_l0", "emit_layer0"),
    (1, "L1_DRAM_SPECS", "host_prep_l1", "emit_layer1"),
    (2, "L2_DRAM_SPECS", "host_prep_l2", "emit_layer2"),
    (3, "L3_DRAM_SPECS", "host_prep_l3", "emit_layer3"),
]


def kernel(**inputs):
    g = globals()
    x = np.ascontiguousarray(inputs["L0_attn_x"].astype(np.float32, copy=False))
    layers = [row[0] for row in LAYER_TABLE]
    specs_all = [g[row[1]] for row in LAYER_TABLE]
    emitters = [g[row[3]] for row in LAYER_TABLE]
    nc, k = build_program(layers, specs_all, emitters)
    base = {}
    for lid, spec_n, prep_n, _ in LAYER_TABLE:
        prep = g[prep_n](inputs)
        for nm, _shape in g[spec_n]:
            base[f"L{lid}_{nm}"] = prep[nm]
    in_maps = []
    for c in range(8):
        m = dict(base)
        m["x_in"] = np.ascontiguousarray(x[c])
        in_maps.append(m)
    res = run_bass_kernel_spmd(nc, in_maps, core_ids=list(range(8)))
    return np.stack([np.asarray(r["x_out"]) for r in res.results]).astype(np.float32)


# version 15
# speedup vs baseline: 1.0427x; 1.0427x over previous
import numpy as np
import ml_dtypes
import concourse.bass as bass
import concourse.mybir as mybir
from concourse.bass_utils import run_bass_kernel_spmd
from contextlib import ExitStack

F32 = mybir.dt.float32
BF16 = mybir.dt.bfloat16
AF = mybir.ActivationFunctionType
ALU = mybir.AluOpType
AX = mybir.AxisListType
BF = ml_dtypes.bfloat16

T = 2048
D = 1024
NT = T // 128
NCT = D // 128

ENGS = ("te", "ve", "se", "gp", "sp")
SECT = {"te": "tensor", "ve": "vector", "se": "scalar", "gp": "gpsimd", "sp": "sync"}
DMA_QUEUES = ("gp", "sp", "se")
NSLOT = 8


class Buf:
    __slots__ = ("name", "writer", "readers")

    def __init__(self, name):
        self.name = name
        self.writer = None
        self.readers = []


class Op:
    __slots__ = ("eng", "emit", "idx", "gidx", "deps", "dma", "q", "dma_idx",
                 "need_inc", "waits", "sem_key", "sem_val", "fragile")

    def __init__(self):
        self.deps = set()
        self.need_inc = False
        self.waits = []
        self.dma = False
        self.fragile = False
        self.sem_key = None
        self.sem_val = 0


class Sched:

    def __init__(self, nc):
        self.nc = nc
        self.ops = {e: [] for e in ENGS}
        self.glist = []
        self.dma_counts = {q: 0 for q in DMA_QUEUES}
        self.dma_slot_last = {}

    def buf(self, name):
        return Buf(name)

    def op(self, eng, emit, reads=(), writes=(), dma=False, extra_deps=(), fragile=False):
        o = Op()
        o.eng = eng
        o.emit = emit
        o.dma = dma
        o.fragile = fragile or getattr(emit, "_fragile", False)
        o.idx = len(self.ops[eng])
        o.gidx = len(self.glist)
        for d in extra_deps:
            o.deps.add(d)
        if dma:
            assert eng in DMA_QUEUES
            i = self.dma_counts[eng]
            o.q = eng
            o.dma_idx = i
            self.dma_counts[eng] = i + 1
            prev = self.dma_slot_last.get((eng, i % NSLOT))
            if prev is not None:
                o.deps.add(prev)
            self.dma_slot_last[(eng, i % NSLOT)] = o
        raw = set()
        for b in reads:
            if b.writer is not None:
                o.deps.add(b.writer)
                raw.add(b.writer)
            if dma:
                b.readers.append(o)
            else:
                b.readers = [r for r in b.readers if r.dma or r.eng != eng] + [o]
        for b in writes:
            if b.writer is not None:
                o.deps.add(b.writer)
            for r in b.readers:
                o.deps.add(r)
            b.writer = o
            b.readers = []
        o.deps = {d for d in o.deps
                  if (d.dma or d.eng != eng or d.fragile or o.fragile
                      or (d in raw and eng in ("ve", "se"))) and d is not o}
        self.ops[eng].append(o)
        self.glist.append(o)
        return o

    def finalize(self):
        for o in self.glist:
            for d in o.deps:
                if not d.dma:
                    d.need_inc = True
        cnt = {e: 0 for e in ENGS}
        for e in ENGS:
            for o in self.ops[e]:
                if o.dma:
                    o.sem_key = (o.q, o.dma_idx % NSLOT)
                    o.sem_val = 16 * (o.dma_idx // NSLOT + 1)
                elif o.need_inc:
                    cnt[e] += 1
                    o.sem_key = e
                    o.sem_val = cnt[e]
        self.inc_totals = cnt
        clocks = {e: {} for e in ENGS}
        snap = [None] * len(self.glist)
        for o in self.glist:
            c = clocks[o.eng]
            for d in sorted(o.deps, key=lambda d: d.gidx):
                key, val = d.sem_key, d.sem_val
                if c.get(key, 0) < val:
                    o.waits.append((key, val))
                    for k2, v2 in snap[d.gidx].items():
                        if c.get(k2, 0) < v2:
                            c[k2] = v2
                    if c.get(key, 0) < val:
                        c[key] = val
            if o.dma:
                s = dict(c)
                s[o.sem_key] = max(s.get(o.sem_key, 0), o.sem_val)
                snap[o.gidx] = s
            else:
                s = dict(c)
                if o.need_inc:
                    s[o.eng] = max(s.get(o.eng, 0), o.sem_val)
                snap[o.gidx] = s
        n_waits = sum(len(o.waits) for o in self.glist)
        self.stats = dict(ops={e: len(self.ops[e]) for e in ENGS},
                          incs=cnt, waits=n_waits, dmas=dict(self.dma_counts))

    def emit_all(self, stack):
        nc = self.nc
        esems = {e: stack.enter_context(nc.semaphore(f"sem_{e}")) for e in ENGS}
        dsems = {}
        for q in DMA_QUEUES:
            for s in range(min(NSLOT, self.dma_counts[q])):
                dsems[(q, s)] = stack.enter_context(nc.semaphore(f"dsem_{q}{s}"))

        def sem_for(key):
            return esems[key] if isinstance(key, str) else dsems[key]

        with nc.Block() as blk:
            for e in ENGS:
                if not self.ops[e]:
                    continue

                def replay(eng, e=e):
                    for o in self.ops[e]:
                        for key, val in o.waits:
                            eng.wait_ge(sem_for(key), val)
                        if o.emit is None:
                            continue
                        inst = o.emit(eng)
                        if o.dma:
                            inst.then_inc(sem_for(o.sem_key), 16)
                        elif o.need_inc:
                            inst.then_inc(sem_for(o.sem_key), 1)

                getattr(blk, SECT[e])(replay)


class Arena:

    def __init__(self, sched, tensor, cols, name):
        self.k = sched
        self.t = tensor
        self.cols = cols
        self.name = name
        self.free_list = [(0, cols)]
        self.allocs = {}
        self.ghosts = []
        self.peak = 0

    def alloc(self, name, size):
        assert name not in self.allocs, name
        size = (size + 7) & ~7
        for i, (off, sz) in enumerate(self.free_list):
            if sz >= size:
                if sz == size:
                    self.free_list.pop(i)
                else:
                    self.free_list[i] = (off + size, sz - size)
                b = self.k.buf(f"{self.name}.{name}")
                self.peak = max(self.peak, self.cols - sum(z for _, z in self.free_list))
                keep = []
                for goff, gsz, gbuf in self.ghosts:
                    if goff < off + size and off < goff + gsz:
                        if gbuf.writer is not None:
                            b.readers.append(gbuf.writer)
                        b.readers.extend(gbuf.readers)
                        if not (off <= goff and goff + gsz <= off + size):
                            keep.append((goff, gsz, gbuf))
                    else:
                        keep.append((goff, gsz, gbuf))
                self.ghosts = keep
                self.allocs[name] = (off, size, b)
                return b
        raise MemoryError(f"arena {self.name}: cannot alloc {name}({size}); free={self.free_list}")

    def off(self, name):
        return self.allocs[name][0]

    def ap(self, name, c0=0, c1=None):
        off, size, _ = self.allocs[name]
        if c1 is None:
            c1 = size
        assert 0 <= c0 <= c1 <= size, (name, c0, c1, size)
        return self.t[:, off + c0: off + c1]

    def vap(self, name, c0, c1, pattern, **kw):
        off, size, _ = self.allocs[name]
        assert c1 <= size
        return self.t[:, off + c0: off + c1].rearrange(pattern, **kw)

    def rap(self, name, p0, p1, c0, c1):
        off, size, _ = self.allocs[name]
        assert c1 <= size
        return self.t[p0:p1, off + c0: off + c1]

    def buf(self, name):
        return self.allocs[name][2]

    def free(self, name):
        off, size, b = self.allocs.pop(name)
        self.ghosts.append((off, size, b))
        self.free_list.append((off, size))
        self.free_list.sort()
        merged = []
        for off2, sz2 in self.free_list:
            if merged and merged[-1][0] + merged[-1][1] == off2:
                merged[-1] = (merged[-1][0], merged[-1][1] + sz2)
            else:
                merged.append((off2, sz2))
        self.free_list = merged


class Common:
    def __init__(self, nc, k, stack, bf_cols=60544, f32_cols=22912):
        self.nc = nc
        self.k = k
        self.stack = stack
        tb = stack.enter_context(nc.sbuf_tensor("arena_bf", [128, bf_cols], BF16))
        tf = stack.enter_context(nc.sbuf_tensor("arena_f32", [128, f32_cols], F32))
        self.abf = Arena(k, tb, bf_cols, "bf")
        self.af = Arena(k, tf, f32_cols, "f32")
        self.ps = {}
        self.psbuf = {}
        for nm, shape, dt in [("A", [128, 512], F32), ("B", [128, 512], F32),
                              ("C", [128, 1024], F32), ("D", [128, 1024], F32),
                              ("E", [128, 512], F32), ("T", [128, 1024], BF16)]:
            self.ps[nm] = stack.enter_context(nc.psum_tensor(f"ps{nm}", shape, dt))
            self.psbuf[nm] = k.buf(f"ps{nm}")

    def load(self, q, dst_ap, src_ap, wbuf, rbufs=()):
        return self.k.op(q, lambda e: e.dma_start(dst_ap, src_ap),
                         reads=list(rbufs), writes=[wbuf], dma=True)

    def store(self, q, dst_ap, src_ap, rbufs, wbufs=()):
        return self.k.op(q, lambda e: e.dma_start(dst_ap, src_ap),
                         reads=list(rbufs), writes=list(wbufs), dma=True)



def MM(out_ap, lhsT_ap, rhs_ap, start, stop, skip=False):
    return lambda e, o=out_ap, l=lhsT_ap, r=rhs_ap, st=start, sp=stop, sk=skip: \
        e.matmul(o, l, r, start=st, stop=sp, skip_group_check=sk)


def TRANS(out_ap, in_ap, ident_ap):
    return lambda e, o=out_ap, i=in_ap, d=ident_ap: e.transpose(o, i, d)


def CP(out_ap, in_ap):
    return lambda e, o=out_ap, i=in_ap: e.tensor_copy(o, i)


def TT(out_ap, a_ap, b_ap, op):
    return lambda e, o=out_ap, a=a_ap, b=b_ap, p=op: e.tensor_tensor(o, a, b, p)


def _frag(fn):
    fn._fragile = True
    return fn


def TS(out_ap, in_ap, s1, s2, op0, op1=ALU.bypass, accum=None):
    if s2 is None and accum is None:
        fn = lambda e, o=out_ap, i=in_ap, s=s1, p=op0: e.tensor_single_scalar(o, i, s, p)
    else:
        fn = lambda e, o=out_ap, i=in_ap, a=s1, b=s2, p0=op0, p1=op1, ac=accum: \
            e.tensor_scalar(o, i, a, b, p0, p1, accum_out=ac)
    if not isinstance(s1, (int, float)) or (s2 is not None and not isinstance(s2, (int, float))) or accum is not None:
        _frag(fn)
    return fn


def STT(out_ap, in0_ap, scalar, in1_ap, op0, op1, accum=None):
    return _frag(lambda e, o=out_ap, i=in0_ap, s=scalar, j=in1_ap, p0=op0, p1=op1, ac=accum:
                 e.scalar_tensor_tensor(o, i, s, j, p0, p1, accum_out=ac))


def ACT(out_ap, in_ap, func, bias=0.0, scale=1.0, accum=None):
    fn = lambda e, o=out_ap, i=in_ap, f=func, b=bias, s=scale, ac=accum: \
        e.activation(o, i, f, bias=b, scale=s, accum_out=ac)
    if accum is not None:
        _frag(fn)
    return fn


def RED(out_ap, in_ap, op, negate=False):
    return _frag(lambda e, o=out_ap, i=in_ap, p=op, n=negate: e.tensor_reduce(o, i, AX.X, p, negate=n))


def RECIP(out_ap, in_ap):
    return _frag(lambda e, o=out_ap, i=in_ap: e.reciprocal(o, i))


def MEMSET(ap, val):
    return lambda e, a=ap, v=val: e.memset(a, v)


def SCAN(out_ap, d0_ap, d1_ap, init_ap):
    return _frag(lambda e, o=out_ap, a=d0_ap, b=d1_ap, i=init_ap:
                 e.tensor_tensor_scan(o, a, b, i, ALU.mult, ALU.add))


def RED2(out_ap, in_ap, op, negate=False):
    return _frag(lambda e, o=out_ap, i=in_ap, p=op, n=negate: e.tensor_reduce(o, i, AX.X, p, negate=n))



def emit_ln_stats(C, x_tiled, rolls, st, per_tile_cb, src_buf=None):
    k, af = C.k, C.af
    sb = af.buf(st)
    jb = af.buf("JUNK")
    for tt in range(NT):
        slot = rolls[tt % len(rolls)]
        xb = af.buf(slot)
        C.load("sp", af.ap(slot), x_tiled[tt], xb, rbufs=[src_buf] if src_buf is not None else [])
        b = tt * 8
        k.op("ve", fragile=True, emit=RED(af.ap(st, b, b + 1), af.ap(slot), ALU.add, negate=True), reads=[xb], writes=[sb])
        k.op("ve", TS(af.ap(st, b + 1, b + 2), af.ap(st, b, b + 1), 1.0 / D, None, ALU.mult),
             reads=[sb], writes=[sb])
        k.op("se", ACT(af.ap("JUNK"), af.ap(slot), AF.Square, bias=af.ap(st, b + 1, b + 2),
                       scale=1.0, accum=af.ap(st, b + 2, b + 3)),
             reads=[xb, sb], writes=[jb, sb])
        k.op("ve", TS(af.ap(st, b + 2, b + 3), af.ap(st, b + 2, b + 3), 1.0 / D, 1e-5, ALU.mult, ALU.add),
             reads=[sb], writes=[sb])
        k.op("se", ACT(af.ap(st, b + 3, b + 4), af.ap(st, b + 2, b + 3), AF.Ln),
             reads=[sb], writes=[sb])
        k.op("se", ACT(af.ap(st, b + 4, b + 5), af.ap(st, b + 3, b + 4), AF.Exp, scale=-0.5),
             reads=[sb], writes=[sb])
        k.op("ve", TT(af.ap(st, b + 5, b + 6), af.ap(st, b + 1, b + 2), af.ap(st, b + 4, b + 5), ALU.mult),
             reads=[sb], writes=[sb])
        per_tile_cb(tt, slot)


def emit_norm_transpose(C, tt, slot, roll_bf, dstT, scale_ap, bias_ap, sbufs):
    k, af, abf = C.k, C.af, C.abf
    nm = roll_bf[tt % len(roll_bf)]
    k.op("se", ACT(abf.ap(nm), af.ap(slot), AF.Identity, bias=bias_ap, scale=scale_ap),
         reads=[af.buf(slot)] + sbufs, writes=[abf.buf(nm)])
    for ct in range(NCT):
        k.op("te", TRANS(C.ps["T"][:, ct * 128:(ct + 1) * 128],
                         abf.ap(nm, ct * 128, (ct + 1) * 128), abf.ap("IDENT")),
             reads=[abf.buf(nm), abf.buf("IDENT")], writes=[C.psbuf["T"]])
    src = C.ps["T"][:, 0:1024].rearrange("p (a c) -> p a c", a=NCT)
    dst = abf.vap(dstT, 0, NCT * T, "p (a c) -> p a c", c=T)[:, :, tt * 128:(tt + 1) * 128]
    k.op("ve", CP(dst, src), reads=[C.psbuf["T"]], writes=[abf.buf(dstT)])


def emit_bias_rep(C, brow_dram, dst_name, width=D):
    k, af = C.k, C.af
    af.alloc("BROW_t", width)
    C.load("sp", af.t[0:1, af.off("BROW_t"):af.off("BROW_t") + width], brow_dram, af.buf("BROW_t"))
    nh = (width + 511) // 512
    for half in range(nh):
        w = min(512, width - half * 512)
        k.op("te", MM(C.ps["C"][:, half * 512:half * 512 + w],
                      af.t[0:1, af.off("ONES1"):af.off("ONES1") + 128],
                      af.t[0:1, af.off("BROW_t") + half * 512:af.off("BROW_t") + half * 512 + w],
                      True, True),
             reads=[af.buf("BROW_t"), af.buf("ONES1")], writes=[C.psbuf["C"]])
    k.op("ve", CP(af.ap(dst_name), C.ps["C"][:, 0:width]),
         reads=[C.psbuf["C"]], writes=[af.buf(dst_name)])
    af.free("BROW_t")



def emit_layer1(C, names):
    nc, k = C.nc, C.k
    xin_b = names["x_in_buf"]
    xout_b = names["x_out_buf"]
    ssp_b = names["s_spill_buf"]
    abf, af = C.abf, C.af
    HG = 4
    NGRP = 4

    x_in, x_out, s_spill = names["x_in"], names["x_out"], names["s_spill"]
    x_tiled = x_in.rearrange("(a p) c -> a p c", p=128)
    s_tiled = s_spill.rearrange("(a p) c -> a p c", p=128)
    xo_tiled = x_out.rearrange("(a p) c -> a p c", p=128)

    abf.alloc("XT", NCT * T)
    abf.alloc("IDENT", 128)
    C.load("gp", abf.ap("IDENT"), names["ident"], abf.buf("IDENT"))
    af.alloc("STATS", 128)
    af.alloc("ONES1", 128)
    k.op("ve", MEMSET(af.t[0:1, af.off("ONES1"):af.off("ONES1") + 128], 1.0), writes=[af.buf("ONES1")])
    af.alloc("JUNK", 1024)
    af.alloc("LGT", NT * 8)
    af.alloc("MSK", NT * 8)
    af.alloc("SCR", NT * 4)
    rolls = ["XR0", "XR1"]
    for nm in rolls:
        af.alloc(nm, 1024)
    roll_bf = ["XNR0", "XNR1"]
    for nm in roll_bf:
        abf.alloc(nm, 1024)

    emit_ln_stats(C, x_tiled, rolls, "STATS", src_buf=names["x_in_buf"],
                  per_tile_cb=lambda tt, slot: emit_norm_transpose(
                      C, tt, slot, roll_bf, "XT",
                      C.af.ap("STATS", tt * 8 + 4, tt * 8 + 5),
                      C.af.ap("STATS", tt * 8 + 5, tt * 8 + 6), [C.af.buf("STATS")]))

    if "dbg_stats" in names:
        C.store("sp", names["dbg_stats"], af.ap("STATS", 0, 128), [af.buf("STATS")])
    abf.alloc("OUTCT", NCT * T)
    abf.alloc("PST", 16 * 512)
    pstb = abf.buf("PST")
    pst_tiles = [k.buf(f"l1pst{j}") for j in range(16)]
    pst_first = [True] * 16
    abf.alloc("EMSK", 896)
    C.load("gp", abf.ap("EMSK"), names["emask"], abf.buf("EMSK"))
    for nm in ("SP100", "SP101", "SP110", "SP111"):
        af.alloc(nm, 512)
    af.alloc("OSTG", 1024)
    af.alloc("DENS", 1024)
    af.alloc("ONES64", 64)
    k.op("ve", MEMSET(af.rap("ONES64", 64, 65, 0, 64), 1.0), writes=[af.buf("ONES64")])

    for hg in range(NGRP):
        abf.alloc("QT", 2 * T)
        abf.alloc("KT", 2 * T)
        VSTR = HG * 66
        abf.alloc("VB", NT * VSTR)
        abf.alloc("WQ", NCT * 128)
        abf.alloc("WK", NCT * 128)
        abf.alloc("WV", NCT * 256)
        wt = names["wqkvT"]
        for ct in range(NCT):
            C.load("gp", abf.ap("WV", ct * 256, (ct + 1) * 256),
                   wt[ct * 128:(ct + 1) * 128, 2048 + hg * 256:2048 + (hg + 1) * 256], abf.buf("WV"))
        vb4 = abf.vap("VB", 0, NT * VSTR, "p (t h c) -> p t h c", h=HG, c=66)
        k.op("ve", MEMSET(vb4[:, :, :, 64:66], 1.0), writes=[abf.buf("VB")])
        k.op("ve", MEMSET(vb4[:, :, :, 65:66], 0.0), writes=[abf.buf("VB")])

        for pr in range(2):
            for ct in range(NCT):
                C.load("gp", abf.ap("WQ", ct * 128, (ct + 1) * 128),
                       wt[ct * 128:(ct + 1) * 128, hg * 256 + pr * 128:hg * 256 + (pr + 1) * 128],
                       abf.buf("WQ"))
                C.load("gp", abf.ap("WK", ct * 128, (ct + 1) * 128),
                       wt[ct * 128:(ct + 1) * 128, 1024 + hg * 256 + pr * 128:1024 + hg * 256 + (pr + 1) * 128],
                       abf.buf("WK"))
            for dst, wnm in (("QT", "WQ"), ("KT", "WK")):
                for qc in range(4):
                    pj = ("C", "D")[qc % 2]
                    for kt in range(NCT):
                        k.op("te", MM(C.ps[pj][:, 0:512],
                                      abf.ap(wnm, kt * 128, (kt + 1) * 128),
                                      abf.ap("XT", kt * T + qc * 512, kt * T + (qc + 1) * 512),
                                      kt == 0, kt == NCT - 1),
                             reads=[abf.buf(wnm), abf.buf("XT")], writes=[C.psbuf[pj]])
                    k.op("ve", CP(abf.ap(dst, pr * T + qc * 512, pr * T + (qc + 1) * 512),
                                  C.ps[pj][:, 0:512]),
                         reads=[C.psbuf[pj]], writes=[abf.buf(dst)])
        for mt in range(NT):
            pj = ("D", "C")[mt % 2]
            for kt in range(NCT):
                k.op("te", MM(C.ps[pj][:, 0:256],
                              abf.ap("XT", kt * T + mt * 128, kt * T + (mt + 1) * 128),
                              abf.ap("WV", kt * 256, (kt + 1) * 256),
                              kt == 0, kt == NCT - 1),
                     reads=[abf.buf("WV"), abf.buf("XT")], writes=[C.psbuf[pj]])
            k.op("ve", CP(abf.vap("VB", mt * VSTR, (mt + 1) * VSTR, "p (h c) -> p h c", c=66)[:, :, 0:64],
                          C.ps[pj][:, 0:256].rearrange("p (h d) -> p h d", h=HG)),
                 reads=[C.psbuf[pj]], writes=[abf.buf("VB")])
        abf.free("WQ"); abf.free("WK"); abf.free("WV")

        if "dbg_qt" in names and hg == 0:
            C.store("gp", names["dbg_qt"], abf.ap("QT", 0, T), [abf.buf("QT")])
            C.store("gp", names["dbg_kt"], abf.ap("KT", 0, T), [abf.buf("KT")])
            C.store("gp", names["dbg_xt"], abf.ap("XT", 0, T), [abf.buf("XT")])
        for qc in range(4):
            q0 = qc * 512
            ktmax = min(NT, 4 * (qc + 1))
            for gp2 in range(2):
                pr = gp2
                pw_ops = [{}, {}]
                ex_ops = [{}, {}]

                def av1(kt, s_i, ktmax=ktmax, gp2=gp2):
                    off = s_i * 512
                    sl = s_i * 8 + kt % 8
                    ed = [pw_ops[1][kt]] if (s_i == 0 and kt in pw_ops[1]) else []
                    k.op("te", MM(C.ps["C"][0:66, off:off + 512],
                                  abf.ap("VB", kt * VSTR + (gp2 * 2 + s_i) * 66,
                                         kt * VSTR + (gp2 * 2 + s_i + 1) * 66),
                                  abf.ap("PST", sl * 512, (sl + 1) * 512),
                                  kt == 0, kt == ktmax - 1, skip=True),
                         reads=[abf.buf("VB"), pst_tiles[sl]], writes=[C.psbuf["C"]],
                         extra_deps=ed)

                for kt in range(ktmax):
                    dc = q0 - kt * 128
                    for s_i in range(2):
                        half = s_i * 64
                        pband = ("A", "B")[kt % 2] if s_i == 0 else ("E", "D")[kt % 2]
                        psap = C.ps[pband][:, 0:512]
                        sp1n = f"SP1{s_i}{kt % 2}"
                        sp1b = af.buf(sp1n)
                        sl = s_i * 8 + kt % 8
                        ed = [ex_ops[1][kt - 2]] if (s_i == 0 and (kt - 2) in ex_ops[1]) else []
                        k.op("te", MM(psap,
                                      abf.rap("KT", half, half + 64, pr * T + kt * 128, pr * T + (kt + 1) * 128),
                                      abf.rap("QT", half, half + 64, pr * T + q0, pr * T + q0 + 512),
                                      True, True),
                             reads=[abf.buf("KT"), abf.buf("QT")], writes=[C.psbuf[pband]],
                             extra_deps=ed)
                        ex_ops[s_i][kt] = k.op("se", ACT(af.ap(sp1n), psap, AF.Exp, bias=0.0, scale=0.125),
                                               reads=[C.psbuf[pband]], writes=[sp1b])
                        k.op("se", ACT(af.ap(sp1n), af.ap(sp1n), AF.Ln, bias=1.0, scale=1.0),
                             reads=[sp1b], writes=[sp1b])
                        if dc <= 127:
                            k.op("ve", TT(af.ap(sp1n), af.ap(sp1n), abf.ap("EMSK", dc + 384, dc + 384 + 512), ALU.mult),
                                 reads=[sp1b, abf.buf("EMSK")], writes=[sp1b])
                        wl = [pst_tiles[sl]]
                        if pst_first[sl]:
                            wl.append(pstb)
                            pst_first[sl] = False
                        pw_ops[s_i][kt] = k.op("ve", TT(abf.ap("PST", sl * 512, (sl + 1) * 512),
                                                        af.ap(sp1n), af.ap(sp1n), ALU.mult),
                                               reads=[sp1b], writes=wl)
                    if kt >= 2:
                        av1(kt - 2, 0)
                        av1(kt - 2, 1)
                for ktl in (ktmax - 2, ktmax - 1):
                    av1(ktl, 0)
                    av1(ktl, 1)
                for s_i in range(2):
                    off = s_i * 512
                    h = hg * HG + gp2 * 2 + s_i
                    k.op("ve", CP(af.rap("OSTG", 0, 64, off, off + 512), C.ps["C"][0:64, off:off + 512]),
                         reads=[C.psbuf["C"]], writes=[af.buf("OSTG")])
                    k.op("ve", TS(af.rap("DENS", 64, 65, off, off + 512), C.ps["C"][64:65, off:off + 512],
                                  1e-9, None, ALU.add),
                         reads=[C.psbuf["C"]], writes=[af.buf("DENS")])
                    k.op("se", ACT(af.rap("DENS", 64, 65, off, off + 512),
                                   af.rap("DENS", 64, 65, off, off + 512), AF.Ln),
                         reads=[af.buf("DENS")], writes=[af.buf("DENS")])
                    k.op("se", ACT(af.rap("DENS", 64, 65, off, off + 512),
                                   af.rap("DENS", 64, 65, off, off + 512), AF.Exp, scale=-1.0),
                         reads=[af.buf("DENS")], writes=[af.buf("DENS")])
                    k.op("te", MM(C.ps["D"][0:64, off:off + 512], af.rap("ONES64", 64, 65, 0, 64),
                                  af.rap("DENS", 64, 65, off, off + 512), True, True, skip=True),
                         reads=[af.buf("ONES64"), af.buf("DENS")], writes=[C.psbuf["D"]])
                    ct, chalf = h // 2, (h % 2) * 64
                    k.op("ve", TT(abf.rap("OUTCT", chalf, chalf + 64, ct * T + q0, ct * T + q0 + 512),
                                  af.rap("OSTG", 0, 64, off, off + 512),
                                  C.ps["D"][0:64, off:off + 512], ALU.mult),
                         reads=[af.buf("OSTG"), C.psbuf["D"]], writes=[abf.buf("OUTCT")])
        abf.free("QT"); abf.free("KT"); abf.free("VB")
    abf.free("PST"); abf.free("EMSK")
    for nm in ("SP100", "SP101", "SP110", "SP111"):
        af.free(nm)
    af.free("OSTG"); af.free("DENS"); af.free("ONES64")

    abf.alloc("WP", NCT * 1024)
    for ct in range(NCT):
        C.load("gp", abf.ap("WP", ct * 1024, (ct + 1) * 1024),
               names["wprojT"][ct * 128:(ct + 1) * 128, :], abf.buf("WP"))
    af.alloc("BIAS", 1024)
    emit_bias_rep(C, names["bproj"], "BIAS")
    af.alloc("SOUT", 2 * 1024)
    for mt in range(NT):
        pb = "C" if mt % 2 == 0 else "D"
        for ct in range(NCT):
            for half in range(2):
                k.op("te", MM(C.ps[pb][:, half * 512:(half + 1) * 512],
                              abf.ap("OUTCT", ct * T + mt * 128, ct * T + (mt + 1) * 128),
                              abf.ap("WP", ct * 1024 + half * 512, ct * 1024 + (half + 1) * 512),
                              ct == 0, ct == NCT - 1),
                     reads=[abf.buf("OUTCT"), abf.buf("WP")], writes=[C.psbuf[pb]])
        sl = mt % 2
        k.op("ve", TT(af.ap("SOUT", sl * 1024, (sl + 1) * 1024), C.ps[pb][:, 0:1024], af.ap("BIAS"), ALU.add),
             reads=[C.psbuf[pb], af.buf("BIAS")], writes=[af.buf("SOUT")])
        C.store("sp", s_tiled[mt], af.ap("SOUT", sl * 1024, (sl + 1) * 1024), [af.buf("SOUT")], wbufs=[ssp_b])
        if "s_dbg" in names:
            C.store("sp", names["s_dbg"].rearrange("(a p) c -> a p c", p=128)[mt],
                    af.ap("SOUT", sl * 1024, (sl + 1) * 1024), [af.buf("SOUT")])
    abf.free("WP"); abf.free("OUTCT"); af.free("BIAS"); af.free("SOUT")

    abf.alloc("GW", NCT * 8)
    C.load("gp", abf.vap("GW", 0, NCT * 8, "p (a c) -> p a c", c=8),
           names["gwT"].rearrange("(a p) c -> p a c", p=128), abf.buf("GW"))
    for mt in range(NT):
        for kt in range(NCT):
            k.op("te", MM(C.ps["E"][:, 0:8],
                          abf.ap("XT", kt * T + mt * 128, kt * T + (mt + 1) * 128),
                          abf.ap("GW", kt * 8, (kt + 1) * 8),
                          kt == 0, kt == NCT - 1),
                 reads=[abf.buf("XT"), abf.buf("GW")], writes=[C.psbuf["E"]])
        k.op("ve", CP(af.ap("LGT", mt * 8, (mt + 1) * 8), C.ps["E"][:, 0:8]),
             reads=[C.psbuf["E"]], writes=[af.buf("LGT")])
    abf.free("GW")
    for mt in range(NT):
        b4 = mt * 4
        k.op("ve", fragile=True, emit=RED(af.ap("SCR", b4, b4 + 1), af.ap("LGT", mt * 8, (mt + 1) * 8), ALU.max),
             reads=[af.buf("LGT")], writes=[af.buf("SCR")])
        k.op("se", ACT(af.ap("SCR", b4 + 1, b4 + 2), af.ap("SCR", b4, b4 + 1), AF.Sigmoid),
             reads=[af.buf("SCR")], writes=[af.buf("SCR")])
        k.op("ve", TS(af.ap("MSK", mt * 8, (mt + 1) * 8), af.ap("LGT", mt * 8, (mt + 1) * 8),
                      af.ap("SCR", b4, b4 + 1), None, ALU.is_ge),
             reads=[af.buf("LGT"), af.buf("SCR")], writes=[af.buf("MSK")])
        k.op("ve", TT(af.ap("SCR", b4 + 2, b4 + 3), af.ap("STATS", mt * 8 + 4, mt * 8 + 5),
                      af.ap("SCR", b4 + 1, b4 + 2), ALU.mult),
             reads=[af.buf("SCR"), af.buf("STATS")], writes=[af.buf("SCR")])
        k.op("ve", TT(af.ap("SCR", b4 + 3, b4 + 4), af.ap("STATS", mt * 8 + 5, mt * 8 + 6),
                      af.ap("SCR", b4 + 1, b4 + 2), ALU.mult),
             reads=[af.buf("SCR"), af.buf("STATS")], writes=[af.buf("SCR")])

    af.alloc("ACC", NT * 1024)
    accb = af.buf("ACC")
    af.alloc("SILU", 512)
    silub = af.buf("SILU")
    for i in range(16):
        abf.alloc(f"H{i}", 1024)
    for i in range(4):
        abf.alloc(f"WG{i}", NCT * 128)
        abf.alloc(f"WU{i}", NCT * 128)
    abf.alloc("WD", 16 * 1024)
    wdb = abf.buf("WD")

    def expert(ei, wgt, wut, wdt, src_name, masked):
        for tc in range(2):
            t0 = tc * 1024
            for ft in range(16):
                wg_s, wu_s = f"WG{ft % 4}", f"WU{ft % 4}"
                C.load("gp", abf.vap(wg_s, 0, NCT * 128, "p (a c) -> p a c", c=128),
                       wgt.rearrange("(a p) f -> p a f", p=128)[:, :, ft * 128:(ft + 1) * 128],
                       abf.buf(wg_s))
                C.load("gp", abf.vap(wu_s, 0, NCT * 128, "p (a c) -> p a c", c=128),
                       wut.rearrange("(a p) f -> p a f", p=128)[:, :, ft * 128:(ft + 1) * 128],
                       abf.buf(wu_s))
                if tc == 0:
                    C.load("gp", abf.ap("WD", ft * 1024, (ft + 1) * 1024),
                           wdt[ft * 128:(ft + 1) * 128, :], wdb)
                for nh in range(2):
                    q0 = t0 + nh * 512
                    for kt in range(NCT):
                        k.op("te", MM(C.ps["A"][:, 0:512],
                                      abf.ap(wg_s, kt * 128, (kt + 1) * 128),
                                      abf.ap(src_name, kt * T + q0, kt * T + q0 + 512),
                                      kt == 0, kt == NCT - 1),
                             reads=[abf.buf(wg_s), abf.buf(src_name)], writes=[C.psbuf["A"]])
                    k.op("se", ACT(af.ap("SILU"), C.ps["A"][:, 0:512], AF.Silu),
                         reads=[C.psbuf["A"]], writes=[silub])
                    for kt in range(NCT):
                        k.op("te", MM(C.ps["B"][:, 0:512],
                                      abf.ap(wu_s, kt * 128, (kt + 1) * 128),
                                      abf.ap(src_name, kt * T + q0, kt * T + q0 + 512),
                                      kt == 0, kt == NCT - 1),
                             reads=[abf.buf(wu_s), abf.buf(src_name)], writes=[C.psbuf["B"]])
                    k.op("ve", TT(abf.ap(f"H{ft}", nh * 512, (nh + 1) * 512), af.ap("SILU"),
                                  C.ps["B"][:, 0:512], ALU.mult),
                         reads=[silub, C.psbuf["B"]], writes=[abf.buf(f"H{ft}")])
            for mi in range(8):
                mt = tc * 8 + mi
                pb = "C" if mi % 2 == 0 else "D"
                for ft in range(16):
                    for half in range(2):
                        k.op("te", MM(C.ps[pb][:, half * 512:(half + 1) * 512],
                                      abf.ap(f"H{ft}", mi * 128, (mi + 1) * 128),
                                      abf.ap("WD", ft * 1024 + half * 512, ft * 1024 + (half + 1) * 512),
                                      ft == 0, ft == 15),
                             reads=[abf.buf(f"H{ft}"), wdb], writes=[C.psbuf[pb]])
                if masked:
                    k.op("ve", STT(af.ap("ACC", mt * 1024, (mt + 1) * 1024), C.ps[pb][:, 0:1024],
                                   af.ap("MSK", mt * 8 + ei, mt * 8 + ei + 1),
                                   af.ap("ACC", mt * 1024, (mt + 1) * 1024), ALU.mult, ALU.add),
                         reads=[C.psbuf[pb], af.buf("MSK"), accb], writes=[accb])
                else:
                    k.op("ve", CP(af.ap("ACC", mt * 1024, (mt + 1) * 1024), C.ps[pb][:, 0:1024]),
                         reads=[C.psbuf[pb]], writes=[accb])

    expert(-1, names["sgT"], names["suT"], names["sdT"], "XT", masked=False)
    abf.free("XT")

    abf.alloc("XST", NCT * T)
    for tt in range(NT):
        slot = rolls[tt % 2]
        xb = af.buf(slot)
        C.load("sp", af.ap(slot), x_tiled[tt], xb, rbufs=[xin_b])
        b4 = tt * 4
        emit_norm_transpose(C, tt, slot, roll_bf, "XST",
                            af.ap("SCR", b4 + 2, b4 + 3), af.ap("SCR", b4 + 3, b4 + 4),
                            [af.buf("SCR")])
    for e_i in range(8):
        expert(e_i, names["rgT"][e_i], names["ruT"][e_i], names["rdT"][e_i], "XST", masked=True)

    for i in range(16):
        abf.free(f"H{i}")
    for i in range(4):
        abf.free(f"WG{i}"); abf.free(f"WU{i}")
    abf.free("WD"); abf.free("XST")
    af.free("SILU");  af.free("LGT"); af.free("MSK"); af.free("JUNK")

    af.alloc("SRT", 1024)
    for mt in range(NT):
        slot = rolls[mt % 2]
        C.load("sp", af.ap("SRT"), s_tiled[mt], af.buf("SRT"), rbufs=[ssp_b])
        C.load("sp", af.ap(slot), x_tiled[mt], af.buf(slot), rbufs=[xin_b])
        acc_ap = af.ap("ACC", mt * 1024, (mt + 1) * 1024)
        k.op("se", ACT(acc_ap, acc_ap, AF.Silu), reads=[accb], writes=[accb])
        k.op("ve", TT(acc_ap, acc_ap, af.ap("SRT"), ALU.mult),
             reads=[accb, af.buf("SRT")], writes=[accb])
        k.op("ve", TT(acc_ap, acc_ap, af.ap(slot), ALU.add),
             reads=[accb, af.buf(slot)], writes=[accb])
        C.store("sp", xo_tiled[mt], acc_ap, [accb], wbufs=[xout_b])
    af.free("SRT"); af.free("ACC"); af.free("SCR")
    for nm in rolls:
        af.free(nm)
    for nm in roll_bf:
        abf.free(nm)
    af.free("STATS")
    abf.free("IDENT")
    af.free("ONES1")



def host_prep_l1(inp):
    ct = np.ascontiguousarray
    d = {}
    d["wqkvT"] = ct(inp["L1_attn_w_qkv"].T)
    d["wprojT"] = ct(inp["L1_attn_w_proj"].T)
    d["bproj"] = ct(inp["L1_attn_b_proj"].reshape(1, D))
    d["gwT"] = ct(inp["L1_ffn_gate_w"].T)
    d["sgT"] = ct(inp["L1_ffn_shared_gate"].T)
    d["suT"] = ct(inp["L1_ffn_shared_up"].T)
    d["sdT"] = ct(inp["L1_ffn_shared_down"].T)
    d["rgT"] = ct(inp["L1_ffn_r_gate"].transpose(0, 2, 1))
    d["ruT"] = ct(inp["L1_ffn_r_up"].transpose(0, 2, 1))
    d["rdT"] = ct(inp["L1_ffn_r_down"].transpose(0, 2, 1))
    d["ident"] = np.eye(128, dtype=np.float32)
    E = np.zeros((128, 896), dtype=np.float32)
    for i in range(128):
        E[i, 384 + i:] = 1.0
    d["emask"] = E
    return d


L1_DRAM_SPECS = [
    ("wqkvT", [D, 3 * D]), ("wprojT", [D, D]), ("bproj", [1, D]),
    ("gwT", [D, 8]), ("sgT", [D, 2048]), ("suT", [D, 2048]), ("sdT", [2048, D]),
    ("rgT", [8, D, 2048]), ("ruT", [8, D, 2048]), ("rdT", [8, 2048, D]),
    ("ident", [128, 128]), ("emask", [128, 896]),
]




def emit_layer3(C, names):
    nc, k = C.nc, C.k
    xin_b = names["x_in_buf"]
    xout_b = names["x_out_buf"]
    ssp_b = names["s_spill_buf"]
    abf, af = C.abf, C.af
    HG = 4
    NGRP = 4

    x_in, x_out, s_spill = names["x_in"], names["x_out"], names["s_spill"]
    x_tiled = x_in.rearrange("(a p) c -> a p c", p=128)
    s_tiled = s_spill.rearrange("(a p) c -> a p c", p=128)
    xo_tiled = x_out.rearrange("(a p) c -> a p c", p=128)

    abf.alloc("XT", NCT * T)
    abf.alloc("IDENT", 128)
    C.load("gp", abf.ap("IDENT"), names["ident"], abf.buf("IDENT"))
    af.alloc("STATS", 128)
    af.alloc("ONES1", 128)
    k.op("ve", MEMSET(af.t[0:1, af.off("ONES1"):af.off("ONES1") + 128], 1.0), writes=[af.buf("ONES1")])
    af.alloc("JUNK", 1024)
    af.alloc("LGT", NT * 8)
    af.alloc("CW", NT * 8)
    af.alloc("SSC", NT * 8)
    rolls = ["XR0", "XR1"]
    for nm in rolls:
        af.alloc(nm, 1024)
    roll_bf = ["XNR0", "XNR1"]
    for nm in roll_bf:
        abf.alloc(nm, 1024)

    emit_ln_stats(C, x_tiled, rolls, "STATS", src_buf=names["x_in_buf"],
                  per_tile_cb=lambda tt, slot: emit_norm_transpose(
                      C, tt, slot, roll_bf, "XT",
                      C.af.ap("STATS", tt * 8 + 4, tt * 8 + 5),
                      C.af.ap("STATS", tt * 8 + 5, tt * 8 + 6), [C.af.buf("STATS")]))

    abf.alloc("OUTCT", NCT * T)
    abf.alloc("PST", 16 * 512)
    pstb = abf.buf("PST")
    pst_tiles = [k.buf(f"l3pst{j}") for j in range(16)]
    pst_first = [True] * 16
    for nm in ("SP100", "SP101", "SP110", "SP111"):
        af.alloc(nm, 512)
    af.alloc("STRB0", 3968)
    af.alloc("STRB1", 3968)
    af.alloc("OSTG", 1024)
    af.alloc("DENS", 1024)
    af.alloc("ONES64", 64)
    k.op("ve", MEMSET(af.rap("ONES64", 64, 65, 0, 64), 1.0), writes=[af.buf("ONES64")])
    af.alloc("BCOL", 24)
    C.load("sp", af.ap("BCOL", 0, 16), names["b_in_cols"], af.buf("BCOL"))
    af.alloc("VBIAS", 256)
    af.alloc("BINV", 1024)
    C.load("sp", af.t[0:1, af.off("BINV"):af.off("BINV") + 1024], names["b_inR"][0:1, 2048:3072], af.buf("BINV"))

    win = names["W_in"]
    for hg in range(NGRP):
        abf.alloc("QT", 2 * T)
        abf.alloc("KT", 2 * T)
        VSTR = HG * 66
        abf.alloc("VB", NT * VSTR)
        abf.alloc("WQ", NCT * 128)
        abf.alloc("WK", NCT * 128)
        abf.alloc("WV", NCT * 256)
        for ct in range(NCT):
            C.load("gp", abf.ap("WV", ct * 256, (ct + 1) * 256),
                   win[ct * 128:(ct + 1) * 128, 2048 + hg * 256:2048 + (hg + 1) * 256], abf.buf("WV"))
        vb4 = abf.vap("VB", 0, NT * VSTR, "p (t h c) -> p t h c", h=HG, c=66)
        k.op("ve", MEMSET(vb4[:, :, :, 64:66], 1.0), writes=[abf.buf("VB")])
        k.op("ve", MEMSET(vb4[:, :, :, 65:66], 0.0), writes=[abf.buf("VB")])
        k.op("te", MM(C.ps["C"][:, 0:256],
                      af.t[0:1, af.off("ONES1"):af.off("ONES1") + 128],
                      af.t[0:1, af.off("BINV") + hg * 256:af.off("BINV") + (hg + 1) * 256],
                      True, True),
             reads=[af.buf("ONES1"), af.buf("BINV")], writes=[C.psbuf["C"]])
        k.op("ve", CP(af.ap("VBIAS"), C.ps["C"][:, 0:256]),
             reads=[C.psbuf["C"]], writes=[af.buf("VBIAS")])

        for pr in range(2):
            for ct in range(NCT):
                C.load("gp", abf.ap("WQ", ct * 128, (ct + 1) * 128),
                       win[ct * 128:(ct + 1) * 128, hg * 256 + pr * 128:hg * 256 + (pr + 1) * 128],
                       abf.buf("WQ"))
                C.load("gp", abf.ap("WK", ct * 128, (ct + 1) * 128),
                       win[ct * 128:(ct + 1) * 128, 1024 + hg * 256 + pr * 128:1024 + hg * 256 + (pr + 1) * 128],
                       abf.buf("WK"))
            acol = hg * 2 + pr
            for dst, wnm, boff in (("QT", "WQ", acol), ("KT", "WK", 8 + acol)):
                for qc in range(4):
                    pj = ("C", "D")[qc % 2]
                    for kt in range(NCT):
                        k.op("te", MM(C.ps[pj][:, 0:512],
                                      abf.ap(wnm, kt * 128, (kt + 1) * 128),
                                      abf.ap("XT", kt * T + qc * 512, kt * T + (qc + 1) * 512),
                                      kt == 0, kt == NCT - 1),
                             reads=[abf.buf(wnm), abf.buf("XT")], writes=[C.psbuf[pj]])
                    k.op("ve", TS(abf.ap(dst, pr * T + qc * 512, pr * T + (qc + 1) * 512),
                                  C.ps[pj][:, 0:512], af.ap("BCOL", boff, boff + 1), None, ALU.add),
                         reads=[C.psbuf[pj], af.buf("BCOL")], writes=[abf.buf(dst)])
        for mt in range(NT):
            pj = ("D", "C")[mt % 2]
            for kt in range(NCT):
                k.op("te", MM(C.ps[pj][:, 0:256],
                              abf.ap("XT", kt * T + mt * 128, kt * T + (mt + 1) * 128),
                              abf.ap("WV", kt * 256, (kt + 1) * 256),
                              kt == 0, kt == NCT - 1),
                     reads=[abf.buf("WV"), abf.buf("XT")], writes=[C.psbuf[pj]])
            k.op("ve", TT(abf.vap("VB", mt * VSTR, (mt + 1) * VSTR, "p (h c) -> p h c", c=66)[:, :, 0:64],
                          C.ps[pj][:, 0:256].rearrange("p (h d) -> p h d", h=HG),
                          af.ap("VBIAS").rearrange("p (h d) -> p h d", h=HG), ALU.add),
                 reads=[C.psbuf[pj], af.buf("VBIAS")], writes=[abf.buf("VB")])
        abf.free("WQ"); abf.free("WK"); abf.free("WV")

        for gp2 in range(2):
            pr = gp2
            for s_i in range(2):
                h = hg * HG + gp2 * 2 + s_i
                strb = f"STRB{s_i}"
                C.load("sp", af.ap(strb), names["strips"][h], af.buf(strb))
                k.op("ve", TS(af.ap(strb), af.ap(strb), 8.0, None, ALU.mult),
                     reads=[af.buf(strb)], writes=[af.buf(strb)])
            for qc in range(4):
                q0 = qc * 512
                tt_ops = [{}, {}]
                pw_ops = [{}, {}]

                def av3(kt, s_i):
                    off = s_i * 512
                    sl = s_i * 8 + kt % 8
                    ed = [pw_ops[1][kt]] if (s_i == 0 and kt in pw_ops[1]) else []
                    k.op("te", MM(C.ps["C"][0:66, off:off + 512],
                                  abf.ap("VB", kt * VSTR + (gp2 * 2 + s_i) * 66,
                                         kt * VSTR + (gp2 * 2 + s_i + 1) * 66),
                                  abf.ap("PST", sl * 512, (sl + 1) * 512),
                                  kt == 0, kt == NT - 1, skip=True),
                         reads=[abf.buf("VB"), pst_tiles[sl]], writes=[C.psbuf["C"]],
                         extra_deps=ed)

                for kt in range(NT):
                    for s_i in range(2):
                        half = s_i * 64
                        pband = ("A", "B")[kt % 2] if s_i == 0 else ("E", "D")[kt % 2]
                        psap = C.ps[pband][:, 0:512]
                        sp1n = f"SP1{s_i}{kt % 2}"
                        sp1b = af.buf(sp1n)
                        s0 = 1920 + q0 - kt * 128
                        sl = s_i * 8 + kt % 8
                        ed = [tt_ops[1][kt - 2]] if (s_i == 0 and (kt - 2) in tt_ops[1]) else []
                        k.op("te", MM(psap,
                                      abf.rap("KT", half, half + 64, pr * T + kt * 128, pr * T + (kt + 1) * 128),
                                      abf.rap("QT", half, half + 64, pr * T + q0, pr * T + q0 + 512),
                                      True, True),
                             reads=[abf.buf("KT"), abf.buf("QT")], writes=[C.psbuf[pband]],
                             extra_deps=ed)
                        tt_ops[s_i][kt] = k.op("ve", TT(af.ap(sp1n), psap,
                                                        af.ap(f"STRB{s_i}", s0, s0 + 512), ALU.add),
                                               reads=[C.psbuf[pband], af.buf(f"STRB{s_i}")], writes=[sp1b])
                        wl = [pst_tiles[sl]]
                        if pst_first[sl]:
                            wl.append(pstb)
                            pst_first[sl] = False
                        pw_ops[s_i][kt] = k.op("se", ACT(abf.ap("PST", sl * 512, (sl + 1) * 512), af.ap(sp1n),
                                                         AF.Exp, bias=0.0, scale=0.125),
                                               reads=[sp1b], writes=wl)
                    if kt >= 2:
                        av3(kt - 2, 0)
                        av3(kt - 2, 1)
                for ktl in (NT - 2, NT - 1):
                    av3(ktl, 0)
                    av3(ktl, 1)
                for s_i in range(2):
                    off = s_i * 512
                    h = hg * HG + gp2 * 2 + s_i
                    k.op("ve", CP(af.rap("OSTG", 0, 64, off, off + 512), C.ps["C"][0:64, off:off + 512]),
                         reads=[C.psbuf["C"]], writes=[af.buf("OSTG")])
                    k.op("se", ACT(af.rap("DENS", 64, 65, off, off + 512), C.ps["C"][64:65, off:off + 512], AF.Ln),
                         reads=[C.psbuf["C"]], writes=[af.buf("DENS")])
                    k.op("se", ACT(af.rap("DENS", 64, 65, off, off + 512),
                                   af.rap("DENS", 64, 65, off, off + 512), AF.Exp, scale=-1.0),
                         reads=[af.buf("DENS")], writes=[af.buf("DENS")])
                    k.op("te", MM(C.ps["D"][0:64, off:off + 512], af.rap("ONES64", 64, 65, 0, 64),
                                  af.rap("DENS", 64, 65, off, off + 512), True, True, skip=True),
                         reads=[af.buf("ONES64"), af.buf("DENS")], writes=[C.psbuf["D"]])
                    ct2, chalf = h // 2, (h % 2) * 64
                    k.op("ve", TT(abf.rap("OUTCT", chalf, chalf + 64, ct2 * T + q0, ct2 * T + q0 + 512),
                                  af.rap("OSTG", 0, 64, off, off + 512),
                                  C.ps["D"][0:64, off:off + 512], ALU.mult),
                         reads=[af.buf("OSTG"), C.psbuf["D"]], writes=[abf.buf("OUTCT")])
        abf.free("QT"); abf.free("KT"); abf.free("VB")
    abf.free("PST")
    for nm in ("SP100", "SP101", "SP110", "SP111", "STRB0", "STRB1"):
        af.free(nm)
    af.free("OSTG"); af.free("DENS"); af.free("ONES64"); af.free("BCOL"); af.free("VBIAS"); af.free("BINV")

    abf.alloc("WP", NCT * 1024)
    for ct in range(NCT):
        C.load("gp", abf.ap("WP", ct * 1024, (ct + 1) * 1024),
               names["W_out"][ct * 128:(ct + 1) * 128, :], abf.buf("WP"))
    af.alloc("BIAS", 1024)
    emit_bias_rep(C, names["b_out"], "BIAS")
    af.alloc("SOUT", 2 * 1024)
    for mt in range(NT):
        pb = "C" if mt % 2 == 0 else "D"
        for ct in range(NCT):
            for half in range(2):
                k.op("te", MM(C.ps[pb][:, half * 512:(half + 1) * 512],
                              abf.ap("OUTCT", ct * T + mt * 128, ct * T + (mt + 1) * 128),
                              abf.ap("WP", ct * 1024 + half * 512, ct * 1024 + (half + 1) * 512),
                              ct == 0, ct == NCT - 1),
                     reads=[abf.buf("OUTCT"), abf.buf("WP")], writes=[C.psbuf[pb]])
        sl = mt % 2
        k.op("ve", TT(af.ap("SOUT", sl * 1024, (sl + 1) * 1024), C.ps[pb][:, 0:1024], af.ap("BIAS"), ALU.add),
             reads=[C.psbuf[pb], af.buf("BIAS")], writes=[af.buf("SOUT")])
        C.store("sp", s_tiled[mt], af.ap("SOUT", sl * 1024, (sl + 1) * 1024), [af.buf("SOUT")], wbufs=[ssp_b])
        if "s_dbg" in names:
            C.store("sp", names["s_dbg"].rearrange("(a p) c -> a p c", p=128)[mt],
                    af.ap("SOUT", sl * 1024, (sl + 1) * 1024), [af.buf("SOUT")])
    abf.free("WP"); abf.free("OUTCT"); af.free("BIAS"); af.free("SOUT")

    abf.alloc("GW", NCT * 8)
    C.load("gp", abf.vap("GW", 0, NCT * 8, "p (a c) -> p a c", c=8),
           names["gwT"].rearrange("(a p) c -> p a c", p=128), abf.buf("GW"))
    for mt in range(NT):
        for kt in range(NCT):
            k.op("te", MM(C.ps["E"][:, 0:8],
                          abf.ap("XT", kt * T + mt * 128, kt * T + (mt + 1) * 128),
                          abf.ap("GW", kt * 8, (kt + 1) * 8),
                          kt == 0, kt == NCT - 1),
                 reads=[abf.buf("XT"), abf.buf("GW")], writes=[C.psbuf["E"]])
        k.op("ve", CP(af.ap("LGT", mt * 8, (mt + 1) * 8), C.ps["E"][:, 0:8]),
             reads=[C.psbuf["E"]], writes=[af.buf("LGT")])
    abf.free("GW")
    BIG = 1.0e30
    for mt in range(NT):
        b8 = mt * 8
        lg = af.ap("LGT", mt * 8, (mt + 1) * 8)
        lgb = af.buf("LGT")
        scb = af.buf("SSC")
        cwb = af.buf("CW")
        cw = af.ap("CW", mt * 8, (mt + 1) * 8)
        k.op("ve", fragile=True, emit=RED(af.ap("SSC", b8, b8 + 1), lg, ALU.max, negate=True),
             reads=[lgb], writes=[scb])
        k.op("se", ACT(lg, lg, AF.Exp, bias=af.ap("SSC", b8, b8 + 1), scale=1.0,
                       accum=af.ap("SSC", b8 + 1, b8 + 2)),
             reads=[lgb, scb], writes=[lgb, scb])
        k.op("ve", fragile=True, emit=RECIP(af.ap("SSC", b8 + 2, b8 + 3), af.ap("SSC", b8 + 1, b8 + 2)),
             reads=[scb], writes=[scb])
        k.op("se", ACT(lg, lg, AF.Identity, bias=0.0, scale=af.ap("SSC", b8 + 2, b8 + 3)),
             reads=[lgb, scb], writes=[lgb])
        k.op("ve", fragile=True, emit=RED(af.ap("SSC", b8 + 3, b8 + 4), lg, ALU.max),
             reads=[lgb], writes=[scb])
        k.op("ve", TS(af.ap("CW", mt * 8, (mt + 1) * 8), lg, af.ap("SSC", b8 + 3, b8 + 4), None, ALU.is_ge),
             reads=[lgb, scb], writes=[cwb])
        k.op("ve", STT(af.ap("JUNK", 0, 8), cw, -BIG, lg, ALU.mult, ALU.add),
             reads=[cwb, lgb], writes=[af.buf("JUNK")])
        k.op("ve", fragile=True, emit=RED(af.ap("SSC", b8 + 5, b8 + 6), af.ap("JUNK", 0, 8), ALU.max),
             reads=[af.buf("JUNK")], writes=[scb])
        k.op("ve", TS(af.ap("JUNK", 8, 16), af.ap("JUNK", 0, 8), af.ap("SSC", b8 + 5, b8 + 6), None, ALU.is_ge),
             reads=[af.buf("JUNK"), scb], writes=[af.buf("JUNK")])
        if "dbg_rt" in names and mt == 0:
            C.store("sp", names["dbg_rt"][:, 0:16], af.ap("JUNK", 0, 16), [af.buf("JUNK")])
            C.store("sp", names["dbg_rt"][:, 16:24], af.ap("SSC", b8, b8 + 8), [scb])
            C.store("sp", names["dbg_rt"][:, 24:32], lg, [lgb])
        k.op("ve", TT(cw, cw, af.ap("JUNK", 8, 16), ALU.add),
             reads=[cwb, af.buf("JUNK")], writes=[cwb])
        k.op("ve", TT(cw, cw, lg, ALU.mult),
             reads=[cwb, lgb], writes=[cwb])

    if "dbg_cw" in names:
        C.store("sp", names["dbg_cw"].rearrange("a p c -> p a c"), af.ap("CW", 0, NT * 8), [af.buf("CW")])
    af.alloc("ACC", NT * 1024)
    accb = af.buf("ACC")
    af.alloc("SILU", 512)
    silub = af.buf("SILU")
    abf.alloc("H3", 4 * 2048)
    for sl in range(2):
        abf.alloc(f"EWG{sl}", 8 * 512)
        abf.alloc(f"EWU{sl}", 8 * 512)
        abf.alloc(f"EWD{sl}", 4 * 1024)

    def expert3(ei, wgt, wut, wdt, masked, sl):
        wg_s, wu_s, wd_s = f"EWG{sl}", f"EWU{sl}", f"EWD{sl}"
        C.load("gp", abf.vap(wg_s, 0, 8 * 512, "p (a c) -> p a c", c=512),
               wgt.rearrange("(a p) f -> p a f", p=128), abf.buf(wg_s))
        C.load("gp", abf.vap(wu_s, 0, 8 * 512, "p (a c) -> p a c", c=512),
               wut.rearrange("(a p) f -> p a f", p=128), abf.buf(wu_s))
        C.load("gp", abf.vap(wd_s, 0, 4 * 1024, "p (a c) -> p a c", c=1024),
               wdt.rearrange("(a p) f -> p a f", p=128), abf.buf(wd_s))
        for ft in range(4):
            for qc in range(4):
                q0 = qc * 512
                for kt in range(NCT):
                    k.op("te", MM(C.ps["A"][:, 0:512],
                                  abf.ap(wg_s, kt * 512 + ft * 128, kt * 512 + (ft + 1) * 128),
                                  abf.ap("XT", kt * T + q0, kt * T + q0 + 512),
                                  kt == 0, kt == NCT - 1),
                         reads=[abf.buf(wg_s), abf.buf("XT")], writes=[C.psbuf["A"]])
                k.op("se", ACT(af.ap("SILU"), C.ps["A"][:, 0:512], AF.Silu),
                     reads=[C.psbuf["A"]], writes=[silub])
                for kt in range(NCT):
                    k.op("te", MM(C.ps["B"][:, 0:512],
                                  abf.ap(wu_s, kt * 512 + ft * 128, kt * 512 + (ft + 1) * 128),
                                  abf.ap("XT", kt * T + q0, kt * T + q0 + 512),
                                  kt == 0, kt == NCT - 1),
                         reads=[abf.buf(wu_s), abf.buf("XT")], writes=[C.psbuf["B"]])
                k.op("ve", TT(abf.ap("H3", ft * 2048 + q0, ft * 2048 + q0 + 512), af.ap("SILU"),
                              C.ps["B"][:, 0:512], ALU.mult),
                     reads=[silub, C.psbuf["B"]], writes=[abf.buf("H3")])
        for mt in range(NT):
            pb = "C" if mt % 2 == 0 else "D"
            for ft in range(4):
                for half in range(2):
                    k.op("te", MM(C.ps[pb][:, half * 512:(half + 1) * 512],
                                  abf.ap("H3", ft * 2048 + mt * 128, ft * 2048 + (mt + 1) * 128),
                                  abf.ap(wd_s, ft * 1024 + half * 512, ft * 1024 + (half + 1) * 512),
                                  ft == 0, ft == 3),
                         reads=[abf.buf("H3"), abf.buf(wd_s)], writes=[C.psbuf[pb]])
            if masked:
                k.op("ve", STT(af.ap("ACC", mt * 1024, (mt + 1) * 1024), C.ps[pb][:, 0:1024],
                               af.ap("CW", mt * 8 + ei, mt * 8 + ei + 1),
                               af.ap("ACC", mt * 1024, (mt + 1) * 1024), ALU.mult, ALU.add),
                     reads=[C.psbuf[pb], af.buf("CW"), accb], writes=[accb])
            else:
                k.op("ve", CP(af.ap("ACC", mt * 1024, (mt + 1) * 1024), C.ps[pb][:, 0:1024]),
                     reads=[C.psbuf[pb]], writes=[accb])

    expert3(-1, names["sgT"], names["suT"], names["sdT"], False, 0)
    if "dbg_sh" in names:
        C.store("sp", names["dbg_sh"].rearrange("a p c -> p a c"), af.ap("ACC", 0, NT * 1024), [accb])
    for e_i in range(8):
        expert3(e_i, names["wgT"][e_i], names["wuT"][e_i], names["wdT"][e_i], True, (e_i + 1) % 2)
    if "dbg_acc" in names:
        C.store("sp", names["dbg_acc"].rearrange("a p c -> p a c"), af.ap("ACC", 0, NT * 1024), [accb])

    abf.free("H3"); abf.free("XT")
    for sl in range(2):
        abf.free(f"EWG{sl}"); abf.free(f"EWU{sl}"); abf.free(f"EWD{sl}")
    af.free("SILU");  af.free("LGT"); af.free("CW"); af.free("SSC"); af.free("JUNK")

    af.alloc("SRT", 1024)
    for mt in range(NT):
        slot = rolls[mt % 2]
        C.load("sp", af.ap("SRT"), s_tiled[mt], af.buf("SRT"), rbufs=[ssp_b])
        C.load("sp", af.ap(slot), x_tiled[mt], af.buf(slot), rbufs=[xin_b])
        acc_ap = af.ap("ACC", mt * 1024, (mt + 1) * 1024)
        k.op("se", ACT(acc_ap, acc_ap, AF.Silu), reads=[accb], writes=[accb])
        k.op("ve", TT(acc_ap, acc_ap, af.ap("SRT"), ALU.mult),
             reads=[accb, af.buf("SRT")], writes=[accb])
        k.op("ve", TT(acc_ap, acc_ap, af.ap(slot), ALU.add),
             reads=[accb, af.buf(slot)], writes=[accb])
        C.store("sp", xo_tiled[mt], acc_ap, [accb], wbufs=[xout_b])
    af.free("SRT"); af.free("ACC")
    for nm in rolls:
        af.free(nm)
    for nm in roll_bf:
        abf.free(nm)
    af.free("STATS")
    abf.free("IDENT")
    af.free("ONES1")


def host_prep_l3(inp):
    ct = np.ascontiguousarray
    d = {}
    d["W_in"] = ct(inp["L3_attn_W_in"])
    b_in = inp["L3_attn_b_in"]
    d["b_inR"] = ct(b_in.reshape(1, 3072))
    d["b_in_cols"] = ct(b_in[0:2048].reshape(16, 128).T)
    d["W_out"] = ct(inp["L3_attn_W_out"])
    d["b_out"] = ct(inp["L3_attn_b_out"].reshape(1, D))
    rel = inp["L3_attn_rel_emb"]
    i_idx = np.arange(128)[:, None]
    u_idx = np.arange(3968)[None, :]
    pidx = np.minimum(6015 + i_idx - u_idx, 4095)
    pidx = np.maximum(pidx, 0)
    d["strips"] = ct(rel[pidx, :].transpose(2, 0, 1))
    d["gwT"] = ct(inp["L3_ffn_gate_w"].T)
    d["sgT"] = ct(inp["L3_ffn_sg"].T)
    d["suT"] = ct(inp["L3_ffn_su"].T)
    d["sdT"] = ct(inp["L3_ffn_sd"].T)
    d["wgT"] = ct(inp["L3_ffn_wg"].transpose(0, 2, 1))
    d["wuT"] = ct(inp["L3_ffn_wu"].transpose(0, 2, 1))
    d["wdT"] = ct(inp["L3_ffn_wd"].transpose(0, 2, 1))
    d["ident"] = np.eye(128, dtype=np.float32)
    return d


L3_DRAM_SPECS = [
    ("W_in", [D, 3 * D]), ("b_inR", [1, 3 * D]), ("b_in_cols", [128, 16]),
    ("W_out", [D, D]), ("b_out", [1, D]), ("strips", [16, 128, 3968]),
    ("gwT", [D, 8]), ("sgT", [D, 512]), ("suT", [D, 512]), ("sdT", [512, D]),
    ("wgT", [8, D, 512]), ("wuT", [8, D, 512]), ("wdT", [8, 512, D]),
    ("ident", [128, 128]),
]




def emit_layer2(C, names):
    nc, k = C.nc, C.k
    xin_b = names["x_in_buf"]
    xout_b = names["x_out_buf"]
    ssp_b = names["s_spill_buf"]
    abf, af = C.abf, C.af

    x_in, x_out, s_spill = names["x_in"], names["x_out"], names["s_spill"]
    x_tiled = x_in.rearrange("(a p) c -> a p c", p=128)
    s_tiled = s_spill.rearrange("(a p) c -> a p c", p=128)
    xo_tiled = x_out.rearrange("(a p) c -> a p c", p=128)

    abf.alloc("XT", NCT * T)
    abf.alloc("IDENT", 128)
    C.load("gp", abf.ap("IDENT"), names["ident"], abf.buf("IDENT"))
    af.alloc("STATS", 128)
    af.alloc("ONES1", 128)
    k.op("ve", MEMSET(af.t[0:1, af.off("ONES1"):af.off("ONES1") + 128], 1.0), writes=[af.buf("ONES1")])
    af.alloc("JUNK", 1024)
    af.alloc("LGT", NT * 16)
    af.alloc("CW", NT * 16)
    af.alloc("ONES4", 8)
    k.op("ve", MEMSET(af.ap("ONES4"), 1.0), writes=[af.buf("ONES4")])
    rolls = ["XR0", "XR1"]
    for nm in rolls:
        af.alloc(nm, 1024)
    roll_bf = ["XNR0", "XNR1"]
    for nm in roll_bf:
        abf.alloc(nm, 1024)

    emit_ln_stats(C, x_tiled, rolls, "STATS", src_buf=names["x_in_buf"],
                  per_tile_cb=lambda tt, slot: emit_norm_transpose(
                      C, tt, slot, roll_bf, "XT",
                      C.af.ap("STATS", tt * 8 + 4, tt * 8 + 5),
                      C.af.ap("STATS", tt * 8 + 5, tt * 8 + 6), [C.af.buf("STATS")]))

    abf.alloc("OUTCT", NCT * T)
    abf.alloc("PST", 16 * 512)
    pstb = abf.buf("PST")
    pst_tiles = [k.buf(f"l2pst{j}") for j in range(16)]
    pst_first = [True] * 16
    abf.alloc("HT", T)
    abf.alloc("QT1", T)
    VSTR = 2 * 66
    abf.alloc("VB", NT * VSTR)
    abf.alloc("PW", NCT * 128)
    abf.alloc("AWB", 128)
    abf.alloc("VWB", 128)
    af.alloc("OSTG", 1024)
    af.alloc("DENS", 1024)
    af.alloc("ONES64", 64)
    k.op("ve", MEMSET(af.rap("ONES64", 64, 65, 0, 64), 1.0), writes=[af.buf("ONES64")])

    for pr in range(8):
        for ct in range(NCT):
            C.load("gp", abf.ap("PW", ct * 128, (ct + 1) * 128),
                   names["projw"][pr][ct * 128:(ct + 1) * 128, :], abf.buf("PW"))
        C.load("gp", abf.ap("AWB"), names["attnw"][pr], abf.buf("AWB"))
        C.load("gp", abf.ap("VWB"), names["valuew"][pr], abf.buf("VWB"))
        vb4 = abf.vap("VB", 0, NT * VSTR, "p (t h c) -> p t h c", h=2, c=66)
        k.op("ve", MEMSET(vb4[:, :, :, 64:66], 1.0), writes=[abf.buf("VB")])
        k.op("ve", MEMSET(vb4[:, :, :, 65:66], 0.0), writes=[abf.buf("VB")])
        for qc in range(4):
            pj = ("C", "D")[qc % 2]
            for kt in range(NCT):
                k.op("te", MM(C.ps[pj][:, 0:512],
                              abf.ap("PW", kt * 128, (kt + 1) * 128),
                              abf.ap("XT", kt * T + qc * 512, kt * T + (qc + 1) * 512),
                              kt == 0, kt == NCT - 1),
                     reads=[abf.buf("PW"), abf.buf("XT")], writes=[C.psbuf[pj]])
            k.op("ve", CP(abf.ap("HT", qc * 512, (qc + 1) * 512), C.ps[pj][:, 0:512]),
                 reads=[C.psbuf[pj]], writes=[abf.buf("HT")])
        for qc in range(4):
            pj = ("C", "D")[qc % 2]
            k.op("te", MM(C.ps[pj][:, 0:512], abf.ap("AWB"),
                          abf.ap("HT", qc * 512, (qc + 1) * 512), True, True),
                 reads=[abf.buf("AWB"), abf.buf("HT")], writes=[C.psbuf[pj]])
            k.op("ve", CP(abf.ap("QT1", qc * 512, (qc + 1) * 512), C.ps[pj][:, 0:512]),
                 reads=[C.psbuf[pj]], writes=[abf.buf("QT1")])
        for mt in range(NT):
            pj = ("D", "C")[mt % 2]
            k.op("te", MM(C.ps[pj][:, 0:128],
                          abf.ap("HT", mt * 128, (mt + 1) * 128),
                          abf.ap("VWB"), True, True),
                 reads=[abf.buf("HT"), abf.buf("VWB")], writes=[C.psbuf[pj]])
            k.op("ve", CP(abf.vap("VB", mt * VSTR, (mt + 1) * VSTR, "p (h c) -> p h c", c=66)[:, :, 0:64],
                          C.ps[pj][:, 0:128].rearrange("p (h d) -> p h d", h=2)),
                 reads=[C.psbuf[pj]], writes=[abf.buf("VB")])
        for qc in range(4):
            q0 = qc * 512
            pw_ops = [{}, {}]

            def av2b(kt, s_i):
                off = s_i * 512
                sl = s_i * 8 + kt % 8
                ed = [pw_ops[1][kt]] if (s_i == 0 and kt in pw_ops[1]) else []
                k.op("te", MM(C.ps["C"][0:66, off:off + 512],
                              abf.ap("VB", kt * VSTR + s_i * 66, kt * VSTR + (s_i + 1) * 66),
                              abf.ap("PST", sl * 512, (sl + 1) * 512),
                              kt == 0, kt == NT - 1, skip=True),
                     reads=[abf.buf("VB"), pst_tiles[sl]], writes=[C.psbuf["C"]],
                     extra_deps=ed)

            for kt in range(NT):
                for s_i in range(2):
                    half = s_i * 64
                    pband = ("A", "B")[kt % 2] if s_i == 0 else ("E", "D")[kt % 2]
                    psap = C.ps[pband][:, 0:512]
                    sl = s_i * 8 + kt % 8
                    ed = [pw_ops[1][kt - 2]] if (s_i == 0 and (kt - 2) in pw_ops[1]) else []
                    k.op("te", MM(psap,
                                  abf.rap("HT", half, half + 64, kt * 128, (kt + 1) * 128),
                                  abf.rap("QT1", half, half + 64, q0, q0 + 512),
                                  True, True),
                         reads=[abf.buf("HT"), abf.buf("QT1")], writes=[C.psbuf[pband]],
                         extra_deps=ed)
                    wl = [pst_tiles[sl]]
                    if pst_first[sl]:
                        wl.append(pstb)
                        pst_first[sl] = False
                    pw_ops[s_i][kt] = k.op("se", ACT(abf.ap("PST", sl * 512, (sl + 1) * 512), psap,
                                                     AF.Exp, bias=0.0, scale=1.0 / 64.0),
                                           reads=[C.psbuf[pband]], writes=wl)
                if kt >= 2:
                    av2b(kt - 2, 0)
                    av2b(kt - 2, 1)
            for ktl in (NT - 2, NT - 1):
                av2b(ktl, 0)
                av2b(ktl, 1)
            for s_i in range(2):
                off = s_i * 512
                h = 2 * pr + s_i
                k.op("ve", CP(af.rap("OSTG", 0, 64, off, off + 512), C.ps["C"][0:64, off:off + 512]),
                     reads=[C.psbuf["C"]], writes=[af.buf("OSTG")])
                k.op("se", ACT(af.rap("DENS", 64, 65, off, off + 512), C.ps["C"][64:65, off:off + 512], AF.Ln),
                     reads=[C.psbuf["C"]], writes=[af.buf("DENS")])
                k.op("se", ACT(af.rap("DENS", 64, 65, off, off + 512),
                               af.rap("DENS", 64, 65, off, off + 512), AF.Exp, scale=-1.0),
                     reads=[af.buf("DENS")], writes=[af.buf("DENS")])
                k.op("te", MM(C.ps["D"][0:64, off:off + 512], af.rap("ONES64", 64, 65, 0, 64),
                              af.rap("DENS", 64, 65, off, off + 512), True, True, skip=True),
                     reads=[af.buf("ONES64"), af.buf("DENS")], writes=[C.psbuf["D"]])
                ct2, chalf = h // 2, (h % 2) * 64
                k.op("ve", TT(abf.rap("OUTCT", chalf, chalf + 64, ct2 * T + q0, ct2 * T + q0 + 512),
                              af.rap("OSTG", 0, 64, off, off + 512),
                              C.ps["D"][0:64, off:off + 512], ALU.mult),
                     reads=[af.buf("OSTG"), C.psbuf["D"]], writes=[abf.buf("OUTCT")])
    abf.free("PST"); abf.free("HT"); abf.free("QT1"); abf.free("VB")
    abf.free("PW"); abf.free("AWB"); abf.free("VWB")
    af.free("OSTG"); af.free("DENS"); af.free("ONES64")

    abf.alloc("WP", NCT * 1024)
    for ct in range(NCT):
        C.load("gp", abf.ap("WP", ct * 1024, (ct + 1) * 1024),
               names["outwT"][ct * 128:(ct + 1) * 128, :], abf.buf("WP"))
    af.alloc("BIAS", 1024)
    emit_bias_rep(C, names["outb"], "BIAS")
    af.alloc("SOUT", 2 * 1024)
    for mt in range(NT):
        pb = "C" if mt % 2 == 0 else "D"
        for ct in range(NCT):
            for half in range(2):
                k.op("te", MM(C.ps[pb][:, half * 512:(half + 1) * 512],
                              abf.ap("OUTCT", ct * T + mt * 128, ct * T + (mt + 1) * 128),
                              abf.ap("WP", ct * 1024 + half * 512, ct * 1024 + (half + 1) * 512),
                              ct == 0, ct == NCT - 1),
                     reads=[abf.buf("OUTCT"), abf.buf("WP")], writes=[C.psbuf[pb]])
        sl = mt % 2
        k.op("ve", TT(af.ap("SOUT", sl * 1024, (sl + 1) * 1024), C.ps[pb][:, 0:1024], af.ap("BIAS"), ALU.add),
             reads=[C.psbuf[pb], af.buf("BIAS")], writes=[af.buf("SOUT")])
        C.store("sp", s_tiled[mt], af.ap("SOUT", sl * 1024, (sl + 1) * 1024), [af.buf("SOUT")], wbufs=[ssp_b])
        if "s_dbg" in names:
            C.store("sp", names["s_dbg"].rearrange("(a p) c -> a p c", p=128)[mt],
                    af.ap("SOUT", sl * 1024, (sl + 1) * 1024), [af.buf("SOUT")])
    abf.free("WP"); abf.free("OUTCT"); af.free("BIAS"); af.free("SOUT")

    abf.alloc("GW", NCT * 16)
    C.load("gp", abf.vap("GW", 0, NCT * 16, "p (a c) -> p a c", c=16),
           names["rwT"].rearrange("(a p) c -> p a c", p=128), abf.buf("GW"))
    for mt in range(NT):
        for kt in range(NCT):
            k.op("te", MM(C.ps["E"][:, 0:16],
                          abf.ap("XT", kt * T + mt * 128, kt * T + (mt + 1) * 128),
                          abf.ap("GW", kt * 16, (kt + 1) * 16),
                          kt == 0, kt == NCT - 1),
                 reads=[abf.buf("XT"), abf.buf("GW")], writes=[C.psbuf["E"]])
        k.op("ve", CP(af.ap("LGT", mt * 16, (mt + 1) * 16), C.ps["E"][:, 0:16]),
             reads=[C.psbuf["E"]], writes=[af.buf("LGT")])
    abf.free("GW")
    af.alloc("CBR", 16)
    af.alloc("CBROW", 16)
    C.load("sp", af.t[0:1, af.off("CBROW"):af.off("CBROW") + 16], names["cbias"], af.buf("CBROW"))
    k.op("te", MM(C.ps["E"][:, 0:16], af.t[0:1, af.off("ONES1"):af.off("ONES1") + 128],
                  af.t[0:1, af.off("CBROW"):af.off("CBROW") + 16], True, True),
         reads=[af.buf("ONES1"), af.buf("CBROW")], writes=[C.psbuf["E"]])
    k.op("ve", CP(af.ap("CBR"), C.ps["E"][:, 0:16]), reads=[C.psbuf["E"]], writes=[af.buf("CBR")])

    BIG = 1.0e30
    for mt in range(NT):
        b16 = mt * 16
        lgb, cwb, scb, jkb = af.buf("LGT"), af.buf("CW"), af.buf("JUNK"), af.buf("JUNK")
        lg = af.ap("LGT", b16, b16 + 16)
        k.op("se", ACT(lg, lg, AF.Sigmoid), reads=[lgb], writes=[lgb])
        k.op("ve", TT(af.ap("JUNK", 0, 16), lg, af.ap("CBR"), ALU.add),
             reads=[lgb, af.buf("CBR")], writes=[jkb])
        for g in range(4):
            sbg = af.ap("JUNK", g * 4, (g + 1) * 4)
            k.op("ve", RED(af.ap("JUNK", 128 + g, 129 + g), sbg, ALU.max), reads=[jkb], writes=[scb])
            k.op("ve", TS(af.ap("JUNK", 16 + g * 4, 16 + (g + 1) * 4), sbg,
                          af.ap("JUNK", 128 + g, 129 + g), None, ALU.is_ge),
                 reads=[jkb, scb], writes=[jkb])
            k.op("ve", STT(af.ap("JUNK", 32 + g * 4, 32 + (g + 1) * 4),
                           af.ap("JUNK", 16 + g * 4, 16 + (g + 1) * 4), -BIG, sbg, ALU.mult, ALU.add),
                 reads=[jkb], writes=[jkb])
            k.op("ve", RED(af.ap("JUNK", 132 + g, 133 + g), af.ap("JUNK", 32 + g * 4, 32 + (g + 1) * 4),
                           ALU.max), reads=[jkb], writes=[scb])
        k.op("ve", TT(af.ap("JUNK", 48, 52), af.ap("JUNK", 128, 132), af.ap("JUNK", 132, 136), ALU.add),
             reads=[scb], writes=[jkb])
        k.op("ve", RED(af.ap("JUNK", 136, 137), af.ap("JUNK", 48, 52), ALU.max), reads=[jkb], writes=[scb])
        k.op("ve", TS(af.ap("JUNK", 52, 56), af.ap("JUNK", 48, 52), af.ap("JUNK", 136, 137), None, ALU.is_ge),
             reads=[jkb, scb], writes=[jkb])
        k.op("ve", STT(af.ap("JUNK", 56, 60), af.ap("JUNK", 52, 56), -BIG, af.ap("JUNK", 48, 52), ALU.mult, ALU.add),
             reads=[jkb], writes=[jkb])
        k.op("ve", RED(af.ap("JUNK", 137, 138), af.ap("JUNK", 56, 60), ALU.max), reads=[jkb], writes=[scb])
        k.op("ve", TS(af.ap("JUNK", 60, 64), af.ap("JUNK", 56, 60), af.ap("JUNK", 137, 138), None, ALU.is_ge),
             reads=[jkb, scb], writes=[jkb])
        k.op("ve", TT(af.ap("JUNK", 52, 56), af.ap("JUNK", 52, 56), af.ap("JUNK", 60, 64), ALU.add),
             reads=[jkb], writes=[jkb])
        for g in range(4):
            k.op("ve", TS(af.ap("JUNK", 64 + g * 4, 64 + (g + 1) * 4), af.ap("ONES4", 0, 4),
                          af.ap("JUNK", 52 + g, 53 + g), None, ALU.mult),
                 reads=[af.buf("ONES4"), jkb], writes=[jkb])
        k.op("ve", TS(af.ap("JUNK", 64, 80), af.ap("JUNK", 64, 80), -1.0, None, ALU.add),
             reads=[jkb], writes=[jkb])
        k.op("ve", STT(af.ap("JUNK", 80, 96), af.ap("JUNK", 64, 80), BIG, af.ap("JUNK", 0, 16), ALU.mult, ALU.add),
             reads=[jkb], writes=[jkb])
        k.op("ve", MEMSET(af.ap("JUNK", 96, 112), 0.0), writes=[jkb])
        for kk in range(4):
            k.op("ve", RED(af.ap("JUNK", 138 + kk, 139 + kk), af.ap("JUNK", 80, 96), ALU.max),
                 reads=[jkb], writes=[scb])
            k.op("ve", TS(af.ap("JUNK", 112, 128), af.ap("JUNK", 80, 96),
                          af.ap("JUNK", 138 + kk, 139 + kk), None, ALU.is_ge),
                 reads=[jkb, scb], writes=[jkb])
            k.op("ve", TT(af.ap("JUNK", 96, 112), af.ap("JUNK", 96, 112), af.ap("JUNK", 112, 128), ALU.add),
                 reads=[jkb], writes=[jkb])
            if kk < 3:
                k.op("ve", STT(af.ap("JUNK", 80, 96), af.ap("JUNK", 112, 128), -BIG,
                               af.ap("JUNK", 80, 96), ALU.mult, ALU.add),
                     reads=[jkb], writes=[jkb])
        if "dbg_rt2" in names and mt == 0:
            C.store("sp", names["dbg_rt2"][:, 0:144], af.ap("JUNK", 0, 144), [jkb])
            C.store("sp", names["dbg_rt2"][:, 144:160], lg, [lgb])
        k.op("ve", TT(af.ap("CW", b16, b16 + 16), lg, af.ap("JUNK", 96, 112), ALU.mult),
             reads=[lgb, jkb], writes=[cwb])
        k.op("ve", RED(af.ap("JUNK", 142, 143), af.ap("CW", b16, b16 + 16), ALU.add),
             reads=[cwb], writes=[scb])
        k.op("ve", TS(af.ap("JUNK", 142, 143), af.ap("JUNK", 142, 143), 1.0e-20, None, ALU.add),
             reads=[scb], writes=[scb])
        k.op("ve", RECIP(af.ap("JUNK", 143, 144), af.ap("JUNK", 142, 143)),
             reads=[scb], writes=[scb])
        k.op("ve", TS(af.ap("CW", b16, b16 + 16), af.ap("CW", b16, b16 + 16),
                      af.ap("JUNK", 143, 144), None, ALU.mult),
             reads=[cwb, scb], writes=[cwb])
    if "dbg_cw" in names:
        C.store("sp", names["dbg_cw"].rearrange("a p c -> p a c"), af.ap("CW", 0, NT * 16), [af.buf("CW")])

    af.alloc("ACC", NT * 1024)
    accb = af.buf("ACC")
    af.alloc("SILU", 512)
    silub = af.buf("SILU")
    for i in range(16):
        abf.alloc(f"H{i}", 1024)
    for i in range(4):
        abf.alloc(f"WG{i}", NCT * 128)
        abf.alloc(f"WU{i}", NCT * 128)
    abf.alloc("WD", 16 * 1024)
    wdb = abf.buf("WD")

    def big_expert(wgt, wut, wdt):
        for tc in range(2):
            t0 = tc * 1024
            for ft in range(16):
                wg_s, wu_s = f"WG{ft % 4}", f"WU{ft % 4}"
                C.load("gp", abf.vap(wg_s, 0, NCT * 128, "p (a c) -> p a c", c=128),
                       wgt.rearrange("(a p) f -> p a f", p=128)[:, :, ft * 128:(ft + 1) * 128],
                       abf.buf(wg_s))
                C.load("gp", abf.vap(wu_s, 0, NCT * 128, "p (a c) -> p a c", c=128),
                       wut.rearrange("(a p) f -> p a f", p=128)[:, :, ft * 128:(ft + 1) * 128],
                       abf.buf(wu_s))
                if tc == 0:
                    C.load("gp", abf.ap("WD", ft * 1024, (ft + 1) * 1024),
                           wdt[ft * 128:(ft + 1) * 128, :], wdb)
                for nh in range(2):
                    q0 = t0 + nh * 512
                    for kt in range(NCT):
                        k.op("te", MM(C.ps["A"][:, 0:512],
                                      abf.ap(wg_s, kt * 128, (kt + 1) * 128),
                                      abf.ap("XT", kt * T + q0, kt * T + q0 + 512),
                                      kt == 0, kt == NCT - 1),
                             reads=[abf.buf(wg_s), abf.buf("XT")], writes=[C.psbuf["A"]])
                    k.op("se", ACT(af.ap("SILU"), C.ps["A"][:, 0:512], AF.Silu),
                         reads=[C.psbuf["A"]], writes=[silub])
                    for kt in range(NCT):
                        k.op("te", MM(C.ps["B"][:, 0:512],
                                      abf.ap(wu_s, kt * 128, (kt + 1) * 128),
                                      abf.ap("XT", kt * T + q0, kt * T + q0 + 512),
                                      kt == 0, kt == NCT - 1),
                             reads=[abf.buf(wu_s), abf.buf("XT")], writes=[C.psbuf["B"]])
                    k.op("ve", TT(abf.ap(f"H{ft}", nh * 512, (nh + 1) * 512), af.ap("SILU"),
                                  C.ps["B"][:, 0:512], ALU.mult),
                         reads=[silub, C.psbuf["B"]], writes=[abf.buf(f"H{ft}")])
            for mi in range(8):
                mt = tc * 8 + mi
                pb = "C" if mi % 2 == 0 else "D"
                for ft in range(16):
                    for half in range(2):
                        k.op("te", MM(C.ps[pb][:, half * 512:(half + 1) * 512],
                                      abf.ap(f"H{ft}", mi * 128, (mi + 1) * 128),
                                      abf.ap("WD", ft * 1024 + half * 512, ft * 1024 + (half + 1) * 512),
                                      ft == 0, ft == 15),
                             reads=[abf.buf(f"H{ft}"), wdb], writes=[C.psbuf[pb]])
                k.op("ve", CP(af.ap("ACC", mt * 1024, (mt + 1) * 1024), C.ps[pb][:, 0:1024]),
                     reads=[C.psbuf[pb]], writes=[accb])

    big_expert(names["sgT"], names["suT"], names["sdT"])
    for i in range(16):
        abf.free(f"H{i}")
    for i in range(4):
        abf.free(f"WG{i}"); abf.free(f"WU{i}")
    abf.free("WD")

    abf.alloc("H2", 2 * 2048)
    for sl in range(2):
        abf.alloc(f"EWG{sl}", NCT * 256)
        abf.alloc(f"EWU{sl}", NCT * 256)
        abf.alloc(f"EWD{sl}", 2 * 1024)

    def down4(ei, sl, qc):
        wd_s = f"EWD{sl}"
        for mt in range(qc * 4, qc * 4 + 4):
            pb = "C" if mt % 2 == 0 else "D"
            for ft in range(2):
                for half in range(2):
                    k.op("te", MM(C.ps[pb][:, half * 512:(half + 1) * 512],
                                  abf.ap("H2", ft * 2048 + mt * 128, ft * 2048 + (mt + 1) * 128),
                                  abf.ap(wd_s, ft * 1024 + half * 512, ft * 1024 + (half + 1) * 512),
                                  ft == 0, ft == 1),
                         reads=[abf.buf("H2"), abf.buf(wd_s)], writes=[C.psbuf[pb]])
            k.op("ve", STT(af.ap("ACC", mt * 1024, (mt + 1) * 1024), C.ps[pb][:, 0:1024],
                           af.ap("CW", mt * 16 + ei, mt * 16 + ei + 1),
                           af.ap("ACC", mt * 1024, (mt + 1) * 1024), ALU.mult, ALU.add),
                 reads=[C.psbuf[pb], af.buf("CW"), accb], writes=[accb])

    def small_expert(ei, sl, prev):
        wg_s, wu_s, wd_s = f"EWG{sl}", f"EWU{sl}", f"EWD{sl}"
        C.load("gp", abf.vap(wg_s, 0, NCT * 256, "p (a c) -> p a c", c=256),
               names["gatew"][ei].rearrange("(a p) f -> p a f", p=128), abf.buf(wg_s))
        C.load("gp", abf.vap(wu_s, 0, NCT * 256, "p (a c) -> p a c", c=256),
               names["upw"][ei].rearrange("(a p) f -> p a f", p=128), abf.buf(wu_s))
        C.load("gp", abf.vap(wd_s, 0, 2 * 1024, "p (a c) -> p a c", c=1024),
               names["downw"][ei].rearrange("(a p) f -> p a f", p=128), abf.buf(wd_s))
        for qc in range(4):
            if prev is not None:
                down4(prev, 1 - sl, qc)
            q0 = qc * 512
            for ft in range(2):
                for kt in range(NCT):
                    k.op("te", MM(C.ps["A"][:, 0:512],
                                  abf.ap(wg_s, kt * 256 + ft * 128, kt * 256 + (ft + 1) * 128),
                                  abf.ap("XT", kt * T + q0, kt * T + q0 + 512),
                                  kt == 0, kt == NCT - 1),
                         reads=[abf.buf(wg_s), abf.buf("XT")], writes=[C.psbuf["A"]])
                k.op("se", ACT(af.ap("SILU"), C.ps["A"][:, 0:512], AF.Silu),
                     reads=[C.psbuf["A"]], writes=[silub])
                for kt in range(NCT):
                    k.op("te", MM(C.ps["B"][:, 0:512],
                                  abf.ap(wu_s, kt * 256 + ft * 128, kt * 256 + (ft + 1) * 128),
                                  abf.ap("XT", kt * T + q0, kt * T + q0 + 512),
                                  kt == 0, kt == NCT - 1),
                         reads=[abf.buf(wu_s), abf.buf("XT")], writes=[C.psbuf["B"]])
                k.op("ve", TT(abf.ap("H2", ft * 2048 + q0, ft * 2048 + q0 + 512), af.ap("SILU"),
                              C.ps["B"][:, 0:512], ALU.mult),
                     reads=[silub, C.psbuf["B"]], writes=[abf.buf("H2")])

    for e_i in range(16):
        small_expert(e_i, e_i % 2, e_i - 1 if e_i > 0 else None)
    for qc in range(4):
        down4(15, 1, qc)
    abf.free("H2"); abf.free("XT")
    for sl in range(2):
        abf.free(f"EWG{sl}"); abf.free(f"EWU{sl}"); abf.free(f"EWD{sl}")
    af.free("SILU");  af.free("LGT"); af.free("CW"); af.free("JUNK")
    af.free("CBR"); af.free("CBROW"); af.free("ONES4")

    af.alloc("SRT", 1024)
    for mt in range(NT):
        slot = rolls[mt % 2]
        C.load("sp", af.ap("SRT"), s_tiled[mt], af.buf("SRT"), rbufs=[ssp_b])
        C.load("sp", af.ap(slot), x_tiled[mt], af.buf(slot), rbufs=[xin_b])
        acc_ap = af.ap("ACC", mt * 1024, (mt + 1) * 1024)
        k.op("se", ACT(acc_ap, acc_ap, AF.Silu), reads=[accb], writes=[accb])
        k.op("ve", TT(acc_ap, acc_ap, af.ap("SRT"), ALU.mult),
             reads=[accb, af.buf("SRT")], writes=[accb])
        k.op("ve", TT(acc_ap, acc_ap, af.ap(slot), ALU.add),
             reads=[accb, af.buf(slot)], writes=[accb])
        C.store("sp", xo_tiled[mt], acc_ap, [accb], wbufs=[xout_b])
    af.free("SRT"); af.free("ACC")
    for nm in rolls:
        af.free(nm)
    for nm in roll_bf:
        abf.free(nm)
    af.free("STATS")
    abf.free("IDENT")
    af.free("ONES1")


def host_prep_l2(inp):
    ct = np.ascontiguousarray
    d = {}
    pw = inp["L2_attn_proj_weight"]
    d["projw"] = ct(pw.reshape(1024, 8, 128).transpose(1, 0, 2))
    aw = inp["L2_attn_attn_weight"]
    vw = inp["L2_attn_value_weight"]
    bd_a = np.zeros((8, 128, 128), dtype=np.float32)
    bd_v = np.zeros((8, 128, 128), dtype=np.float32)
    for p in range(8):
        bd_a[p, 0:64, 0:64] = aw[2 * p]
        bd_a[p, 64:128, 64:128] = aw[2 * p + 1]
        bd_v[p, 0:64, 0:64] = vw[2 * p]
        bd_v[p, 64:128, 64:128] = vw[2 * p + 1]
    d["attnw"] = bd_a
    d["valuew"] = bd_v
    d["outwT"] = ct(inp["L2_attn_out_w"].T)
    d["outb"] = ct(inp["L2_attn_out_b"].reshape(1, D))
    d["rwT"] = ct(inp["L2_ffn_router_w"].T)
    d["cbias"] = ct(inp["L2_ffn_correction_bias"].reshape(1, 16))
    d["gatew"] = ct(inp["L2_ffn_gate_w"])
    d["upw"] = ct(inp["L2_ffn_up_w"])
    d["downw"] = ct(inp["L2_ffn_down_w"])
    d["sgT"] = ct(inp["L2_ffn_shared_gate_w"].T)
    d["suT"] = ct(inp["L2_ffn_shared_up_w"].T)
    d["sdT"] = ct(inp["L2_ffn_shared_down_w"].T)
    d["ident"] = np.eye(128, dtype=np.float32)
    return d


L2_DRAM_SPECS = [
    ("projw", [8, D, 128]), ("attnw", [8, 128, 128]), ("valuew", [8, 128, 128]),
    ("outwT", [D, D]), ("outb", [1, D]),
    ("rwT", [D, 16]), ("cbias", [1, 16]),
    ("gatew", [16, D, 256]), ("upw", [16, D, 256]), ("downw", [16, 256, D]),
    ("sgT", [D, 2048]), ("suT", [D, 2048]), ("sdT", [2048, D]),
    ("ident", [128, 128]),
]




def emit_layer0(C, names):
    nc, k = C.nc, C.k
    xin_b = names["x_in_buf"]
    xout_b = names["x_out_buf"]
    ssp_b = names["s_spill_buf"]
    abf, af = C.abf, C.af
    DIN = 2048
    NDT = DIN // 128

    x_in, x_out, s_spill = names["x_in"], names["x_out"], names["s_spill"]
    x_tiled = x_in.rearrange("(a p) c -> a p c", p=128)
    s_tiled = s_spill.rearrange("(a p) c -> a p c", p=128)
    xo_tiled = x_out.rearrange("(a p) c -> a p c", p=128)
    xn_sp = names["xn_spill"].rearrange("(a p) c -> a p c", p=128)
    sz_sp = names["sz_scr"]
    xt_sp = names["xt_scr"]
    wq1_sp = names["wq1_scr"]
    wq2_sp = names["wq2_scr"]
    dtm_sp = names["dtm_scr"]

    abf.alloc("IDENT", 128)
    C.load("gp", abf.ap("IDENT"), names["ident"], abf.buf("IDENT"))
    af.alloc("STATS", 128)
    af.alloc("ONES1", 128)
    k.op("ve", MEMSET(af.t[0:1, af.off("ONES1"):af.off("ONES1") + 128], 1.0), writes=[af.buf("ONES1")])
    af.alloc("JUNK", 1024)
    rolls = ["XR0", "XR1"]
    for nm in rolls:
        af.alloc(nm, 1024)
    roll_bf = ["XNR0", "XNR1"]
    for nm in roll_bf:
        abf.alloc(nm, 1024)

    def ln_of(src_ap_fn, src_buf, tt, dst_nm, dst_buf, grep=None, brep=None):
        b = tt * 8
        st = "STATS"
        sb = af.buf(st)
        src = src_ap_fn()
        k.op("ve", RED(af.ap(st, b, b + 1), src, ALU.add, negate=True), reads=[src_buf], writes=[sb])
        k.op("ve", TS(af.ap(st, b + 1, b + 2), af.ap(st, b, b + 1), 1.0 / D, None, ALU.mult),
             reads=[sb], writes=[sb])
        k.op("se", ACT(af.ap("JUNK"), src, AF.Square, bias=af.ap(st, b + 1, b + 2),
                       scale=1.0, accum=af.ap(st, b + 2, b + 3)),
             reads=[src_buf, sb], writes=[af.buf("JUNK"), sb])
        k.op("ve", TS(af.ap(st, b + 2, b + 3), af.ap(st, b + 2, b + 3), 1.0 / D, 1e-5, ALU.mult, ALU.add),
             reads=[sb], writes=[sb])
        k.op("se", ACT(af.ap(st, b + 3, b + 4), af.ap(st, b + 2, b + 3), AF.Ln), reads=[sb], writes=[sb])
        k.op("se", ACT(af.ap(st, b + 4, b + 5), af.ap(st, b + 3, b + 4), AF.Exp, scale=-0.5),
             reads=[sb], writes=[sb])
        k.op("ve", TT(af.ap(st, b + 5, b + 6), af.ap(st, b + 1, b + 2), af.ap(st, b + 4, b + 5), ALU.mult),
             reads=[sb], writes=[sb])
        if grep is None:
            k.op("se", ACT(abf.ap(dst_nm), src, AF.Identity,
                           bias=af.ap(st, b + 5, b + 6), scale=af.ap(st, b + 4, b + 5)),
                 reads=[src_buf, sb], writes=[dst_buf])
        else:
            k.op("se", ACT(af.ap("LNN"), src, AF.Identity,
                           bias=af.ap(st, b + 5, b + 6), scale=af.ap(st, b + 4, b + 5)),
                 reads=[src_buf, sb], writes=[af.buf("LNN")])
            k.op("ve", TT(af.ap("LNN"), af.ap("LNN"), af.ap(grep), ALU.mult),
                 reads=[af.buf("LNN"), af.buf(grep)], writes=[af.buf("LNN")])
            k.op("ve", TT(abf.ap(dst_nm), af.ap("LNN"), af.ap(brep), ALU.add),
                 reads=[af.buf("LNN"), af.buf(brep)], writes=[dst_buf])

    def transpose_tile(nm, dstT, tt):
        for ct in range(NCT):
            k.op("te", TRANS(C.ps["T"][:, ct * 128:(ct + 1) * 128],
                             abf.ap(nm, ct * 128, (ct + 1) * 128), abf.ap("IDENT")),
                 reads=[abf.buf(nm), abf.buf("IDENT")], writes=[C.psbuf["T"]])
        src = C.ps["T"][:, 0:1024].rearrange("p (a c) -> p a c", a=NCT)
        dst = abf.vap(dstT, 0, NCT * T, "p (a c) -> p a c", c=T)[:, :, tt * 128:(tt + 1) * 128]
        k.op("ve", CP(dst, src), reads=[C.psbuf["T"]], writes=[abf.buf(dstT)])

    abf.alloc("XT", NCT * T)
    af.alloc("XNF", 1024)

    def p1_cb(tt, slot):
        emit_norm_transpose(C, tt, slot, roll_bf, "XT",
                            C.af.ap("STATS", tt * 8 + 4, tt * 8 + 5),
                            C.af.ap("STATS", tt * 8 + 5, tt * 8 + 6), [C.af.buf("STATS")])
        k.op("se", ACT(af.ap("XNF"), af.ap(slot), AF.Identity,
                       bias=af.ap("STATS", tt * 8 + 5, tt * 8 + 6),
                       scale=af.ap("STATS", tt * 8 + 4, tt * 8 + 5)),
             reads=[af.buf(slot), af.buf("STATS")], writes=[af.buf("XNF")])
        C.store("sp", xn_sp[tt], af.ap("XNF"), [af.buf("XNF")], wbufs=[names["xn_spill_buf"]])

    emit_ln_stats(C, x_tiled, rolls, "STATS", p1_cb, src_buf=names["x_in_buf"])
    af.free("XNF")
    C.store("sp", xt_sp, abf.ap("XT"), [abf.buf("XT")], wbufs=[names["xt_scr_buf"]])
    abf.free("XT")

    abf.alloc("WCV0", 1024)
    abf.alloc("WCV1", 1024)
    wff2_sp = names["wff2_scr"]
    for wt2 in range(32):
        ws = f"WCV{wt2 % 2}"
        C.load("gp", abf.ap(ws), names["w_ff2"][wt2 * 128:(wt2 + 1) * 128, :], abf.buf(ws))
        C.store("sp", wff2_sp[wt2 * 128:(wt2 + 1) * 128, :], abf.ap(ws), [abf.buf(ws)],
                wbufs=[names["wff2_scr_buf"]])
    abf.free("WCV0"); abf.free("WCV1")

    af.alloc("G1R", 1024)
    af.alloc("B1R", 1024)
    af.alloc("LNN", 1024)
    emit_bias_rep(C, names["ln1_g"], "G1R")
    emit_bias_rep(C, names["ln1_b"], "B1R")
    abf.alloc("H1T", NCT * T)
    for tt in range(NT):
        slot = rolls[tt % 2]
        C.load("sp", af.ap(slot), xn_sp[tt], af.buf(slot), rbufs=[names["xn_spill_buf"]])
        nm = roll_bf[tt % 2]
        ln_of(lambda slot=slot: af.ap(slot), af.buf(slot), tt, nm, abf.buf(nm), "G1R", "B1R")
        transpose_tile(nm, "H1T", tt)
    af.free("G1R"); af.free("B1R"); af.free("LNN")

    abf.alloc("XCT", NDT * T)
    abf.alloc("WIN0", NCT * 128)
    abf.alloc("WIN1", NCT * 128)
    abf.alloc("SZR", T)
    af.alloc("CVT0", 2052)
    af.alloc("CVT1", 2052)
    af.alloc("SCN0", 2048)
    k.op("ve", MEMSET(af.ap("CVT0", 0, 4), 0.0), writes=[af.buf("CVT0")])
    k.op("ve", MEMSET(af.ap("CVT1", 0, 4), 0.0), writes=[af.buf("CVT1")])
    af.alloc("CWC", 64)
    C.load("sp", af.ap("CWC"), names["convw_cols"], af.buf("CWC"))
    af.alloc("CBC", 16)
    C.load("sp", af.ap("CBC"), names["convb_cols"], af.buf("CBC"))
    win = names["w_in"]
    for ft in range(32):
        wslot = f"WIN{ft % 2}"
        C.load("gp", abf.vap(wslot, 0, NCT * 128, "p (a c) -> p a c", c=128),
               win.rearrange("(a p) f -> p a f", p=128)[:, :, ft * 128:(ft + 1) * 128], abf.buf(wslot))
        cv = f"CVT{ft % 2}"
        for qc in range(4):
            pb = "A" if qc % 2 == 0 else "B"
            for kt in range(NCT):
                k.op("te", MM(C.ps[pb][:, 0:512],
                              abf.ap(wslot, kt * 128, (kt + 1) * 128),
                              abf.ap("H1T", kt * T + qc * 512, kt * T + (qc + 1) * 512),
                              kt == 0, kt == NCT - 1),
                     reads=[abf.buf(wslot), abf.buf("H1T")], writes=[C.psbuf[pb]])
            if ft < 16:
                k.op("ve", CP(af.ap(cv, 4 + qc * 512, 4 + (qc + 1) * 512), C.ps[pb][:, 0:512]),
                     reads=[C.psbuf[pb]], writes=[af.buf(cv)])
            else:
                k.op("se", ACT(abf.ap("SZR", qc * 512, (qc + 1) * 512), C.ps[pb][:, 0:512], AF.Silu),
                     reads=[C.psbuf[pb]], writes=[abf.buf("SZR")])
        if ft < 16:
            cwcol = lambda j, ft=ft: af.ap("CWC", ft * 4 + j, ft * 4 + j + 1)
            k.op("ve", TS(af.ap("SCN0"), af.ap(cv, 1, 2049), cwcol(0), None, ALU.mult),
                 reads=[af.buf(cv), af.buf("CWC")], writes=[af.buf("SCN0")])
            for j in range(1, 4):
                k.op("ve", STT(af.ap("SCN0"), af.ap(cv, 1 + j, 1 + j + 2048), cwcol(j),
                               af.ap("SCN0"), ALU.mult, ALU.add),
                     reads=[af.buf(cv), af.buf("CWC"), af.buf("SCN0")], writes=[af.buf("SCN0")])
            k.op("se", ACT(abf.ap("XCT", ft * T, (ft + 1) * T), af.ap("SCN0"), AF.Silu,
                           bias=af.ap("CBC", ft, ft + 1), scale=1.0),
                 reads=[af.buf("SCN0"), af.buf("CBC")], writes=[abf.buf("XCT")])
        else:
            C.store("sp", sz_sp[(ft - 16) * 128:(ft - 15) * 128, :], abf.ap("SZR"), [abf.buf("SZR")], wbufs=[names["sz_scr_buf"]])
    abf.free("WIN0"); abf.free("WIN1"); abf.free("SZR"); abf.free("H1T")
    af.free("CVT0"); af.free("CVT1"); af.free("SCN0"); af.free("CWC"); af.free("CBC")

    abf.alloc("DTLO", T)
    af.alloc("BROWS", T)
    af.alloc("CROWS", T)
    abf.alloc("WXA", NDT * 96)
    C.load("gp", abf.vap("WXA", 0, NDT * 96, "p (a c) -> p a c", c=96),
           names["w_x"].rearrange("(a p) c -> p a c", p=128), abf.buf("WXA"))
    for (c0, c1, M, dst, isbf) in ((0, 64, 64, "DTLO", True), (64, 80, 16, "BROWS", False),
                                   (80, 96, 16, "CROWS", False)):
        for qc in range(4):
            pb = "A" if qc % 2 == 0 else "B"
            for kt in range(NDT):
                k.op("te", MM(C.ps[pb][0:M, 0:512],
                              abf.ap("WXA", kt * 96 + c0, kt * 96 + c1),
                              abf.ap("XCT", kt * T + qc * 512, kt * T + (qc + 1) * 512),
                              kt == 0, kt == NDT - 1),
                     reads=[abf.buf("WXA"), abf.buf("XCT")], writes=[C.psbuf[pb]])
            if isbf:
                k.op("ve", CP(abf.rap(dst, 0, M, qc * 512, (qc + 1) * 512), C.ps[pb][0:M, 0:512]),
                     reads=[C.psbuf[pb]], writes=[abf.buf(dst)])
            else:
                k.op("ve", CP(af.rap(dst, 0, M, qc * 512, (qc + 1) * 512), C.ps[pb][0:M, 0:512]),
                     reads=[C.psbuf[pb]], writes=[af.buf(dst)])
    abf.free("WXA")
    abf.alloc("WDT", 128)
    af.alloc("NBDT", 16)
    af.alloc("SPR", 512)
    C.load("sp", af.ap("NBDT"), names["bdt_cols"], af.buf("NBDT"))
    abf.alloc("DTM0", T)
    abf.alloc("DTM1", T)
    for ct in range(NDT):
        dts = f"DTM{ct % 2}"
        C.load("gp", abf.rap("WDT", 0, 64, 0, 128), names["w_dt"][0:64, ct * 128:(ct + 1) * 128],
               abf.buf("WDT"))
        for qc in range(4):
            pb = "A" if qc % 2 == 0 else "B"
            k.op("te", MM(C.ps[pb][:, 0:512], abf.rap("WDT", 0, 64, 0, 128),
                          abf.rap("DTLO", 0, 64, qc * 512, (qc + 1) * 512), True, True),
                 reads=[abf.buf("WDT"), abf.buf("DTLO")], writes=[C.psbuf[pb]])
            k.op("se", ACT(af.ap("SPR"), C.ps[pb][:, 0:512], AF.Exp,
                           bias=af.ap("NBDT", ct, ct + 1), scale=1.0),
                 reads=[C.psbuf[pb], af.buf("NBDT")], writes=[af.buf("SPR")])
            k.op("se", ACT(abf.ap(dts, qc * 512, (qc + 1) * 512),
                           af.ap("SPR"), AF.Ln, bias=1.0),
                 reads=[af.buf("SPR")], writes=[abf.buf(dts)])
        C.store("sp", dtm_sp[ct * 128:(ct + 1) * 128, :], abf.ap(dts), [abf.buf(dts)], wbufs=[names["dtm_scr_buf"]])
    abf.free("WDT"); abf.free("DTLO"); af.free("SPR"); af.free("NBDT")
    abf.free("DTM0"); abf.free("DTM1")

    af.alloc("APERM", 256)
    C.load("sp", af.ap("APERM"), names["alog_perm"], af.buf("APERM"))
    k.op("se", ACT(af.ap("APERM"), af.ap("APERM"), AF.Exp), reads=[af.buf("APERM")], writes=[af.buf("APERM")])
    k.op("ve", TS(af.ap("APERM"), af.ap("APERM"), -1.0, None, ALU.mult),
         reads=[af.buf("APERM")], writes=[af.buf("APERM")])
    af.alloc("RB", 128)
    C.load("sp", af.rap("RB", 0, 16, 0, 128), names["rbsel"], af.buf("RB"))
    af.alloc("BREP", T)
    af.alloc("CREP", T)
    for (src, dst) in (("BROWS", "BREP"), ("CROWS", "CREP")):
        for qc in range(4):
            k.op("te", MM(C.ps["A"][:, 0:512], af.rap("RB", 0, 16, 0, 128),
                          af.rap(src, 0, 16, qc * 512, (qc + 1) * 512), True, True),
                 reads=[af.buf("RB"), af.buf(src)], writes=[C.psbuf["A"]])
            k.op("ve", CP(af.ap(dst, qc * 512, (qc + 1) * 512), C.ps["A"][:, 0:512]),
                 reads=[C.psbuf["A"]], writes=[af.buf(dst)])
    af.free("BROWS"); af.free("CROWS"); af.free("RB")
    abf.alloc("RSEL", 16 * 128)
    C.load("gp", abf.vap("RSEL", 0, 16 * 128, "p (a c) -> p a c", c=128), names["rmats"], abf.buf("RSEL"))
    abf.alloc("SSEL", 16 * 128)
    C.load("gp", abf.vap("SSEL", 0, 16 * 128, "p (a c) -> p a c", c=128), names["smats"], abf.buf("SSEL"))
    af.alloc("DSKC", 16)
    C.load("sp", af.ap("DSKC"), names["dskip_cols"], af.buf("DSKC"))

    af.alloc("GAM", 8)
    for i4 in range(2):
        af.alloc(f"WQC{i4}", 1024)
    af.alloc("WQC_A0", 1024)
    af.alloc("WQC_A1", 1024)
    af.alloc("ONECOL", 8)
    qthunks = []

    def _q_init():
        k.op("ve", MEMSET(af.ap("GAM", 0, 2), 0.0), writes=[af.buf("GAM")])
        k.op("ve", MEMSET(af.rap("ONECOL", 0, 128, 0, 1), 1.0), writes=[af.buf("ONECOL")])
    qthunks.append(_q_init)

    def quantize_p1(wt_dram, nrow_tiles, gcol):
        ncol = wt_dram.shape[1]
        nchunk = max(1, ncol // 1024)
        w = ncol // nchunk
        for rt in range(nrow_tiles):
            for cc in range(nchunk):
                def th(rt=rt, cc=cc, w=w, wt_dram=wt_dram, gcol=gcol, nchunk=nchunk):
                    i = rt * nchunk + cc
                    slot = f"WQC{i % 2}"
                    C.load("sp" if i % 2 == 0 else "se", af.ap(slot, 0, w),
                           wt_dram[rt * 128:(rt + 1) * 128, cc * w:(cc + 1) * w],
                           af.buf(slot))
                    k.op("ve", _frag(lambda e, o=af.ap("GAM", 4, 5), i2=af.ap(slot, 0, w): e.tensor_reduce(
                        o, i2, AX.X, ALU.add, apply_absolute_value=True)),
                        reads=[af.buf(slot)], writes=[af.buf("GAM")])
                    k.op("ve", TT(af.ap("GAM", gcol, gcol + 1), af.ap("GAM", gcol, gcol + 1),
                                  af.ap("GAM", 4, 5), ALU.add),
                         reads=[af.buf("GAM")], writes=[af.buf("GAM")])
                qthunks.append(th)

    def quantize_p2(wt_dram, wq_dram, nrow_tiles, rgcol, wq_buf):
        ncol = wt_dram.shape[1]
        nchunk = max(1, ncol // 1024)
        w = ncol // nchunk
        for rt in range(nrow_tiles):
            for cc in range(nchunk):
                def th(rt=rt, cc=cc, w=w, wt_dram=wt_dram, wq_dram=wq_dram, rgcol=rgcol,
                       wq_buf=wq_buf, nchunk=nchunk):
                    i = rt * nchunk + cc
                    slot = f"WQC{i % 2}"
                    aslot = f"WQC_A{i % 2}"
                    C.load("sp" if i % 2 == 0 else "se", af.ap(slot, 0, w),
                           wt_dram[rt * 128:(rt + 1) * 128, cc * w:(cc + 1) * w],
                           af.buf(slot))
                    k.op("se", ACT(af.ap(slot, 0, w), af.ap(slot, 0, w), AF.Identity,
                                   scale=af.ap("GAM", rgcol, rgcol + 1)),
                         reads=[af.buf(slot), af.buf("GAM")], writes=[af.buf(slot)])
                    k.op("ve", TS(af.ap(aslot, 0, w), af.ap(slot, 0, w), -0.5, None, ALU.is_lt),
                         reads=[af.buf(slot)], writes=[af.buf(aslot)])
                    k.op("ve", STT(af.ap(aslot, 0, w), af.ap(slot, 0, w), 0.5,
                                   af.ap(aslot, 0, w), ALU.is_gt, ALU.subtract),
                         reads=[af.buf(slot), af.buf(aslot)], writes=[af.buf(aslot)])
                    C.store("gp", wq_dram[rt * 128:(rt + 1) * 128, cc * w:(cc + 1) * w],
                            af.ap(aslot, 0, w), [af.buf(aslot)], wbufs=[wq_buf])
                qthunks.append(th)

    quantize_p1(names["w1T"], 8, 0)
    quantize_p1(names["w2T"], 32, 1)

    def _q_stats():
        for gcol, n_elems, rgcol in ((0, 4096 * 1024, 2), (1, 4096 * 1024, 3)):
            k.op("te", MM(C.ps["E"][0:1, 0:1], af.rap("GAM", 0, 128, gcol, gcol + 1),
                          af.rap("ONECOL", 0, 128, 0, 1), True, True),
                 reads=[af.buf("GAM"), af.buf("ONECOL")], writes=[C.psbuf["E"]])
            k.op("ve", TS(af.rap("GAM", 0, 1, 4, 5), C.ps["E"][0:1, 0:1], 1.0 / n_elems, 1e-5, ALU.mult, ALU.add),
                 reads=[C.psbuf["E"]], writes=[af.buf("GAM")])
            k.op("ve", RECIP(af.rap("GAM", 0, 1, 5, 6), af.rap("GAM", 0, 1, 4, 5)),
                 reads=[af.buf("GAM")], writes=[af.buf("GAM")])
            k.op("te", MM(C.ps["E"][:, 0:1], af.t[0:1, af.off("ONES1"):af.off("ONES1") + 128],
                          af.rap("GAM", 0, 1, 5, 6), True, True),
                 reads=[af.buf("ONES1"), af.buf("GAM")], writes=[C.psbuf["E"]])
            k.op("ve", CP(af.ap("GAM", rgcol, rgcol + 1), C.ps["E"][:, 0:1]),
                 reads=[C.psbuf["E"]], writes=[af.buf("GAM")])
    qthunks.append(_q_stats)
    quantize_p2(names["w1T"], wq1_sp, 8, 2, names["wq1_scr_buf"])
    quantize_p2(names["w2T"], wq2_sp, 32, 3, names["wq2_scr_buf"])

    abf.alloc("DXT", T)
    abf.alloc("SZT0", T)
    abf.alloc("SZT1", T)
    abf.alloc("DTT0", T)
    abf.alloc("DTT1", T)
    for i4 in range(4):
        abf.alloc(f"HCB{i4}", 512)
        af.alloc(f"DA{i4}", 512)
        af.alloc(f"BB{i4}", 512)
        af.alloc(f"HH{i4}", 512)
    af.alloc("V1", 512)
    ydst = [("C", 0), ("C", 512), ("D", 0), ("D", 512)]
    for ct in range(NDT):
        szs = f"SZT{ct % 2}"
        dtt = f"DTT{ct % 2}"
        C.load("sp", abf.ap(szs), sz_sp[ct * 128:(ct + 1) * 128, :], abf.buf(szs), rbufs=[names["sz_scr_buf"]])
        C.load("sp", abf.ap(dtt), dtm_sp[ct * 128:(ct + 1) * 128, :], abf.buf(dtt), rbufs=[names["dtm_scr_buf"]])
        k.op("ve", TT(abf.ap("DXT"), abf.ap(dtt),
                      abf.ap("XCT", ct * T, (ct + 1) * T), ALU.mult),
             reads=[abf.buf(dtt), abf.buf("XCT")], writes=[abf.buf("DXT")])
        def front(it):
            j2, qc = it // 4, it % 4
            j = ct * 16 + j2
            p2 = it % 4
            pa = "A" if it % 2 == 0 else "E"
            da, bb2 = f"DA{p2}", f"BB{p2}"
            k.op("te", MM(C.ps[pa][:, 0:512],
                          abf.ap("RSEL", j2 * 128, (j2 + 1) * 128),
                          abf.ap(dtt, qc * 512, (qc + 1) * 512), True, True),
                 reads=[abf.buf("RSEL"), abf.buf(dtt)], writes=[C.psbuf[pa]])
            k.op("te", MM(C.ps["B"][:, 0:512],
                          abf.ap("RSEL", j2 * 128, (j2 + 1) * 128),
                          abf.ap("DXT", qc * 512, (qc + 1) * 512), True, True),
                 reads=[abf.buf("RSEL"), abf.buf("DXT")], writes=[C.psbuf["B"]])
            k.op("se", ACT(af.ap(da), C.ps[pa][:, 0:512], AF.Exp,
                           bias=0.0, scale=af.ap("APERM", j, j + 1)),
                 reads=[C.psbuf[pa], af.buf("APERM")], writes=[af.buf(da)])
            k.op("ve", STT(af.ap(bb2), C.ps["B"][:, 0:512], 1.0,
                           af.ap("BREP", qc * 512, (qc + 1) * 512), ALU.mult, ALU.mult),
                 reads=[C.psbuf["B"], af.buf("BREP")], writes=[af.buf(bb2)])

        front(0)
        front(1)
        for it in range(64):
            j2, qc = it // 4, it % 4
            p2 = it % 4
            da, bb2, hh, hcb = f"DA{p2}", f"BB{p2}", f"HH{qc}", f"HCB{p2}"
            hhprev = f"HH{qc - 1}"
            init = 0.0 if qc == 0 else af.ap(hhprev, 511, 512)
            k.op("ve", SCAN(af.ap(hh), af.ap(da), af.ap(bb2), init),
                 reads=[af.buf(da), af.buf(bb2)] + ([af.buf(hhprev)] if qc else []),
                 writes=[af.buf(hh)])
            if it + 2 < 64:
                front(it + 2)
            k.op("gp", TT(abf.ap(hcb), af.ap(hh),
                          af.ap("CREP", qc * 512, (qc + 1) * 512), ALU.mult),
                 reads=[af.buf(hh), af.buf("CREP")], writes=[abf.buf(hcb)])
            yb, yoff = ydst[qc]
            k.op("te", MM(C.ps[yb][:, yoff:yoff + 512],
                          abf.ap("SSEL", j2 * 128, (j2 + 1) * 128),
                          abf.ap(hcb, 0, 512),
                          j2 == 0, j2 == 15, skip=True),
                 reads=[abf.buf("SSEL"), abf.buf(hcb)], writes=[C.psbuf[yb]])
        for qc in range(4):
            yb, yoff = ydst[qc]
            k.op("ve", TS(af.ap("V1"), abf.ap("XCT", ct * T + qc * 512, ct * T + (qc + 1) * 512),
                          af.ap("DSKC", ct, ct + 1), None, ALU.mult),
                 reads=[abf.buf("XCT"), af.buf("DSKC")], writes=[af.buf("V1")])
            k.op("ve", TT(af.ap("V1"), af.ap("V1"), C.ps[yb][:, yoff:yoff + 512], ALU.add),
                 reads=[af.buf("V1"), C.psbuf[yb]], writes=[af.buf("V1")])
            k.op("ve", TT(abf.ap("XCT", ct * T + qc * 512, ct * T + (qc + 1) * 512),
                          af.ap("V1"), abf.ap(szs, qc * 512, (qc + 1) * 512), ALU.mult),
                 reads=[af.buf("V1"), abf.buf(szs)], writes=[abf.buf("XCT")])
        nth = (len(qthunks) * (ct + 1)) // NDT - (len(qthunks) * ct) // NDT
        for _ in range(nth):
            qthunks.pop(0)()
    for th in qthunks:
        th()
    for i4 in range(2):
        af.free(f"WQC{i4}")
    af.free("WQC_A0"); af.free("WQC_A1"); af.free("ONECOL"); af.free("GAM")
    abf.free("DXT"); abf.free("SZT0"); abf.free("SZT1")
    abf.free("DTT0"); abf.free("DTT1")
    abf.free("RSEL"); abf.free("SSEL")
    for i4 in range(4):
        abf.free(f"HCB{i4}")
        af.free(f"DA{i4}"); af.free(f"BB{i4}"); af.free(f"HH{i4}")
    af.free("V1")
    af.free("APERM"); af.free("BREP"); af.free("CREP"); af.free("DSKC")

    af.alloc("T1", NT * 1024)
    t1b = af.buf("T1")
    abf.alloc("WOF", NDT * 1024)
    for ct in range(NDT):
        C.load("gp", abf.ap("WOF", ct * 1024, (ct + 1) * 1024),
               names["w_out"][ct * 128:(ct + 1) * 128, :], abf.buf("WOF"))
    for mt in range(NT):
        pb = "C" if mt % 2 == 0 else "D"
        for ct in range(NDT):
            for half in range(2):
                k.op("te", MM(C.ps[pb][:, half * 512:(half + 1) * 512],
                              abf.ap("XCT", ct * T + mt * 128, ct * T + (mt + 1) * 128),
                              abf.ap("WOF", ct * 1024 + half * 512, ct * 1024 + (half + 1) * 512),
                              ct == 0, ct == NDT - 1),
                     reads=[abf.buf("XCT"), abf.buf("WOF")], writes=[C.psbuf[pb]])
        slot = rolls[mt % 2]
        C.load("sp", af.ap(slot), xn_sp[mt], af.buf(slot), rbufs=[names["xn_spill_buf"]])
        k.op("ve", TT(af.ap("T1", mt * 1024, (mt + 1) * 1024), C.ps[pb][:, 0:1024], af.ap(slot), ALU.add),
             reads=[C.psbuf[pb], af.buf(slot)], writes=[t1b])
    abf.free("XCT"); abf.free("WOF")
    for nm in rolls:
        af.free(nm)

    af.alloc("G2R", 1024)
    af.alloc("B2R", 1024)
    emit_bias_rep(C, names["ln2_g"], "G2R")
    emit_bias_rep(C, names["ln2_b"], "B2R")
    af.alloc("LNN", 1024)
    abf.alloc("H2T", NCT * T)
    for tt in range(NT):
        nm = roll_bf[tt % 2]
        ln_of(lambda tt=tt: af.ap("T1", tt * 1024, (tt + 1) * 1024), t1b, tt, nm, abf.buf(nm),
              "G2R", "B2R")
        transpose_tile(nm, "H2T", tt)
    af.free("G2R"); af.free("B2R"); af.free("LNN")

    abf.alloc("AGC", 32 * 1024)
    abf.alloc("WF0", NCT * 128)
    abf.alloc("WF1", NCT * 128)
    abf.alloc("WF2", NCT * 128)
    abf.alloc("WF3", NCT * 128)
    abf.alloc("WFD0", 1024)
    abf.alloc("WFD1", 1024)
    af.alloc("BFF1", 64)
    C.load("sp", af.ap("BFF1"), names["bff1_cols"], af.buf("BFF1"))
    af.alloc("BIAS2", 1024)
    emit_bias_rep(C, names["b_ff2"], "BIAS2")
    af.alloc("GEL", 512)
    wff1 = names["w_ff1"]
    wff2 = names["w_ff2"]
    for tcp in range(2):
        for ft in range(32):
            wa, wg = f"WF{ft % 2}", f"WF{2 + ft % 2}"
            C.load("gp", abf.vap(wa, 0, NCT * 128, "p (a c) -> p a c", c=128),
                   wff1.rearrange("(a p) f -> p a f", p=128)[:, :, ft * 128:(ft + 1) * 128], abf.buf(wa))
            C.load("gp", abf.vap(wg, 0, NCT * 128, "p (a c) -> p a c", c=128),
                   wff1.rearrange("(a p) f -> p a f", p=128)[:, :, 4096 + ft * 128:4096 + (ft + 1) * 128],
                   abf.buf(wg))
            for sub in range(2):
                q0 = tcp * 1024 + sub * 512
                pa = "A" if sub == 0 else "E"
                for kt in range(NCT):
                    k.op("te", MM(C.ps[pa][:, 0:512],
                                  abf.ap(wa, kt * 128, (kt + 1) * 128),
                                  abf.ap("H2T", kt * T + q0, kt * T + q0 + 512),
                                  kt == 0, kt == NCT - 1),
                         reads=[abf.buf(wa), abf.buf("H2T")], writes=[C.psbuf[pa]])
                for kt in range(NCT):
                    k.op("te", MM(C.ps["B"][:, 0:512],
                                  abf.ap(wg, kt * 128, (kt + 1) * 128),
                                  abf.ap("H2T", kt * T + q0, kt * T + q0 + 512),
                                  kt == 0, kt == NCT - 1),
                         reads=[abf.buf(wg), abf.buf("H2T")], writes=[C.psbuf["B"]])
                k.op("se", ACT(af.ap("GEL"), C.ps["B"][:, 0:512], AF.Gelu_apprx_tanh,
                               bias=af.ap("BFF1", 32 + ft, 33 + ft), scale=1.0),
                     reads=[C.psbuf["B"], af.buf("BFF1")], writes=[af.buf("GEL")])
                k.op("ve", STT(abf.ap("AGC", ft * 1024 + sub * 512, ft * 1024 + (sub + 1) * 512),
                               C.ps[pa][:, 0:512],
                               af.ap("BFF1", ft, ft + 1), af.ap("GEL"), ALU.add, ALU.mult),
                     reads=[C.psbuf[pa], af.buf("BFF1"), af.buf("GEL")], writes=[abf.buf("AGC")])
        for mip in range(4):
            for ft in range(32):
                ws = f"WFD{ft % 2}"
                C.load("sp" if ft % 2 == 0 else "se", abf.ap(ws), wff2_sp[ft * 128:(ft + 1) * 128, :],
                       abf.buf(ws), rbufs=[names["wff2_scr_buf"]])
                for mi in (2 * mip, 2 * mip + 1):
                    pb = "C" if mi % 2 == 0 else "D"
                    for half in range(2):
                        k.op("te", MM(C.ps[pb][:, half * 512:(half + 1) * 512],
                                      abf.ap("AGC", ft * 1024 + mi * 128, ft * 1024 + (mi + 1) * 128),
                                      abf.ap(ws, half * 512, (half + 1) * 512),
                                      ft == 0, ft == 31),
                             reads=[abf.buf("AGC"), abf.buf(ws)], writes=[C.psbuf[pb]])
            for mi in (2 * mip, 2 * mip + 1):
                mt = tcp * 8 + mi
                pb = "C" if mi % 2 == 0 else "D"
                k.op("ve", TT(af.ap("JUNK"), C.ps[pb][:, 0:1024], af.ap("BIAS2"), ALU.add),
                     reads=[C.psbuf[pb], af.buf("BIAS2")], writes=[af.buf("JUNK")])
                k.op("gp", TT(af.ap("T1", mt * 1024, (mt + 1) * 1024),
                              af.ap("T1", mt * 1024, (mt + 1) * 1024), af.ap("JUNK"), ALU.add),
                     reads=[t1b, af.buf("JUNK")], writes=[t1b])
                C.store("sp", s_tiled[mt], af.ap("T1", mt * 1024, (mt + 1) * 1024), [t1b], wbufs=[ssp_b])
                if "s_dbg" in names:
                    C.store("sp", names["s_dbg"].rearrange("(a p) c -> a p c", p=128)[mt],
                            af.ap("T1", mt * 1024, (mt + 1) * 1024), [t1b])
    abf.free("H2T"); abf.free("AGC")
    for nm in ("WF0", "WF1", "WF2", "WF3", "WFD0", "WFD1"):
        abf.free(nm)
    af.free("BFF1"); af.free("BIAS2"); af.free("GEL"); af.free("T1")

    abf.alloc("XT2", NCT * T)
    C.load("sp", abf.ap("XT2"), xt_sp, abf.buf("XT2"), rbufs=[names["xt_scr_buf"]])
    af.alloc("ACC", NT * 1024)
    accb = af.buf("ACC")
    af.alloc("B1C", 32)
    C.load("sp", af.ap("B1C"), names["b1_cols"], af.buf("B1C"))
    af.alloc("BIAS2b", 1024)
    emit_bias_rep(C, names["b2row"], "BIAS2b")
    abf.alloc("HBC", 32 * 1024)
    abf.alloc("Q10", NCT * 128)
    abf.alloc("Q11", NCT * 128)
    abf.alloc("Q20", 1024)
    abf.alloc("Q21", 1024)
    for tcp in range(2):
        for ft in range(32):
            ws = f"Q1{ft % 2}"
            C.load("sp", abf.vap(ws, 0, NCT * 128, "p (a c) -> p a c", c=128),
                   wq1_sp.rearrange("(a p) f -> p a f", p=128)[:, :, ft * 128:(ft + 1) * 128], abf.buf(ws),
                   rbufs=[names["wq1_scr_buf"]])
            for sub in range(2):
                q0 = tcp * 1024 + sub * 512
                pa = "A" if sub == 0 else "E"
                for kt in range(NCT):
                    k.op("te", MM(C.ps[pa][:, 0:512],
                                  abf.ap(ws, kt * 128, (kt + 1) * 128),
                                  abf.ap("XT2", kt * T + q0, kt * T + q0 + 512),
                                  kt == 0, kt == NCT - 1),
                         reads=[abf.buf(ws), abf.buf("XT2")], writes=[C.psbuf[pa]])
                k.op("se", ACT(abf.ap("HBC", ft * 1024 + sub * 512, ft * 1024 + (sub + 1) * 512),
                               C.ps[pa][:, 0:512], AF.Relu,
                               bias=af.ap("B1C", ft, ft + 1), scale=1.0),
                     reads=[C.psbuf[pa], af.buf("B1C")], writes=[abf.buf("HBC")])
        for mip in range(4):
            for ft in range(32):
                ws = f"Q2{ft % 2}"
                C.load("sp" if ft % 2 == 0 else "se", abf.ap(ws), wq2_sp[ft * 128:(ft + 1) * 128, :],
                       abf.buf(ws), rbufs=[names["wq2_scr_buf"]])
                for mi in (2 * mip, 2 * mip + 1):
                    pb = "C" if mi % 2 == 0 else "D"
                    for half in range(2):
                        k.op("te", MM(C.ps[pb][:, half * 512:(half + 1) * 512],
                                      abf.ap("HBC", ft * 1024 + mi * 128, ft * 1024 + (mi + 1) * 128),
                                      abf.ap(ws, half * 512, (half + 1) * 512),
                                      ft == 0, ft == 31),
                             reads=[abf.buf("HBC"), abf.buf(ws)], writes=[C.psbuf[pb]])
            for mi in (2 * mip, 2 * mip + 1):
                mt = tcp * 8 + mi
                pb = "C" if mi % 2 == 0 else "D"
                k.op("ve", TT(af.ap("ACC", mt * 1024, (mt + 1) * 1024), C.ps[pb][:, 0:1024],
                              af.ap("BIAS2b"), ALU.add),
                     reads=[C.psbuf[pb], af.buf("BIAS2b")], writes=[accb])
    abf.free("XT2"); abf.free("HBC")
    for nm in ("Q10", "Q11", "Q20", "Q21"):
        abf.free(nm)
    af.free("B1C"); af.free("BIAS2b")

    for nm in rolls:
        af.alloc(nm, 1024)
    af.alloc("SRT", 1024)
    for mt in range(NT):
        slot = rolls[mt % 2]
        C.load("sp", af.ap("SRT"), s_tiled[mt], af.buf("SRT"), rbufs=[ssp_b])
        C.load("sp", af.ap(slot), x_tiled[mt], af.buf(slot), rbufs=[xin_b])
        acc_ap = af.ap("ACC", mt * 1024, (mt + 1) * 1024)
        k.op("se", ACT(acc_ap, acc_ap, AF.Silu), reads=[accb], writes=[accb])
        k.op("ve", TT(acc_ap, acc_ap, af.ap("SRT"), ALU.mult),
             reads=[accb, af.buf("SRT")], writes=[accb])
        k.op("ve", TT(acc_ap, acc_ap, af.ap(slot), ALU.add),
             reads=[accb, af.buf(slot)], writes=[accb])
        C.store("sp", xo_tiled[mt], acc_ap, [accb], wbufs=[xout_b])
    af.free("SRT"); af.free("ACC")
    for nm in rolls:
        af.free(nm)
    for nm in roll_bf:
        abf.free(nm)
    af.free("STATS"); af.free("JUNK")
    abf.free("IDENT")
    af.free("ONES1")


def host_prep_l0(inp):
    ct = np.ascontiguousarray
    d = {}
    d["ln1_g"] = ct(inp["L0_attn_ln1_g"].reshape(1, D))
    d["ln1_b"] = ct(inp["L0_attn_ln1_b"].reshape(1, D))
    d["w_in"] = ct(inp["L0_attn_w_in"])
    d["convw_cols"] = ct(inp["L0_attn_conv_w"].reshape(16, 128, 4).transpose(1, 0, 2).reshape(128, 64))
    d["convb_cols"] = ct(inp["L0_attn_conv_b"].reshape(16, 128).T)
    d["w_x"] = ct(inp["L0_attn_w_x"])
    d["w_dt"] = ct(inp["L0_attn_w_dt"])
    d["bdt_cols"] = ct(inp["L0_attn_b_dt"].reshape(16, 128).T)
    al = inp["L0_attn_a_log"]
    ap_ = np.zeros((128, 256), np.float32)
    for di in range(8):
        for n in range(16):
            ap_[di * 16 + n, :] = al[8 * np.arange(256) + di, n]
    d["alog_perm"] = ap_
    d["dskip_cols"] = ct(inp["L0_attn_d_skip"].reshape(16, 128).T)
    d["w_out"] = ct(inp["L0_attn_w_out"])
    d["ln2_g"] = ct(inp["L0_attn_ln2_g"].reshape(1, D))
    d["ln2_b"] = ct(inp["L0_attn_ln2_b"].reshape(1, D))
    d["w_ff1"] = ct(inp["L0_attn_w_ff1"])
    bff1 = inp["L0_attn_b_ff1"]
    d["bff1_cols"] = ct(np.concatenate([bff1[:4096].reshape(32, 128).T,
                                        bff1[4096:].reshape(32, 128).T], axis=1))
    d["w_ff2"] = ct(inp["L0_attn_w_ff2"])
    d["b_ff2"] = ct(inp["L0_attn_b_ff2"].reshape(1, D))
    d["w1T"] = ct(inp["L0_ffn_W1"].T)
    d["b1_cols"] = ct(inp["L0_ffn_b1"].reshape(32, 128).T)
    d["w2T"] = ct(inp["L0_ffn_W2"].T)
    d["b2row"] = ct(inp["L0_ffn_b2"].reshape(1, D))
    rm = np.zeros((16, 128, 128), np.float32)
    sm = np.zeros((16, 128, 128), np.float32)
    for jj in range(16):
        for di in range(8):
            for n in range(16):
                rm[jj, jj * 8 + di, di * 16 + n] = 1.0
                sm[jj, di * 16 + n, jj * 8 + di] = 1.0
    d["rmats"] = rm
    d["smats"] = sm
    rb = np.zeros((16, 128), np.float32)
    for di in range(8):
        for n in range(16):
            rb[n, di * 16 + n] = 1.0
    d["rbsel"] = rb
    d["ident"] = np.eye(128, dtype=np.float32)
    return d


L0_DRAM_SPECS = [
    ("ln1_g", [1, D]), ("ln1_b", [1, D]), ("w_in", [D, 4096]),
    ("convw_cols", [128, 64]), ("convb_cols", [128, 16]),
    ("w_x", [2048, 96]), ("w_dt", [64, 2048]), ("bdt_cols", [128, 16]),
    ("alog_perm", [128, 256]), ("dskip_cols", [128, 16]), ("w_out", [2048, D]),
    ("ln2_g", [1, D]), ("ln2_b", [1, D]), ("w_ff1", [D, 8192]),
    ("bff1_cols", [128, 64]), ("w_ff2", [4096, D]), ("b_ff2", [1, D]),
    ("w1T", [D, 4096]), ("b1_cols", [128, 32]), ("w2T", [4096, D]), ("b2row", [1, D]),
    ("rmats", [16, 128, 128]), ("smats", [16, 128, 128]), ("rbsel", [16, 128]),
    ("ident", [128, 128]),
]

LAYER_SCRATCH = {0: [("xn_spill", [T, D], F32), ("sz_scr", [2048, T], BF16),
                    ("xt_scr", [D, T], BF16), ("wq1_scr", [D, 4096], BF16),
                    ("wq2_scr", [4096, D], BF16), ("dtm_scr", [2048, T], BF16),
                    ("wff2_scr", [4096, D], BF16)]}


def build_program(layers, dram_specs_per_layer, emitters):
    nc = bass.Bass("TRN2", target_bir_lowering=False)
    names_all = []
    x_cur = nc.dram_tensor("x_in", [T, D], F32, kind="ExternalInput").ap()
    x_final = nc.dram_tensor("x_out", [T, D], F32, kind="ExternalOutput").ap()
    with ExitStack() as stack:
        k = Sched(nc)
        x_cur_buf = k.buf("dram.x_in")
        for li, (lid, specs) in enumerate(zip(layers, dram_specs_per_layer)):
            names = {}
            names["x_in"] = x_cur
            names["x_in_buf"] = x_cur_buf
            if li == len(layers) - 1:
                names["x_out"] = x_final
            else:
                names["x_out"] = nc.dram_tensor(f"x_mid{li}", [T, D], F32, kind="Internal").ap()
            names["x_out_buf"] = k.buf(f"dram.x_out{li}")
            names["s_spill"] = nc.dram_tensor(f"s_spill{li}", [T, D], F32, kind="Internal").ap()
            names["s_spill_buf"] = k.buf(f"dram.s_spill{li}")
            for nm, shape, dt in LAYER_SCRATCH.get(lid, []):
                names[nm] = nc.dram_tensor(f"L{lid}_{nm}", shape, dt, kind="Internal").ap()
                names[nm + "_buf"] = k.buf(f"dram.L{lid}_{nm}")
            for nm, shape in specs:
                names[nm] = nc.dram_tensor(f"L{lid}_{nm}", shape, F32, kind="ExternalInput").ap()
            names_all.append(names)
            x_cur = names["x_out"]
            x_cur_buf = names["x_out_buf"]
        C = Common(nc, k, stack)
        for names, emitter in zip(names_all, emitters):
            emitter(C, names)
        fo = k.op("sp", None)
        k.finalize()
        last = {}
        for q in DMA_QUEUES:
            for o in k.ops[q]:
                if o.dma:
                    last[o.sem_key] = o.sem_val
        fo.waits.extend([(key, val) for key, val in sorted(last.items()) if (key, val) not in fo.waits])
        k.emit_all(stack)
    return nc, k


LAYER_TABLE = [
    (0, "L0_DRAM_SPECS", "host_prep_l0", "emit_layer0"),
    (1, "L1_DRAM_SPECS", "host_prep_l1", "emit_layer1"),
    (2, "L2_DRAM_SPECS", "host_prep_l2", "emit_layer2"),
    (3, "L3_DRAM_SPECS", "host_prep_l3", "emit_layer3"),
]


def kernel(**inputs):
    g = globals()
    x = np.ascontiguousarray(inputs["L0_attn_x"].astype(np.float32, copy=False))
    layers = [row[0] for row in LAYER_TABLE]
    specs_all = [g[row[1]] for row in LAYER_TABLE]
    emitters = [g[row[3]] for row in LAYER_TABLE]
    nc, k = build_program(layers, specs_all, emitters)
    base = {}
    for lid, spec_n, prep_n, _ in LAYER_TABLE:
        prep = g[prep_n](inputs)
        for nm, _shape in g[spec_n]:
            base[f"L{lid}_{nm}"] = prep[nm]
    in_maps = []
    for c in range(8):
        m = dict(base)
        m["x_in"] = np.ascontiguousarray(x[c])
        in_maps.append(m)
    res = run_bass_kernel_spmd(nc, in_maps, core_ids=list(range(8)))
    return np.stack([np.asarray(r["x_out"]) for r in res.results]).astype(np.float32)
